# version 04
# baseline (speedup 1.0000x reference)
import contextlib
import numpy as np
import concourse.bass as bass
import concourse.mybir as mybir
from concourse.bass_utils import run_bass_kernel_spmd

F32 = mybir.dt.float32
BF16 = mybir.dt.bfloat16
AF = mybir.ActivationFunctionType
ALU = mybir.AluOpType
AX = mybir.AxisListType

MAGIC = float(2 ** 23 + 2 ** 22)
NEG = -1.0e30

D = 1024
S = 2048
B = 2
NT = 32
NTS = 16
NH = 16
HD = 64
NKV = 4
INNER = 4096
N_SUB = 4
K_DBUF = True
K_PIPE_GU = True
K_PIPE_DN = True
K_WD_KG = True
DEBUG = []
_dbg_out = {}


NSLOT = 4


class St:

    def __init__(self, name):
        self.name = name
        self.ops = []
        self.n = 0
        self.dn = [0] * NSLOT
        self.sem = None
        self.dsem = [None] * NSLOT

    def do(self, fn):
        self.ops.append(("o", fn))

    def inc(self, fn):
        self.n += 1
        self.ops.append(("i", fn))
        return self.n

    def wait(self, other, tok):
        if tok > 0:
            self.ops.append(("w", other, tok))

    def dma(self, fn, slot=0):
        self.dn[slot] += 16
        self.ops.append(("d", fn, slot))
        return (slot, self.dn[slot])

    def wait_dma(self, other, tok):
        if tok and tok[1] > 0:
            self.ops.append(("wd", other, tok))

    def wait_dma_all(self, other):
        for s in range(NSLOT):
            if other.dn[s]:
                self.ops.append(("wd", other, (s, other.dn[s])))

    def run(self, e):
        for op in self.ops:
            k = op[0]
            if k == "o":
                op[1](e)
            elif k == "i":
                op[1](e).then_inc(self.sem, 1)
            elif k == "d":
                op[1](e).then_inc(self.dsem[op[2]], 16)
            elif k == "w":
                e.wait_ge(op[1].sem, op[2])
            elif k == "wd":
                slot, v = op[2]
                e.wait_ge(op[1].dsem[slot], v)
            elif k == "b":
                e.sem_inc(op[1], op[2])
            elif k == "wb":
                e.wait_ge(op[1], op[2])


def c512(q0, qend):
    out = []
    c0 = q0
    while c0 < qend:
        nxt = min((c0 // 512 + 1) * 512, qend)
        out.append((c0, nxt - c0, c0 // 512))
        c0 = nxt
    return out


def build(n_sub=N_SUB, DEBUG=None):
    DEBUG = DEBUG if DEBUG is not None else globals()['DEBUG']
    K_DBUF = globals()["K_DBUF"]
    K_PIPE_GU = globals()["K_PIPE_GU"]
    K_PIPE_DN = globals()["K_PIPE_DN"]
    K_WD_KG = globals()["K_WD_KG"]
    nc = bass.Bass("TRN2", target_bir_lowering=False)

    x_in = nc.dram_tensor("x", [B, S, D], F32, kind="ExternalInput").ap()
    y_out = nc.dram_tensor("y", [B, S, D], F32, kind="ExternalOutput").ap()
    wq_a = nc.dram_tensor("wq_a", [D, D], F32, kind="ExternalInput").ap()
    wk_a = nc.dram_tensor("wk_a", [NKV * HD, D], F32, kind="ExternalInput").ap()
    wv_a = nc.dram_tensor("wv_a", [NKV * HD, D], F32, kind="ExternalInput").ap()
    wo_a = nc.dram_tensor("wo_a", [D, D], F32, kind="ExternalInput").ap()
    wqkv_b = nc.dram_tensor("wqkv_b", [3 * D, D], F32, kind="ExternalInput").ap()
    bqkv_b = nc.dram_tensor("bqkv_b", [3 * D], F32, kind="ExternalInput").ap()
    wo_b = nc.dram_tensor("wo_b", [D, D], F32, kind="ExternalInput").ap()
    bo_b = nc.dram_tensor("bo_b", [D], F32, kind="ExternalInput").ap()
    ghost = nc.dram_tensor("ghost", [NH], F32, kind="ExternalInput").ap()
    w_gate = nc.dram_tensor("w_gate", [INNER, D], F32, kind="ExternalInput").ap()
    w_up = nc.dram_tensor("w_up", [INNER, D], F32, kind="ExternalInput").ap()
    w_down = nc.dram_tensor("w_down", [D, INNER], F32, kind="ExternalInput").ap()

    scr = {}
    scr["wqT_a"] = nc.dram_tensor("scr_wqT_a", [D, D], BF16, kind="Internal").ap()
    scr["wkT_a"] = nc.dram_tensor("scr_wkT_a", [D, NKV * HD], BF16, kind="Internal").ap()
    scr["wvT_a"] = nc.dram_tensor("scr_wvT_a", [D, NKV * HD], BF16, kind="Internal").ap()
    scr["woT_a"] = nc.dram_tensor("scr_woT_a", [D, D], BF16, kind="Internal").ap()
    scr["qkvT_b"] = nc.dram_tensor("scr_qkvT_b", [D, 3 * D], BF16, kind="Internal").ap()
    scr["woT_b"] = nc.dram_tensor("scr_woT_b", [D, D], BF16, kind="Internal").ap()
    scr["wgT"] = nc.dram_tensor("scr_wgT", [D, INNER], BF16, kind="Internal").ap()
    scr["wuT"] = nc.dram_tensor("scr_wuT", [D, INNER], BF16, kind="Internal").ap()
    scr["wdT"] = nc.dram_tensor("scr_wdT", [INNER, D], BF16, kind="Internal").ap()

    x_flat = x_in.rearrange("b s d -> (b s) d")
    y_flat = y_out.rearrange("b s d -> (b s) d")

    dbg_tensors = {}

    def dbg(name, ap_fn, shape, dt=F32):
        if name not in DEBUG:
            return
        t = nc.dram_tensor("dbg_" + name, shape, dt, kind="ExternalOutput").ap()
        dbg_tensors[name] = t
        ap_fn(t)

    TE, VE, SC, GP, SY = (St(n) for n in ("te", "ve", "sc", "gp", "sy"))
    streams = [TE, VE, SC, GP, SY]
    barn = [0]

    def barrier():
        for s in streams:
            s.wait_dma_all(s)
            s.ops.append(("b", None, 1))
        barn[0] += len(streams)
        for s in streams:
            s.ops.append(("wb", None, barn[0]))

    uid = [0]
    with contextlib.ExitStack() as ctx:
        def sb(name, shape, dt=BF16, st=ctx):
            uid[0] += 1
            return st.enter_context(nc.sbuf_tensor("%s_%d" % (name, uid[0]), shape, dt))

        def psum(name, shape, dt=F32, st=ctx):
            uid[0] += 1
            return st.enter_context(nc.psum_tensor("%s_%d" % (name, uid[0]), shape, dt))

        for s in streams:
            s.sem = ctx.enter_context(nc.semaphore(s.name + "_c"))
            for sl in range(NSLOT):
                s.dsem[sl] = ctx.enter_context(nc.semaphore("%s_d%d" % (s.name, sl)))
        barsem = ctx.enter_context(nc.semaphore("bar"))
        for s in streams:
            pass

        def finalize_barrier_ops():
            for s in streams:
                s.ops = [(op[0], barsem, op[2]) if op[0] in ("b", "wb") else op for op in s.ops]

        xr = sb("xr", [128, NT * D], BF16)
        ident = sb("ident", [128, 128], BF16)
        identf = sb("identf", [128, 128], F32)
        maskS = sb("maskS", [128, 128], F32)
        maskT = sb("maskT", [128, 128], F32)
        e16 = sb("e16", [16, 8 * 128], BF16)
        ones1 = sb("ones1", [1, 128], BF16)
        ones1f = sb("ones1f", [1, 128], F32)
        onescol = sb("onescol", [128, 1], F32)
        ghost_sb = sb("ghost_sb", [16, 1], F32)
        eps_t = sb("eps_t", [128, 1], F32)
        magic_t = sb("magic_t", [128, 1], F32)
        bqkT = sb("bqkT", [128, 24], F32)
        bv_row = sb("bv_row", [1, D], BF16)
        bo_row = sb("bo_row", [1, D], BF16)
        meanw = sb("meanw", [128, 3], F32)
        lnstat = sb("lnstat", [128, 8], F32)
        st8 = sb("st8", [128, 8], F32)
        trash1k = sb("trash1k", [128, 1024], BF16)
        xln_tok = sb("xln_tok", [128, 1024], BF16)
        bufs = {}
        psLN = psum("psLN", [128, 1024], BF16)
        ps_small = psum("ps_small", [128, 128], F32)

        GP.do(lambda e: e.memset(ident[:], 0.0))
        GP.do(lambda e: e.affine_select(out=ident[:], in_=ident[:], compare_op=ALU.not_equal,
                                        fill=1.0, base=0, pattern=[[-1, 128]], channel_multiplier=1))
        GP.do(lambda e: e.memset(identf[:], 0.0))
        GP.do(lambda e: e.affine_select(out=identf[:], in_=identf[:], compare_op=ALU.not_equal,
                                        fill=1.0, base=0, pattern=[[-1, 128]], channel_multiplier=1))
        GP.do(lambda e: e.memset(maskS[:], 0.0))
        GP.do(lambda e: e.affine_select(out=maskS[:], in_=maskS[:], compare_op=ALU.is_ge,
                                        fill=NEG, base=0, pattern=[[-1, 128]], channel_multiplier=1))
        GP.do(lambda e: e.memset(maskT[:], 0.0))
        GP.do(lambda e: e.affine_select(out=maskT[:], in_=maskT[:], compare_op=ALU.is_ge,
                                        fill=NEG, base=0, pattern=[[1, 128]], channel_multiplier=-1))
        GP.do(lambda e: e.memset(e16[:], 0.0))
        for c in range(8):
            for half in range(2):
                cols = slice(c * 128 + half * 64, c * 128 + half * 64 + 64)
                GP.do(lambda e, cols=cols, c=c, half=half: e.affine_select(
                    out=e16[:, cols], in_=e16[:, cols], compare_op=ALU.not_equal,
                    fill=1.0, base=-(2 * c + half), pattern=[[0, 64]], channel_multiplier=1))
        GP.do(lambda e: e.memset(ones1[:], 1.0))
        GP.do(lambda e: e.memset(ones1f[:], 1.0))
        GP.do(lambda e: e.memset(onescol[:], 1.0))
        GP.do(lambda e: e.memset(eps_t[:], 1e-5))
        t_gp_const = GP.inc(lambda e: e.memset(magic_t[:], MAGIC))
        t_ghost = GP.dma(lambda e: e.dma_start(out=ghost_sb[:], in_=ghost.rearrange("(h o) -> h o", o=1)), slot=2)
        t_bv = GP.dma(lambda e: e.dma_start(out=bv_row[:], in_=bqkv_b.rearrange("(o d) -> o d", o=1)[0:1, 2048:3072]), slot=2)
        t_bo = GP.dma(lambda e: e.dma_start(out=bo_row[:], in_=bo_b.rearrange("(o d) -> o d", o=1)), slot=2)
        t_x = 0
        for piece in range(4):
            dst = xr[:].rearrange("p (t d) -> p t d", d=D)[:, piece * 8:(piece + 1) * 8, :]
            src = x_flat[piece * 1024:(piece + 1) * 1024, :].rearrange("(t p) d -> p t d", p=128)
            t_x = GP.dma(lambda e, dst=dst, src=src: e.dma_start(out=dst, in_=src), slot=1)

        with contextlib.ExitStack() as pctx:
            wbuf = sb("wbuf", [128, 1024], BF16, pctx)
            wfl = sb("wfl", [128, 8192], F32, pctx)
            wql = sb("wql", [128, 1024], BF16, pctx)
            wtb = sb("wtb", [128, 8 * 1024], BF16, pctx)
            absacc = sb("absacc", [128, 32], F32, pctx)
            b24 = sb("b24", [128, 128], F32, pctx)
            psT0 = psum("psT0", [128, 1024], BF16, pctx)

            GP.do(lambda e: e.memset(b24[:], 0.0))
            t_b24 = GP.dma(lambda e: e.dma_start(out=b24[0:24, :], in_=bqkv_b.rearrange("(a b) -> a b", b=128)), slot=2)
            TE.wait_dma(GP, t_b24)
            TE.wait(GP, t_gp_const)
            t_te_b = TE.inc(lambda e: e.transpose(ps_small[:], b24[:], identf[:]))
            VE.wait(TE, t_te_b)
            t_bqk = VE.inc(lambda e: e.tensor_copy(bqkT[:], ps_small[:, 0:24]))
            TE.wait(VE, t_bqk)

            state = {"last_store": 0, "last_evac": 0}

            def prep_cast_w2(w_ap, dout, scr_ap):
                ntile = dout // 128
                tok_s = 0
                for g0 in range(0, ntile, 8):
                    gn = min(8, ntile - g0)
                    for mm in range(gn):
                        m = g0 + mm
                        tok_l = GP.dma(lambda e, m=m, w_ap=w_ap: e.dma_start(out=wbuf[:], in_=w_ap[m * 128:(m + 1) * 128, :]))
                        TE.wait_dma(GP, tok_l)
                        VE.wait_dma(SY, state["last_store"])
                        for k in range(8):
                            fn = lambda e, k=k: e.transpose(psT0[:, k * 128:(k + 1) * 128], wbuf[:, k * 128:(k + 1) * 128], ident[:])
                            if k == 7:
                                TE.inc(fn)
                            else:
                                TE.do(fn)
                        VE.wait(TE, TE.n)
                        dst = wtb[:].rearrange("p (k d) -> p k d", k=8)[:, :, mm * 128:(mm + 1) * 128]
                        src = psT0[:].rearrange("p (k c) -> p k c", k=8)
                        tok_e = VE.inc(lambda e, dst=dst, src=src: e.tensor_copy(dst, src))
                        state["last_evac"] = tok_e
                        GP.wait(TE, TE.n)
                        TE.wait(VE, tok_e)
                    for k in range(8):
                        SY.wait(VE, state["last_evac"])
                        tok_s = SY.dma(lambda e, k=k, scr_ap=scr_ap, g0=g0, gn=gn: e.dma_start(
                            out=scr_ap[k * 128:(k + 1) * 128, g0 * 128:g0 * 128 + gn * 128],
                            in_=wtb[:, k * 1024:k * 1024 + gn * 128]), slot=3)
                    state["last_store"] = tok_s
                return tok_s

            TE.wait(GP, t_gp_const)
            prep_cast_w2(wq_a, D, scr["wqT_a"])
            prep_cast_w2(wk_a, NKV * HD, scr["wkT_a"])
            prep_cast_w2(wv_a, NKV * HD, scr["wvT_a"])
            prep_cast_w2(wo_a, D, scr["woT_a"])
            prep_cast_w2(wqkv_b, 3 * D, scr["qkvT_b"])
            prep_cast_w2(wo_b, D, scr["woT_b"])

            def ffn_mean_pass(w_ap, ntile, ncol, meanw_col):
                tok_r = 0
                r_hist = {}
                for m in range(ntile):
                    off = (m % 2) * 4096
                    if m - 2 in r_hist:
                        SY.wait(VE, r_hist[m - 2])
                    tok_l = SY.dma(lambda e, m=m, w_ap=w_ap, ncol=ncol, off=off: e.dma_start(
                        out=wfl[:, off:off + ncol], in_=w_ap[m * 128:(m + 1) * 128, :]), slot=m % 2)
                    VE.wait_dma(SY, tok_l)
                    tok_r = VE.inc(lambda e, m=m, ncol=ncol, off=off: e.reduce_sum(
                        absacc[:, m:m + 1], wfl[:, off:off + ncol], axis=AX.X, apply_absolute_value=True))
                    r_hist[m] = tok_r
                SY.wait(VE, tok_r)
                VE.wait(VE, tok_r)
                t1 = VE.inc(lambda e, ntile=ntile: e.reduce_sum(absacc[:, 0:1], absacc[:, 0:ntile], axis=AX.X))
                TE.wait(VE, t1)
                t2 = TE.inc(lambda e: e.matmul(ps_small[0:1, 0:1], absacc[:, 0:1], onescol[:], start=True, stop=True))
                VE.wait(TE, t2)
                t3 = VE.inc(lambda e, n=float(ntile * 128 * ncol): e.tensor_scalar(
                    st8[0:1, 0:1], ps_small[0:1, 0:1], 1.0 / n, 1e-5, ALU.mult, ALU.max))
                TE.wait(VE, t3)
                t4 = TE.inc(lambda e: e.matmul(ps_small[:, 0:1], ones1f[0:1, :], st8[0:1, 0:1], start=True, stop=True))
                VE.wait(TE, t4)
                t5 = VE.inc(lambda e, c=meanw_col: e.tensor_copy(meanw[:, c:c + 1], ps_small[:, 0:1]))
                VE.wait(VE, t5)
                t6 = VE.inc(lambda e, c=meanw_col: e.reciprocal(st8[:, 1:2], meanw[:, c:c + 1]))
                TE.wait(VE, t6)
                SC.wait(VE, t6)
                return t6

            def ffn_tern_tile(src_ap_fn, m):
                tok_l = SY.dma(lambda e, m=m: e.dma_start(out=wfl[:, 0:1024], in_=src_ap_fn(m)), slot=0)
                SC.wait_dma(SY, tok_l)
                ta = SC.inc(lambda e: e.activation(wfl[:, 0:1024], wfl[:, 0:1024], AF.Identity, bias=magic_t[:], scale=st8[:, 1:2]))
                SC.wait(SC, ta)
                tb = SC.inc(lambda e: e.activation(wfl[:, 0:1024], wfl[:, 0:1024], AF.Copy, bias=-MAGIC, scale=1.0))
                VE.wait(SC, tb)
                tc = VE.inc(lambda e: e.tensor_scalar(wql[:, 0:1024], wfl[:, 0:1024], -1.0, 1.0, ALU.max, ALU.min))
                SY.wait(VE, tc)
                TE.wait(VE, tc)
                return tc

            def prep_ffn_gu(w_ap, scr_ap, meanw_col):
                ffn_mean_pass(w_ap, 32, 1024, meanw_col)
                for grp in range(4):
                    for mm in range(8):
                        m = grp * 8 + mm
                        VE.wait_dma(SY, state["last_store"])
                        ffn_tern_tile(lambda m: w_ap[m * 128:(m + 1) * 128, :], m)
                        for k in range(8):
                            fn = lambda e, k=k: e.transpose(psT0[:, k * 128:(k + 1) * 128], wql[:, k * 128:(k + 1) * 128], ident[:])
                            if k == 7:
                                TE.inc(fn)
                            else:
                                TE.do(fn)
                        VE.wait(TE, TE.n)
                        dst = wtb[:].rearrange("p (k d) -> p k d", k=8)[:, :, mm * 128:(mm + 1) * 128]
                        src = psT0[:].rearrange("p (k c) -> p k c", k=8)
                        tok_e = VE.inc(lambda e, dst=dst, src=src: e.tensor_copy(dst, src))
                        state["last_evac"] = tok_e
                        TE.wait(VE, tok_e)
                        SC.wait(VE, tok_e)
                    for k in range(8):
                        SY.wait(VE, state["last_evac"])
                        tok_s = SY.dma(lambda e, k=k, scr_ap=scr_ap, grp=grp: e.dma_start(
                            out=scr_ap[k * 128:(k + 1) * 128, grp * 1024:(grp + 1) * 1024],
                            in_=wtb[:, k * 1024:(k + 1) * 1024]), slot=3)
                    state["last_store"] = tok_s

            def prep_ffn_wd():
                ffn_mean_pass(w_down, 8, 4096, 2)
                for kg in range(4):
                    for m in range(8):
                        VE.wait_dma(SY, state["last_store"])
                        ffn_tern_tile(lambda m, kg=kg: w_down[m * 128:(m + 1) * 128, kg * 1024:(kg + 1) * 1024], m)
                        for k in range(8):
                            fn = lambda e, k=k: e.transpose(psT0[:, k * 128:(k + 1) * 128], wql[:, k * 128:(k + 1) * 128], ident[:])
                            if k == 7:
                                TE.inc(fn)
                            else:
                                TE.do(fn)
                        VE.wait(TE, TE.n)
                        dst = wtb[:].rearrange("p (k d) -> p k d", k=8)[:, :, m * 128:(m + 1) * 128]
                        src = psT0[:].rearrange("p (k c) -> p k c", k=8)
                        tok_e = VE.inc(lambda e, dst=dst, src=src: e.tensor_copy(dst, src))
                        state["last_evac"] = tok_e
                        TE.wait(VE, tok_e)
                        SC.wait(VE, tok_e)
                    for k in range(8):
                        SY.wait(VE, state["last_evac"])
                        tok_s = SY.dma(lambda e, k=k, kg=kg: e.dma_start(
                            out=scr["wdT"][(kg * 8 + k) * 128:(kg * 8 + k + 1) * 128, :],
                            in_=wtb[:, k * 1024:(k + 1) * 1024]), slot=3)
                    state["last_store"] = tok_s

            prep_ffn_gu(w_gate, scr["wgT"], 0)
            prep_ffn_gu(w_up, scr["wuT"], 1)
            prep_ffn_wd()
            barrier()

        def emit_ln_tile(tile, dst_fm, gate_tok, quant):
            sx, ssq = lnstat[:, 0:1], lnstat[:, 1:2]
            mean, rstd, nbias = lnstat[:, 2:3], lnstat[:, 3:4], lnstat[:, 4:5]
            src = xr[:, tile * D:(tile + 1) * D]
            SC.wait(VE, gate_tok)
            VE.wait(VE, gate_tok)
            t1 = SC.inc(lambda e, src=src: e.activation(trash1k[:], src, AF.Square, bias=0.0, scale=1.0, accum_out=ssq))
            t2 = VE.inc(lambda e, src=src: e.reduce_sum(sx, src, axis=AX.X))
            VE.wait(VE, t2)
            t3 = VE.inc(lambda e: e.tensor_scalar_mul(mean, sx, 1.0 / D))
            VE.wait(VE, t3)
            t4 = VE.inc(lambda e: e.tensor_mul(nbias, mean, mean))
            VE.wait(SC, t1)
            VE.wait(VE, t4)
            t5 = VE.inc(lambda e: e.scalar_tensor_tensor(rstd, ssq, 1.0 / D, nbias, ALU.mult, ALU.subtract))
            SC.wait(VE, t5)
            t6 = SC.inc(lambda e: e.activation(rstd, rstd, AF.Sqrt, bias=eps_t[:], scale=1.0))
            VE.wait(SC, t6)
            t7 = VE.inc(lambda e: e.reciprocal(rstd, rstd))
            VE.wait(VE, t7)
            t8 = VE.inc(lambda e: e.tensor_mul(nbias, mean, rstd))
            VE.wait(VE, t8)
            t9 = VE.inc(lambda e: e.tensor_scalar_mul(nbias, nbias, -1.0))
            SC.wait(VE, t9)
            if not quant:
                t_x = SC.inc(lambda e, src=src: e.activation(xln_tok[:], src, AF.Identity, bias=nbias, scale=rstd))
            else:
                amax, sxq, qsc = lnstat[:, 5:6], lnstat[:, 6:7], lnstat[:, 7:8]
                xf = bufs["xf"]
                t_f = SC.inc(lambda e, src=src, xf=xf: e.activation(xf[:], src, AF.Identity, bias=nbias, scale=rstd))
                VE.wait(SC, t_f)
                ta = VE.inc(lambda e, xf=xf: e.tensor_reduce(amax, xf[:], AX.X, ALU.max, apply_absolute_value=True))
                VE.wait(VE, ta)
                tb = VE.inc(lambda e: e.tensor_scalar_max(amax, amax, 1e-5))
                VE.wait(VE, tb)
                tc = VE.inc(lambda e: e.reciprocal(sxq, amax))
                VE.wait(VE, tc)
                td = VE.inc(lambda e: e.tensor_scalar_mul(sxq, sxq, 127.0))
                tq = VE.inc(lambda e: e.tensor_scalar_mul(qsc, amax, 1.0 / 127.0))
                SC.wait(VE, td)
                tg = SC.inc(lambda e, xf=xf: e.activation(xf[:], xf[:], AF.Identity, bias=magic_t[:], scale=sxq))
                SC.wait(SC, tg)
                t_x = SC.inc(lambda e, xf=xf: e.activation(xln_tok[:], xf[:], AF.Copy, bias=-MAGIC, scale=1.0))
            TE.wait(SC, t_x)
            for k in range(8):
                fn = lambda e, k=k: e.transpose(psLN[:, k * 128:(k + 1) * 128], xln_tok[:, k * 128:(k + 1) * 128], ident[:])
                if k == 7:
                    TE.inc(fn)
                else:
                    TE.do(fn)
            VE.wait(TE, TE.n)
            src_ps = psLN[:].rearrange("p (k c) -> p k c", k=8)
            t_e = VE.inc(lambda e, dst=dst_fm, src=src_ps: e.tensor_copy(dst_fm, src))
            TE.wait(VE, t_e)
            SC.wait(VE, t_e)
            return t_e

        def emit_attn(is_b, resid_tok):
            nkv = NH if is_b else NKV
            vheads = 8 if is_b else 4
            with contextlib.ExitStack() as actx:
                vbuf = sb("vbuf", [128, 16 * vheads * 65], BF16, actx)
                ctxb = sb("ctxb", [128, 8 * 2048], BF16, actx)
                kbuf = sb("kbuf", [64, 2048], BF16, actx)
                qbuf = sb("qbuf", [64, 2048], BF16, actx)
                ptb = sb("ptb", [128, 2 * 2048], BF16, actx)
                lrow = sb("lrow", [65, 2048], BF16, actx)
                lmatT = sb("lmatT", [16, 2048], F32, actx)
                ctxn = sb("ctxn", [128, 1024], BF16, actx)
                recipb = sb("recipb", [16, 2048], BF16, actx)
                wslice = sb("wslice", [128, 2 * 1024], BF16, actx)
                if is_b:
                    mabuf = sb("mabuf", [128, 16 * 16], F32, actx)
                    mTb = lrow[0:16, :]
                ps_st = [psum("ps_st0", [128, 512], F32, actx), psum("ps_st1", [128, 512], F32, actx)]
                ps_ctx = psum("ps_ctx", [128, 2048], F32, actx)

                for seq in range(B):
                    base = seq * NTS
                    qoff = 0
                    with contextlib.ExitStack() as bctx:
                        xlnT = sb("xlnT", [128, 8 * 2048], BF16, bctx)
                        wv_sb = sb("wv_sb", [128, 8 * vheads * 64], BF16, bctx)
                        if not is_b:
                            wk_sb = sb("wk_sb", [128, 8 * 256], BF16, bctx)
                        xlnT_r = xlnT[:].rearrange("p (k t) -> p k t", k=8)

                        for tt in range(NTS):
                            tile = base + tt
                            emit_ln_tile(tile, xlnT_r[:, :, tt * 128:(tt + 1) * 128], resid_tok[tile], quant=False)
                        t_ln_all = VE.n
                        if seq == 0 and not is_b:
                            def _tap(name, ap, shape):
                                def go(t):
                                    for s in (VE, SC, TE):
                                        GP.wait(s, s.n)
                                    GP.dma(lambda e, t=t, ap=ap: e.dma_start(out=t, in_=ap))
                                dbg(name, go, shape)
                            _tap("xlnT", xlnT[:, 0:2048], [128, 2048])

                        if not is_b:
                            t_wk = 0
                            for k in range(8):
                                t_wk = SY.dma(lambda e, k=k: e.dma_start(out=wk_sb[:, k * 256:(k + 1) * 256],
                                                                  in_=scr["wkT_a"][k * 128:(k + 1) * 128, :]), slot=2)

                        prev_proj_mm = [0, 0]
                        wsl_tok = [0] * (NH + 1)

                        def prefetch_slices(h):
                            if h >= NH:
                                return
                            wb = (h % 2) * 1024
                            SY.wait(TE, prev_proj_mm[h % 2])
                            if is_b:
                                SY.dma(lambda e, h=h, wb=wb: e.dma_start(
                                    out=wslice[:, wb:wb + 512].rearrange("p (k c) -> p k c", k=8),
                                    in_=scr["qkvT_b"][:, h * 64:(h + 1) * 64].rearrange("(k p) c -> p k c", p=128)), slot=h % 2)
                                wsl_tok[h] = SY.dma(lambda e, h=h, wb=wb: e.dma_start(
                                    out=wslice[:, wb + 512:wb + 1024].rearrange("p (k c) -> p k c", k=8),
                                    in_=scr["qkvT_b"][:, 1024 + h * 64:1024 + (h + 1) * 64].rearrange("(k p) c -> p k c", p=128)), slot=h % 2)
                            else:
                                wsl_tok[h] = SY.dma(lambda e, h=h, wb=wb: e.dma_start(
                                    out=wslice[:, wb:wb + 512].rearrange("p (k c) -> p k c", k=8),
                                    in_=scr["wqT_a"][:, h * 64:(h + 1) * 64].rearrange("(k p) c -> p k c", p=128)), slot=h % 2)

                        prefetch_slices(0)
                        t_kq = 0
                        t_av_hist = {}
                        t_ctx_ev = 0
                        t_lm = [0] * NH
                        for h in range(NH):
                            g = h if is_b else h // 4
                            hh = g % vheads
                            if (is_b and h % 8 == 0) or (not is_b and h == 0):
                                v0 = (h // 8) * 8 if is_b else 0
                                ndv = vheads * 64
                                GP.wait(TE, TE.n)
                                t_vb = GP.inc(lambda e: e.memset(vbuf[:], 1.0))
                                t_wv = 0
                                if is_b:
                                    vsrc = scr["qkvT_b"]
                                    col0 = 2048 + v0 * 64
                                else:
                                    vsrc = scr["wvT_a"]
                                    col0 = 0
                                for k in range(8):
                                    t_wv = SY.dma(lambda e, k=k, vsrc=vsrc, col0=col0, ndv=ndv: e.dma_start(
                                        out=wv_sb[:, k * ndv:(k + 1) * ndv],
                                        in_=vsrc[k * 128:(k + 1) * 128, col0:col0 + ndv]), slot=2)
                                TE.wait(GP, t_vb)
                                TE.wait_dma(SY, t_wv)
                                TE.wait(VE, t_ln_all)
                                vbuf_r = vbuf[:].rearrange("p (t h c) -> p t h c", t=16, h=vheads)
                                for tt in range(NTS):
                                    for (c0, cw, bank) in c512(0, ndv):
                                        for k in range(8):
                                            last = (k == 7) and (not is_b)
                                            fn = lambda e, k=k, tt=tt, c0=c0, cw=cw, last=last: e.matmul(
                                                ps_ctx[:, c0:c0 + cw],
                                                xlnT_r[:, k, tt * 128:(tt + 1) * 128],
                                                wv_sb[:, k * ndv + c0:k * ndv + c0 + cw],
                                                start=(k == 0), stop=last)
                                            if last:
                                                t_mmv = TE.inc(fn)
                                            else:
                                                TE.do(fn)
                                        if is_b:
                                            t_mmv = TE.inc(lambda e, c0=c0, cw=cw, col0=col0: e.matmul(
                                                ps_ctx[:, c0:c0 + cw], ones1[:],
                                                bv_row[0:1, col0 - 2048 + c0:col0 - 2048 + c0 + cw],
                                                start=False, stop=True))
                                        VE.wait(TE, t_mmv)
                                        VE.wait(GP, t_vb)
                                        nh0 = c0 // 64
                                        nh1 = (c0 + cw) // 64
                                        dst = vbuf_r[:, tt, nh0:nh1, 0:64]
                                        t_vev = VE.inc(lambda e, dst=dst, c0=c0, cw=cw: e.tensor_copy(
                                            dst, ps_ctx[:, c0:c0 + cw].rearrange("p (h c) -> p h c", c=64)))
                                        TE.wait(VE, t_vev)

                            def proj64(buf, wof, bias):
                                t_ev = 0
                                ev_hist = {}
                                for pi, (c0, cw, bank) in enumerate(c512(0, 2048)):
                                    pq = ps_st[pi % 2]
                                    if pi - 2 in ev_hist:
                                        TE.wait(SC, ev_hist[pi - 2])
                                    for k in range(8):
                                        fn = lambda e, k=k, c0=c0, cw=cw, buf=buf, wof=wof, pq=pq: e.matmul(
                                            pq[0:64, 0:cw], wof(k), xlnT_r[:, k, c0:c0 + cw],
                                            start=(k == 0), stop=(k == 7))
                                        if k == 7:
                                            t_mm = TE.inc(fn)
                                        else:
                                            TE.do(fn)
                                    SC.wait(TE, t_mm)
                                    if bias is None:
                                        t_ev = SC.inc(lambda e, buf=buf, c0=c0, cw=cw, pq=pq: e.activation(
                                            buf[:, c0:c0 + cw], pq[0:64, 0:cw], AF.Copy, bias=0.0, scale=1.0))
                                    else:
                                        t_ev = SC.inc(lambda e, buf=buf, c0=c0, cw=cw, bias=bias, pq=pq: e.activation(
                                            buf[:, c0:c0 + cw], pq[0:64, 0:cw], AF.Identity, bias=bias, scale=1.0))
                                    ev_hist[pi] = t_ev
                                TE.wait(SC, t_ev)
                                return t_ev, t_mm

                            wb = (h % 2) * 1024
                            TE.wait_dma(SY, wsl_tok[h])
                            if is_b:
                                wq_k = lambda k, wb=wb: wslice[:, wb + k * 64:wb + (k + 1) * 64]
                                wk_k = lambda k, wb=wb: wslice[:, wb + 512 + k * 64:wb + 512 + (k + 1) * 64]
                                bq = bqkT[(h % 2) * 64:(h % 2) * 64 + 64, h // 2:h // 2 + 1]
                                bk = bqkT[(h % 2) * 64:(h % 2) * 64 + 64, 8 + h // 2:8 + h // 2 + 1]
                            else:
                                TE.wait_dma(SY, t_wk)
                                wq_k = lambda k, wb=wb: wslice[:, wb + k * 64:wb + (k + 1) * 64]
                                wk_k = lambda k, g=g: wk_sb[:, k * 256 + g * 64:k * 256 + (g + 1) * 64]
                                bq = bk = None

                            if is_b or h % 4 == 0:
                                proj64(kbuf, wk_k, bk)
                            t_kq, t_proj_mm = proj64(qbuf, wq_k, bq)
                            prev_proj_mm[h % 2] = t_proj_mm
                            TE.wait(SC, t_kq)
                            prefetch_slices(h + 1)
                            if seq == 0 and h == 0 and not is_b:
                                def _tap2(name, ap, shape):
                                    def go(t):
                                        GP.wait(SC, SC.n)
                                        GP.wait(VE, VE.n)
                                        GP.dma(lambda e, t=t, ap=ap: e.dma_start(out=t, in_=ap))
                                    dbg(name, go, shape)
                                _tap2("kbuf", kbuf[:], [64, 2048])
                                _tap2("qbuf", qbuf[:], [64, 2048])
                                _tap2("vbuf", vbuf[:, 0:1040], [128, 1040])

                            cnt = 0
                            exp_hist = {}
                            avq = []
                            last_av_of = {}

                            def flush_one():
                                fn_av, tok_e, jj = avq.pop(0)
                                TE.wait(SC, tok_e)
                                last_av_of[jj] = TE.inc(fn_av)

                            for j in range(NTS):
                                q0 = j * 128
                                par = j % 2
                                while avq and avq[0][2] <= j - 2:
                                    flush_one()
                                if j - 2 in last_av_of:
                                    SC.wait(TE, last_av_of[j - 2])
                                chunks = c512(q0, 2048)
                                for ci, (c0, cw, bank) in enumerate(chunks):
                                    pst = ps_st[cnt % 2]
                                    if cnt - 2 in exp_hist:
                                        TE.wait(SC, exp_hist[cnt - 2])
                                    fn = lambda e, j=j, c0=c0, cw=cw, pst=pst: e.matmul(
                                        pst[:, 0:cw], kbuf[:, j * 128:(j + 1) * 128], qbuf[:, c0:c0 + cw],
                                        start=True, stop=True)
                                    t_mm = TE.inc(fn)
                                    if ci == 0:
                                        VE.wait(TE, t_mm)
                                        t_mk = VE.inc(lambda e, pst=pst: e.tensor_tensor(pst[:, 0:128], pst[:, 0:128], maskT[:], ALU.add))
                                        SC.wait(VE, t_mk)
                                    SC.wait(TE, t_mm)
                                    t_exp = SC.inc(lambda e, pst=pst, par=par, c0=c0, cw=cw: e.activation(
                                        ptb[:, par * 2048 + c0:par * 2048 + c0 + cw], pst[:, 0:cw],
                                        AF.Exp, bias=0.0, scale=0.125))
                                    exp_hist[cnt] = t_exp
                                    vsl = vbuf[:, j * (vheads * 65) + hh * 65:j * (vheads * 65) + (hh + 1) * 65]
                                    fn_av = lambda e, j=j, vsl=vsl, par=par, c0=c0, cw=cw, bank=bank: e.matmul(
                                        ps_ctx[0:65, c0:c0 + cw], vsl,
                                        ptb[:, par * 2048 + c0:par * 2048 + c0 + cw],
                                        start=(j == 0), stop=(j == min(NTS - 1, 4 * bank + 3)))
                                    avq.append((fn_av, t_exp, j))
                                    while len(avq) > 2:
                                        flush_one()
                                    cnt += 1
                            while avq:
                                flush_one()
                            t_av = last_av_of[NTS - 1]

                            if is_b:
                                scnt = 0
                                mx_hist = {}
                                for i in range(NTS):
                                    L = (i + 1) * 128
                                    chunks = c512(0, L)
                                    mparts = st8[:, 0:4]
                                    for ci, (c0, cw, bank) in enumerate(chunks):
                                        pss = ps_st[scnt % 2]
                                        if scnt - 2 in mx_hist:
                                            TE.wait(VE, mx_hist[scnt - 2])
                                        fn = lambda e, i=i, c0=c0, cw=cw, pss=pss: e.matmul(
                                            pss[:, 0:cw], qbuf[:, i * 128:(i + 1) * 128], kbuf[:, c0:c0 + cw],
                                            start=True, stop=True)
                                        t_mm = TE.inc(fn)
                                        VE.wait(TE, t_mm)
                                        if ci == len(chunks) - 1:
                                            off = L - 128 - c0
                                            t_mk = VE.inc(lambda e, pss=pss, off=off: e.tensor_tensor(
                                                pss[:, off:off + 128], pss[:, off:off + 128], maskS[:], ALU.add))
                                            VE.wait(VE, t_mk)
                                        t_mx = VE.inc(lambda e, pss=pss, ci=ci, cw=cw, mparts=mparts: e.tensor_scalar(
                                            trash1k[:, 0:cw], pss[:, 0:cw], 1.0, None, ALU.mult, ALU.max,
                                            accum_out=mparts[:, ci:ci + 1]))
                                        mx_hist[scnt] = t_mx
                                        scnt += 1
                                    VE.wait(VE, t_mx)
                                    t_mr = VE.inc(lambda e, i=i, h=h, n=len(chunks), mparts=mparts: e.reduce_max(
                                        mabuf[:, i * 16 + h:i * 16 + h + 1], mparts[:, 0:n], axis=AX.X))
                                TE.wait(VE, t_mx)

                            SC.wait(TE, t_av)
                            if h >= 2:
                                SC.wait_dma(GP, t_lm[h - 2])
                            lr = lrow[(h % 2) * 64:(h % 2) * 64 + 1, :]
                            t_l = SC.inc(lambda e, lr=lr: e.copy(lr, ps_ctx[64:65, :]))
                            GP.wait(SC, t_l)
                            t_lm[h] = GP.dma(lambda e, h=h, lr=lr: e.dma_start(out=lmatT[h:h + 1, :], in_=lr), slot=2 + h % 2)
                            dst_c = ctxb[:].rearrange("p (k t) -> p k t", k=8)[(h % 2) * 64:(h % 2) * 64 + 64, h // 2, :]
                            t_ctx_ev = SC.inc(lambda e, dst_c=dst_c: e.activation(dst_c, ps_ctx[0:64, :], AF.Copy, bias=0.0, scale=1.0))
                            TE.wait(SC, t_ctx_ev)
                            TE.wait(SC, t_l)

                    with contextlib.ExitStack() as cctx:
                        wo_sb = sb("wo_sb", [128, 8 * 1024], BF16, cctx)
                        SY.wait(TE, TE.n)
                        SY.wait(SC, SC.n)
                        t_wo = 0
                        wo_src = scr["woT_b"] if is_b else scr["woT_a"]
                        for k in range(8):
                            t_wo = SY.dma(lambda e, k=k, wo_src=wo_src: e.dma_start(
                                out=wo_sb[:, k * 1024:(k + 1) * 1024], in_=wo_src[k * 128:(k + 1) * 128, :]), slot=2)
                        if is_b:
                            VE.wait_dma(GP, t_lm[NH - 1])
                            VE.wait_dma(GP, t_lm[NH - 2])
                            for tt in range(NTS):
                                TE.wait(VE, t_mr)
                                t_tp = TE.inc(lambda e, tt=tt: e.transpose(ps_small[0:16, 0:128], mabuf[:, tt * 16:(tt + 1) * 16], identf[:]))
                                VE.wait(TE, t_tp)
                                t_vm = VE.inc(lambda e, tt=tt: e.tensor_copy(mTb[:, tt * 128:(tt + 1) * 128], ps_small[0:16, 0:128]))
                                TE.wait(VE, t_vm)
                            SC.wait(VE, t_vm)
                            t_em = SC.inc(lambda e: e.activation(mTb[:], mTb[:], AF.Exp, bias=0.0, scale=0.125))
                            VE.wait(SC, t_em)
                            VE.wait_dma(GP, t_lm[NH - 1])
                            VE.wait_dma(GP, t_lm[NH - 2])
                            t_dn = VE.inc(lambda e: e.scalar_tensor_tensor(lmatT[:], mTb[:], ghost_sb[:], lmatT[:], ALU.mult, ALU.add))
                            VE.wait(VE, t_dn)
                            t_rc = VE.inc(lambda e: e.reciprocal(lmatT[:], lmatT[:]))
                        else:
                            VE.wait_dma(GP, t_lm[NH - 1])
                            VE.wait_dma(GP, t_lm[NH - 2])
                            t_rc = VE.inc(lambda e: e.reciprocal(lmatT[:], lmatT[:]))
                        VE.wait(VE, t_rc)
                        t_rc = VE.inc(lambda e: e.tensor_copy(recipb[:], lmatT[:]))

                        if seq == 0 and not is_b:
                            def _tap3(name, ap, shape):
                                def go(t):
                                    GP.wait(VE, VE.n)
                                    GP.wait(SC, SC.n)
                                    GP.dma(lambda e, t=t, ap=ap: e.dma_start(out=t, in_=ap))
                                dbg(name, go, shape)
                            _tap3("lmatT", lmatT[:], [16, 2048])
                            _tap3("ctxb", ctxb[:, 0:2048], [128, 2048])
                            _tap3("ptb", ptb[:], [128, 2048])
                        TE.wait(VE, t_rc)
                        TE.wait(SC, t_ctx_ev)
                        TE.wait_dma(SY, t_wo)
                        for tt in range(NTS):
                            tile = base + tt
                            for c in range(8):
                                fn = lambda e, c=c, tt=tt: e.matmul(
                                    ps_ctx[:, c * 128:(c + 1) * 128], e16[:, c * 128:(c + 1) * 128],
                                    recipb[:, tt * 128:(tt + 1) * 128], start=True, stop=True)
                                if c == 7:
                                    t_rm = TE.inc(fn)
                                else:
                                    TE.do(fn)
                            VE.wait(TE, t_rm)
                            src_c = ctxb[:].rearrange("p (k t) -> p k t", k=8)[:, :, tt * 128:(tt + 1) * 128]
                            dst_n = ctxn[:].rearrange("p (k c) -> p k c", k=8)
                            src_r = ps_ctx[:, 0:1024].rearrange("p (k c) -> p k c", k=8)
                            t_nm = VE.inc(lambda e, src_c=src_c, dst_n=dst_n, src_r=src_r: e.tensor_tensor(dst_n, src_c, src_r, ALU.mult))
                            TE.wait(VE, t_nm)
                            for nn in range(2):
                                for k in range(8):
                                    fn = lambda e, nn=nn, k=k, last=(k == 7 and not is_b): e.matmul(
                                        ps_st[nn][:, 0:512], ctxn[:, k * 128:(k + 1) * 128],
                                        wo_sb[:, k * 1024 + nn * 512:k * 1024 + (nn + 1) * 512],
                                        start=(k == 0), stop=(k == 7 and not is_b))
                                    if k == 7 and not is_b:
                                        t_mo = TE.inc(fn)
                                    else:
                                        TE.do(fn)
                                if is_b:
                                    t_mo = TE.inc(lambda e, nn=nn: e.matmul(
                                        ps_st[nn][:, 0:512], ones1[:], bo_row[0:1, nn * 512:(nn + 1) * 512],
                                        start=False, stop=True))
                                VE.wait(TE, t_mo)
                                dst = xr[:, tile * D + nn * 512:tile * D + (nn + 1) * 512]
                                t_res = VE.inc(lambda e, dst=dst, nn=nn: e.tensor_tensor(dst, ps_st[nn][:, 0:512], dst, ALU.add))
                                TE.wait(VE, t_res)
                            resid_tok[tile] = t_res
                        VE.wait(TE, TE.n)
                        SC.wait(TE, TE.n)

        def emit_ffn(final, resid_tok):
            GT = 8
            with contextlib.ExitStack() as fctx:
                stash = sb("stash", [128, GT * 4096], BF16, fctx)
                qscales = sb("qscales", [128, 4 * GT], F32, fctx)
                bufs["xf"] = sb("xln_f32", [128, 1024], F32, fctx)
                outstage = sb("outstage", [128, 2 * 512], F32, fctx) if final else None
                t_outdma = {}
                gate_sc = lambda tt: qscales[:, tt:tt + 1]
                up_sc = lambda tt: qscales[:, GT + tt:GT + tt + 1]
                down_sc = lambda tt: qscales[:, 2 * GT + tt:2 * GT + tt + 1]

                last_dn_mm = [0]
                for grp in range(4):
                    with contextlib.ExitStack() as f1:
                        rT = sb("rT", [128, 8 * 1024], BF16, f1)
                        wgu = sb("wgu", [128, 2 * 4096], BF16, f1)
                        silu_t = sb("silu_t", [128, 512], F32, f1)
                        ps_g = [psum("ps_g0", [128, 512], F32, f1), psum("ps_g1", [128, 512], F32, f1)]
                        ps_u = [psum("ps_u0", [128, 512], F32, f1), psum("ps_u1", [128, 512], F32, f1)]
                        rT_r = rT[:].rearrange("p (k t) -> p k t", k=8)
                        for tt in range(GT):
                            tile = grp * GT + tt
                            emit_ln_tile(tile, rT_r[:, :, tt * 128:(tt + 1) * 128], resid_tok[tile], quant=True)
                            if grp == 0 and tt == 0:
                                def _tapl(name, ap, shape):
                                    def go(t):
                                        GP.wait(VE, VE.n)
                                        GP.wait(SC, SC.n)
                                        GP.dma(lambda e, t=t, ap=ap: e.dma_start(out=t, in_=ap))
                                    dbg(name, go, shape)
                                _tapl("lnstat0", lnstat[:], [128, 8])
                            qsc = lnstat[:, 7:8]
                            VE.wait(VE, VE.n)
                            t1 = VE.inc(lambda e, tt=tt, qsc=qsc: e.tensor_mul(gate_sc(tt), qsc, meanw[:, 0:1]))
                            t2 = VE.inc(lambda e, tt=tt, qsc=qsc: e.tensor_mul(up_sc(tt), qsc, meanw[:, 1:2]))
                            SC.wait(VE, t2)
                        t_wgu = [0, 0]
                        t_gu_by_buf = [0, 0]
                        t_gu2 = 0
                        t_gu2_by_par = {}
                        t_si_by_par = {}
                        CW = 256
                        for c in range(16):
                            buf = c % 2 if K_DBUF else 0
                            SY.wait(TE, t_gu_by_buf[buf])
                            tok = 0
                            for k in range(8):
                                SY.dma(lambda e, k=k, c=c, buf=buf: e.dma_start(
                                    out=wgu[:, buf * 4096 + k * CW:buf * 4096 + (k + 1) * CW],
                                    in_=scr["wgT"][k * 128:(k + 1) * 128, c * CW:(c + 1) * CW]), slot=buf)
                                tok = SY.dma(lambda e, k=k, c=c, buf=buf: e.dma_start(
                                    out=wgu[:, buf * 4096 + 2048 + k * CW:buf * 4096 + 2048 + (k + 1) * CW],
                                    in_=scr["wuT"][k * 128:(k + 1) * 128, c * CW:(c + 1) * CW]), slot=buf)
                            t_wgu[buf] = tok
                            TE.wait_dma(SY, t_wgu[buf])
                            for tt in range(GT):
                                par = tt % 2
                                if par in t_gu2_by_par:
                                    TE.wait(VE, t_gu2_by_par[par])
                                for k in range(8):
                                    TE.do(lambda e, k=k, tt=tt, buf=buf, par=par: e.matmul(
                                        ps_g[par][:, 0:CW], rT_r[:, k, tt * 128:(tt + 1) * 128],
                                        wgu[:, buf * 4096 + k * CW:buf * 4096 + (k + 1) * CW],
                                        start=(k == 0), stop=(k == 7)))
                                    fn = lambda e, k=k, tt=tt, buf=buf, par=par: e.matmul(
                                        ps_u[par][:, 0:CW], rT_r[:, k, tt * 128:(tt + 1) * 128],
                                        wgu[:, buf * 4096 + 2048 + k * CW:buf * 4096 + 2048 + (k + 1) * CW],
                                        start=(k == 0), stop=(k == 7))
                                    if k == 7:
                                        t_gu = TE.inc(fn)
                                    else:
                                        TE.do(fn)
                                t_gu_by_buf[buf] = t_gu
                                SC.wait(TE, t_gu)
                                if par in t_gu2_by_par:
                                    SC.wait(VE, t_gu2_by_par[par])
                                t_si = SC.inc(lambda e, par=par, tt=tt: e.activation(
                                    silu_t[:, par * CW:par * CW + CW], ps_g[par][:, 0:CW], AF.Silu, bias=0.0, scale=gate_sc(tt)))
                                VE.wait(SC, t_si)
                                dst = stash[:, tt * 4096 + c * CW:tt * 4096 + (c + 1) * CW]
                                t_gu2 = VE.inc(lambda e, dst=dst, par=par, tt=tt: e.scalar_tensor_tensor(
                                    dst, ps_u[par][:, 0:CW], up_sc(tt), silu_t[:, par * CW:par * CW + CW], ALU.mult, ALU.mult))
                                t_gu2_by_par[par] = t_gu2
                                if not K_PIPE_GU:
                                    TE.wait(VE, t_gu2)

                    if grp == 0:
                        def _tapscr(name, src_ap, shape):
                            def go(t):
                                GP.dma(lambda e, t=t, src_ap=src_ap: e.dma_start(out=t, in_=src_ap))
                            dbg(name, go, shape)
                        _tapscr("scrg", scr["wgT"][0:128, 3072:4096], [128, 1024])
                        _tapscr("scru", scr["wuT"][0:128, 3072:4096], [128, 1024])
                        def _tapf(name, ap, shape):
                            def go(t):
                                for s in (VE, SC, TE):
                                    GP.wait(s, s.n)
                                GP.dma(lambda e, t=t, ap=ap: e.dma_start(out=t, in_=ap))
                                for s in (VE, SC, TE):
                                    s.wait_dma_all(GP)
                            dbg(name, go, shape)
                        _tapf("meanw", meanw[:], [128, 3])
                        _tapf("gu0", stash[:, 0:4096], [128, 4096])
                        _tapf("gu4", stash[:, 4 * 4096:5 * 4096], [128, 4096])
                        _tapf("qsc", qscales[:], [128, 32])
                    t_phase1_mm = TE.n
                    with contextlib.ExitStack() as f2:
                        wdf = sb("wdf", [128, 32 * 512], BF16, f2)
                        ps_t2 = psum("ps_t2", [128, 2048], BF16, f2)
                        ps_d = [psum("ps_d0", [128, 512], F32, f2), psum("ps_d1", [128, 512], F32, f2)]
                        t_q_done = [0] * GT
                        for tt in range(GT):
                            tile = grp * GT + tt
                            amax = lnstat[:, 5:6]
                            VE.wait(VE, t_gu2)
                            VE.wait(SC, SC.n)
                            ta = VE.inc(lambda e, tt=tt, amax=amax: e.tensor_reduce(
                                amax, stash[:, tt * 4096:(tt + 1) * 4096], AX.X, ALU.max, apply_absolute_value=True))
                            VE.wait(VE, ta)
                            tb = VE.inc(lambda e, amax=amax: e.tensor_scalar_max(amax, amax, 1e-5))
                            VE.wait(VE, tb)
                            tc = VE.inc(lambda e, amax=amax: e.reciprocal(st8[:, 4:5], amax))
                            VE.wait(VE, tc)
                            td = VE.inc(lambda e: e.tensor_scalar_mul(st8[:, 4:5], st8[:, 4:5], 127.0))
                            tq2 = VE.inc(lambda e, tt=tt, amax=amax: e.tensor_scalar_mul(st8[:, 5:6], amax, 1.0 / 127.0))
                            VE.wait(VE, tq2)
                            tq3 = VE.inc(lambda e, tt=tt: e.tensor_mul(down_sc(tt), st8[:, 5:6], meanw[:, 2:3]))
                            SC.wait(VE, td)
                            xf = bufs["xf"]
                            for hf in range(4):
                                sl = slice(tt * 4096 + hf * 1024, tt * 4096 + (hf + 1) * 1024)
                                u1 = SC.inc(lambda e, sl=sl, xf=xf: e.activation(xf[:], stash[:, sl], AF.Identity, bias=magic_t[:], scale=st8[:, 4:5]))
                                SC.wait(SC, u1)
                                u2 = SC.inc(lambda e, sl=sl, xf=xf: e.activation(stash[:, sl], xf[:], AF.Copy, bias=-MAGIC, scale=1.0))
                                SC.wait(SC, u2)
                            for rnd in range(2):
                                TE.wait(SC, u2)
                                for kk in range(16):
                                    k = rnd * 16 + kk
                                    fn = lambda e, k=k, kk=kk, tt=tt: e.transpose(
                                        ps_t2[:, kk * 128:(kk + 1) * 128],
                                        stash[:, tt * 4096 + k * 128:tt * 4096 + (k + 1) * 128], ident[:])
                                    if kk == 15:
                                        t_tr = TE.inc(fn)
                                    else:
                                        TE.do(fn)
                                VE.wait(TE, t_tr)
                                dst = stash[:, tt * 4096 + rnd * 2048:tt * 4096 + (rnd + 1) * 2048]
                                t_bk = VE.inc(lambda e, dst=dst: e.tensor_copy(dst, ps_t2[:]))
                                TE.wait(VE, t_bk)
                            t_q_done[tt] = t_bk
                            if grp == 0 and tt in (0, 4):
                                def _tapg(name, ap, shape):
                                    def go(t):
                                        GP.wait(VE, VE.n)
                                        GP.wait(SC, SC.n)
                                        GP.dma(lambda e, t=t, ap=ap: e.dma_start(out=t, in_=ap))
                                        for s in (VE, SC, TE):
                                            s.wait_dma_all(GP)
                                    dbg(name, go, shape)
                                _tapg("r2T%d" % tt, stash[:, tt * 4096:(tt + 1) * 4096], [128, 4096])
                        t_res_by_par = {}
                        for hf in range(2):
                            SY.wait(TE, max(last_dn_mm[0], t_phase1_mm))
                            wd_tok = []
                            for k in range(32):
                                wd_tok.append(SY.dma(lambda e, k=k, hf=hf: e.dma_start(
                                    out=wdf[:, k * 512:(k + 1) * 512],
                                    in_=scr["wdT"][k * 128:(k + 1) * 128, hf * 512:(hf + 1) * 512]), slot=hf))
                            TE.wait_dma(SY, wd_tok[-1])
                            for tt in range(GT):
                                tile = grp * GT + tt
                                par = tt % 2
                                TE.wait(VE, t_q_done[tt])
                                if par in t_res_by_par:
                                    TE.wait(VE, t_res_by_par[par])
                                for k in range(32):
                                    fn = lambda e, k=k, tt=tt, par=par: e.matmul(
                                        ps_d[par][:, 0:512], stash[:, tt * 4096 + k * 128:tt * 4096 + (k + 1) * 128],
                                        wdf[:, k * 512:(k + 1) * 512], start=(k == 0), stop=(k == 31))
                                    if k == 31:
                                        t_dm = TE.inc(fn)
                                    else:
                                        TE.do(fn)
                                last_dn_mm[0] = t_dm
                                VE.wait(TE, t_dm)
                                if not K_PIPE_DN:
                                    pass
                                dst = xr[:, tile * D + hf * 512:tile * D + (hf + 1) * 512]
                                if final:
                                    reg = tt % 2
                                    ob = outstage[:, reg * 512:reg * 512 + 512]
                                    VE.wait_dma(SY, t_outdma.get(reg, 0))
                                    t_res = VE.inc(lambda e, dst=dst, par=par, tt=tt, ob=ob: e.scalar_tensor_tensor(
                                        ob, ps_d[par][:, 0:512], down_sc(tt), dst, ALU.mult, ALU.add))
                                    SY.wait(VE, t_res)
                                    t_outdma[reg] = SY.dma(lambda e, tile=tile, hf=hf, ob=ob: e.dma_start(
                                        out=y_flat[tile * 128:(tile + 1) * 128, hf * 512:(hf + 1) * 512], in_=ob), slot=2 + reg)
                                else:
                                    t_res = VE.inc(lambda e, dst=dst, par=par, tt=tt: e.scalar_tensor_tensor(
                                        dst, ps_d[par][:, 0:512], down_sc(tt), dst, ALU.mult, ALU.add))
                                t_res_by_par[par] = t_res
                                if not K_PIPE_DN:
                                    TE.wait(VE, t_res)
                                resid_tok[tile] = t_res
                    barrier()

        resid_tok = [0] * NT
        VE.wait_dma(GP, t_x)
        SC.wait_dma(GP, t_x)
        VE.wait_dma(GP, t_ghost)
        TE.wait_dma(GP, t_bo)

        def _tap_sbuf(name, sb_ap, shape):
            def go(t):
                for s in (VE, SC, TE):
                    GP.wait(s, s.n)
                GP.wait_dma_all(GP)
                GP.dma(lambda e, t=t, sb_ap=sb_ap: e.dma_start(out=t, in_=sb_ap))
            dbg(name, go, shape)

        if n_sub >= 1:
            _tap_sbuf("xr0", xr[:, 0:1024], [128, 1024])
            emit_attn(False, resid_tok)
        if n_sub >= 2:
            barrier()
            emit_ffn(final=(n_sub == 2), resid_tok=resid_tok)
        if n_sub >= 3:
            barrier()
            emit_attn(True, resid_tok)
        if n_sub >= 4:
            barrier()
            emit_ffn(final=True, resid_tok=resid_tok)

        if n_sub in (1, 3):
            barrier()
            dumpb = sb("dumpb", [128, 1024], F32)
            for tile in range(NT):
                t_c = SC.inc(lambda e, tile=tile: e.activation(
                    dumpb[:], xr[:, tile * D:(tile + 1) * D], AF.Copy, bias=0.0, scale=1.0))
                SY.wait(SC, t_c)
                tok = SY.dma(lambda e, tile=tile: e.dma_start(out=y_flat[tile * 128:(tile + 1) * 128, :], in_=dumpb[:]))
                SC.wait_dma(SY, tok)

        SY.wait_dma_all(SY)
        GP.wait_dma_all(GP)

        finalize_barrier_ops()
        with nc.Block() as blk:
            blk.tensor(lambda e: TE.run(e))
            blk.vector(lambda e: VE.run(e))
            blk.scalar(lambda e: SC.run(e))
            blk.gpsimd(lambda e: GP.run(e))
            blk.sync(lambda e: SY.run(e))

    return nc


_NC_CACHE = {}


def kernel(L0_attn_a_x, L0_attn_a_wq, L0_attn_a_wk, L0_attn_a_wv, L0_attn_a_wo,
           L0_attn_b_Wqkv_w, L0_attn_b_Wqkv_b, L0_attn_b_Wo_w, L0_attn_b_Wo_b,
           L0_attn_b_ghost, L0_ffn_w_gate, L0_ffn_w_up, L0_ffn_w_down):
    n_sub = N_SUB
    key = (n_sub, tuple(DEBUG))
    if key not in _NC_CACHE:
        _NC_CACHE[key] = build(n_sub)
    nc = _NC_CACHE[key]
    x = np.ascontiguousarray(L0_attn_a_x)
    in_maps = []
    for c in range(8):
        in_maps.append({
            "x": x[2 * c:2 * c + 2],
            "wq_a": L0_attn_a_wq, "wk_a": L0_attn_a_wk, "wv_a": L0_attn_a_wv, "wo_a": L0_attn_a_wo,
            "wqkv_b": L0_attn_b_Wqkv_w, "bqkv_b": L0_attn_b_Wqkv_b,
            "wo_b": L0_attn_b_Wo_w, "bo_b": L0_attn_b_Wo_b, "ghost": L0_attn_b_ghost,
            "w_gate": L0_ffn_w_gate, "w_up": L0_ffn_w_up, "w_down": L0_ffn_w_down,
        })
    res = run_bass_kernel_spmd(nc, in_maps, core_ids=list(range(8)))
    global _dbg_out
    _dbg_out = {k: np.stack([res.results[c]["dbg_" + k] for c in range(8)]) for k in DEBUG}
    return np.concatenate([res.results[c]["y"] for c in range(8)], axis=0)


# version 05
# speedup vs baseline: 1.0115x; 1.0115x over previous
import contextlib
import numpy as np
import concourse.bass as bass
import concourse.mybir as mybir
from concourse.bass_utils import run_bass_kernel_spmd

F32 = mybir.dt.float32
BF16 = mybir.dt.bfloat16
AF = mybir.ActivationFunctionType
ALU = mybir.AluOpType
AX = mybir.AxisListType

MAGIC = float(2 ** 23 + 2 ** 22)
NEG = -1.0e30

D = 1024
S = 2048
B = 2
NT = 32
NTS = 16
NH = 16
HD = 64
NKV = 4
INNER = 4096
N_SUB = 4
K_DBUF = True
K_PIPE_GU = True
K_PIPE_DN = True
K_WD_KG = True
DEBUG = []
_dbg_out = {}


NSLOT = 4


class St:

    def __init__(self, name):
        self.name = name
        self.ops = []
        self.n = 0
        self.dn = [0] * NSLOT
        self.sem = None
        self.dsem = [None] * NSLOT

    def do(self, fn):
        self.ops.append(("o", fn))

    def inc(self, fn):
        self.n += 1
        self.ops.append(("i", fn))
        return self.n

    def wait(self, other, tok):
        if tok > 0:
            self.ops.append(("w", other, tok))

    def dma(self, fn, slot=0):
        self.dn[slot] += 16
        self.ops.append(("d", fn, slot))
        return (slot, self.dn[slot])

    def wait_dma(self, other, tok):
        if tok and tok[1] > 0:
            self.ops.append(("wd", other, tok))

    def wait_dma_all(self, other):
        for s in range(NSLOT):
            if other.dn[s]:
                self.ops.append(("wd", other, (s, other.dn[s])))

    def run(self, e):
        for op in self.ops:
            k = op[0]
            if k == "o":
                op[1](e)
            elif k == "i":
                op[1](e).then_inc(self.sem, 1)
            elif k == "d":
                op[1](e).then_inc(self.dsem[op[2]], 16)
            elif k == "w":
                e.wait_ge(op[1].sem, op[2])
            elif k == "wd":
                slot, v = op[2]
                e.wait_ge(op[1].dsem[slot], v)
            elif k == "b":
                e.sem_inc(op[1], op[2])
            elif k == "wb":
                e.wait_ge(op[1], op[2])


def c512(q0, qend):
    out = []
    c0 = q0
    while c0 < qend:
        nxt = min((c0 // 512 + 1) * 512, qend)
        out.append((c0, nxt - c0, c0 // 512))
        c0 = nxt
    return out


def build(n_sub=N_SUB, DEBUG=None):
    DEBUG = DEBUG if DEBUG is not None else globals()['DEBUG']
    K_DBUF = globals()["K_DBUF"]
    K_PIPE_GU = globals()["K_PIPE_GU"]
    K_PIPE_DN = globals()["K_PIPE_DN"]
    K_WD_KG = globals()["K_WD_KG"]
    nc = bass.Bass("TRN2", target_bir_lowering=False)

    x_in = nc.dram_tensor("x", [B, S, D], F32, kind="ExternalInput").ap()
    y_out = nc.dram_tensor("y", [B, S, D], F32, kind="ExternalOutput").ap()
    wq_a = nc.dram_tensor("wq_a", [D, D], F32, kind="ExternalInput").ap()
    wk_a = nc.dram_tensor("wk_a", [NKV * HD, D], F32, kind="ExternalInput").ap()
    wv_a = nc.dram_tensor("wv_a", [NKV * HD, D], F32, kind="ExternalInput").ap()
    wo_a = nc.dram_tensor("wo_a", [D, D], F32, kind="ExternalInput").ap()
    wqkv_b = nc.dram_tensor("wqkv_b", [3 * D, D], F32, kind="ExternalInput").ap()
    bqkv_b = nc.dram_tensor("bqkv_b", [3 * D], F32, kind="ExternalInput").ap()
    wo_b = nc.dram_tensor("wo_b", [D, D], F32, kind="ExternalInput").ap()
    bo_b = nc.dram_tensor("bo_b", [D], F32, kind="ExternalInput").ap()
    ghost = nc.dram_tensor("ghost", [NH], F32, kind="ExternalInput").ap()
    w_gate = nc.dram_tensor("w_gate", [INNER, D], F32, kind="ExternalInput").ap()
    w_up = nc.dram_tensor("w_up", [INNER, D], F32, kind="ExternalInput").ap()
    w_down = nc.dram_tensor("w_down", [D, INNER], F32, kind="ExternalInput").ap()

    scr = {}
    scr["wqT_a"] = nc.dram_tensor("scr_wqT_a", [D, D], BF16, kind="Internal").ap()
    scr["wkT_a"] = nc.dram_tensor("scr_wkT_a", [D, NKV * HD], BF16, kind="Internal").ap()
    scr["wvT_a"] = nc.dram_tensor("scr_wvT_a", [D, NKV * HD], BF16, kind="Internal").ap()
    scr["woT_a"] = nc.dram_tensor("scr_woT_a", [D, D], BF16, kind="Internal").ap()
    scr["qkvT_b"] = nc.dram_tensor("scr_qkvT_b", [D, 3 * D], BF16, kind="Internal").ap()
    scr["woT_b"] = nc.dram_tensor("scr_woT_b", [D, D], BF16, kind="Internal").ap()
    scr["wgT"] = nc.dram_tensor("scr_wgT", [D, INNER], BF16, kind="Internal").ap()
    scr["wuT"] = nc.dram_tensor("scr_wuT", [D, INNER], BF16, kind="Internal").ap()
    scr["wdT"] = nc.dram_tensor("scr_wdT", [INNER, D], BF16, kind="Internal").ap()

    x_flat = x_in.rearrange("b s d -> (b s) d")
    y_flat = y_out.rearrange("b s d -> (b s) d")

    dbg_tensors = {}

    def dbg(name, ap_fn, shape, dt=F32):
        if name not in DEBUG:
            return
        t = nc.dram_tensor("dbg_" + name, shape, dt, kind="ExternalOutput").ap()
        dbg_tensors[name] = t
        ap_fn(t)

    TE, VE, SC, GP, SY = (St(n) for n in ("te", "ve", "sc", "gp", "sy"))
    streams = [TE, VE, SC, GP, SY]
    barn = [0]

    def barrier():
        for s in streams:
            s.wait_dma_all(s)
            s.ops.append(("b", None, 1))
        barn[0] += len(streams)
        for s in streams:
            s.ops.append(("wb", None, barn[0]))

    uid = [0]
    with contextlib.ExitStack() as ctx:
        def sb(name, shape, dt=BF16, st=ctx):
            uid[0] += 1
            return st.enter_context(nc.sbuf_tensor("%s_%d" % (name, uid[0]), shape, dt))

        def psum(name, shape, dt=F32, st=ctx):
            uid[0] += 1
            return st.enter_context(nc.psum_tensor("%s_%d" % (name, uid[0]), shape, dt))

        for s in streams:
            s.sem = ctx.enter_context(nc.semaphore(s.name + "_c"))
            for sl in range(NSLOT):
                s.dsem[sl] = ctx.enter_context(nc.semaphore("%s_d%d" % (s.name, sl)))
        barsem = ctx.enter_context(nc.semaphore("bar"))
        for s in streams:
            pass

        def finalize_barrier_ops():
            for s in streams:
                s.ops = [(op[0], barsem, op[2]) if op[0] in ("b", "wb") else op for op in s.ops]

        xr = sb("xr", [128, NT * D], BF16)
        ident = sb("ident", [128, 128], BF16)
        identf = sb("identf", [128, 128], F32)
        maskS = sb("maskS", [128, 128], F32)
        maskT = sb("maskT", [128, 128], F32)
        e16 = sb("e16", [16, 8 * 128], BF16)
        ones1 = sb("ones1", [1, 128], BF16)
        ones1f = sb("ones1f", [1, 128], F32)
        onescol = sb("onescol", [128, 1], F32)
        ghost_sb = sb("ghost_sb", [16, 1], F32)
        eps_t = sb("eps_t", [128, 1], F32)
        magic_t = sb("magic_t", [128, 1], F32)
        bqkT = sb("bqkT", [128, 24], F32)
        bv_row = sb("bv_row", [1, D], BF16)
        bo_row = sb("bo_row", [1, D], BF16)
        meanw = sb("meanw", [128, 3], F32)
        lnstat = sb("lnstat", [128, 16], F32)
        qtmp = sb("qtmp", [128, 8], F32)
        st8 = sb("st8", [128, 8], F32)
        trash1k = sb("trash1k", [128, 1024], BF16)
        xln_tok = sb("xln_tok", [128, 2 * 1024], BF16)
        bufs = {}
        psLN = psum("psLN", [128, 1024], BF16)
        ps_small = psum("ps_small", [128, 128], F32)

        GP.do(lambda e: e.memset(ident[:], 0.0))
        GP.do(lambda e: e.affine_select(out=ident[:], in_=ident[:], compare_op=ALU.not_equal,
                                        fill=1.0, base=0, pattern=[[-1, 128]], channel_multiplier=1))
        GP.do(lambda e: e.memset(identf[:], 0.0))
        GP.do(lambda e: e.affine_select(out=identf[:], in_=identf[:], compare_op=ALU.not_equal,
                                        fill=1.0, base=0, pattern=[[-1, 128]], channel_multiplier=1))
        GP.do(lambda e: e.memset(maskS[:], 0.0))
        GP.do(lambda e: e.affine_select(out=maskS[:], in_=maskS[:], compare_op=ALU.is_ge,
                                        fill=NEG, base=0, pattern=[[-1, 128]], channel_multiplier=1))
        GP.do(lambda e: e.memset(maskT[:], 0.0))
        GP.do(lambda e: e.affine_select(out=maskT[:], in_=maskT[:], compare_op=ALU.is_ge,
                                        fill=NEG, base=0, pattern=[[1, 128]], channel_multiplier=-1))
        GP.do(lambda e: e.memset(e16[:], 0.0))
        for c in range(8):
            for half in range(2):
                cols = slice(c * 128 + half * 64, c * 128 + half * 64 + 64)
                GP.do(lambda e, cols=cols, c=c, half=half: e.affine_select(
                    out=e16[:, cols], in_=e16[:, cols], compare_op=ALU.not_equal,
                    fill=1.0, base=-(2 * c + half), pattern=[[0, 64]], channel_multiplier=1))
        GP.do(lambda e: e.memset(ones1[:], 1.0))
        GP.do(lambda e: e.memset(ones1f[:], 1.0))
        GP.do(lambda e: e.memset(onescol[:], 1.0))
        GP.do(lambda e: e.memset(eps_t[:], 1e-5))
        t_gp_const = GP.inc(lambda e: e.memset(magic_t[:], MAGIC))
        t_ghost = GP.dma(lambda e: e.dma_start(out=ghost_sb[:], in_=ghost.rearrange("(h o) -> h o", o=1)), slot=2)
        t_bv = GP.dma(lambda e: e.dma_start(out=bv_row[:], in_=bqkv_b.rearrange("(o d) -> o d", o=1)[0:1, 2048:3072]), slot=2)
        t_bo = GP.dma(lambda e: e.dma_start(out=bo_row[:], in_=bo_b.rearrange("(o d) -> o d", o=1)), slot=2)
        t_x = 0
        for piece in range(4):
            dst = xr[:].rearrange("p (t d) -> p t d", d=D)[:, piece * 8:(piece + 1) * 8, :]
            src = x_flat[piece * 1024:(piece + 1) * 1024, :].rearrange("(t p) d -> p t d", p=128)
            t_x = GP.dma(lambda e, dst=dst, src=src: e.dma_start(out=dst, in_=src), slot=1)

        with contextlib.ExitStack() as pctx:
            wbuf = sb("wbuf", [128, 1024], BF16, pctx)
            wfl = sb("wfl", [128, 8192], F32, pctx)
            wql = sb("wql", [128, 1024], BF16, pctx)
            wtb = sb("wtb", [128, 8 * 1024], BF16, pctx)
            absacc = sb("absacc", [128, 32], F32, pctx)
            b24 = sb("b24", [128, 128], F32, pctx)
            psT0 = psum("psT0", [128, 1024], BF16, pctx)

            GP.do(lambda e: e.memset(b24[:], 0.0))
            t_b24 = GP.dma(lambda e: e.dma_start(out=b24[0:24, :], in_=bqkv_b.rearrange("(a b) -> a b", b=128)), slot=2)
            TE.wait_dma(GP, t_b24)
            TE.wait(GP, t_gp_const)
            t_te_b = TE.inc(lambda e: e.transpose(ps_small[:], b24[:], identf[:]))
            VE.wait(TE, t_te_b)
            t_bqk = VE.inc(lambda e: e.tensor_copy(bqkT[:], ps_small[:, 0:24]))
            TE.wait(VE, t_bqk)

            state = {"last_store": 0, "last_evac": 0}

            def prep_cast_w2(w_ap, dout, scr_ap):
                ntile = dout // 128
                tok_s = 0
                for g0 in range(0, ntile, 8):
                    gn = min(8, ntile - g0)
                    for mm in range(gn):
                        m = g0 + mm
                        tok_l = GP.dma(lambda e, m=m, w_ap=w_ap: e.dma_start(out=wbuf[:], in_=w_ap[m * 128:(m + 1) * 128, :]))
                        TE.wait_dma(GP, tok_l)
                        VE.wait_dma(SY, state["last_store"])
                        for k in range(8):
                            fn = lambda e, k=k: e.transpose(psT0[:, k * 128:(k + 1) * 128], wbuf[:, k * 128:(k + 1) * 128], ident[:])
                            if k == 7:
                                TE.inc(fn)
                            else:
                                TE.do(fn)
                        VE.wait(TE, TE.n)
                        dst = wtb[:].rearrange("p (k d) -> p k d", k=8)[:, :, mm * 128:(mm + 1) * 128]
                        src = psT0[:].rearrange("p (k c) -> p k c", k=8)
                        tok_e = VE.inc(lambda e, dst=dst, src=src: e.tensor_copy(dst, src))
                        state["last_evac"] = tok_e
                        GP.wait(TE, TE.n)
                        TE.wait(VE, tok_e)
                    for k in range(8):
                        SY.wait(VE, state["last_evac"])
                        tok_s = SY.dma(lambda e, k=k, scr_ap=scr_ap, g0=g0, gn=gn: e.dma_start(
                            out=scr_ap[k * 128:(k + 1) * 128, g0 * 128:g0 * 128 + gn * 128],
                            in_=wtb[:, k * 1024:k * 1024 + gn * 128]), slot=3)
                    state["last_store"] = tok_s
                return tok_s

            TE.wait(GP, t_gp_const)
            prep_cast_w2(wq_a, D, scr["wqT_a"])
            prep_cast_w2(wk_a, NKV * HD, scr["wkT_a"])
            prep_cast_w2(wv_a, NKV * HD, scr["wvT_a"])
            prep_cast_w2(wo_a, D, scr["woT_a"])
            prep_cast_w2(wqkv_b, 3 * D, scr["qkvT_b"])
            prep_cast_w2(wo_b, D, scr["woT_b"])

            def ffn_mean_pass(w_ap, ntile, ncol, meanw_col):
                tok_r = 0
                r_hist = {}
                for m in range(ntile):
                    off = (m % 2) * 4096
                    if m - 2 in r_hist:
                        SY.wait(VE, r_hist[m - 2])
                    tok_l = SY.dma(lambda e, m=m, w_ap=w_ap, ncol=ncol, off=off: e.dma_start(
                        out=wfl[:, off:off + ncol], in_=w_ap[m * 128:(m + 1) * 128, :]), slot=m % 2)
                    VE.wait_dma(SY, tok_l)
                    tok_r = VE.inc(lambda e, m=m, ncol=ncol, off=off: e.reduce_sum(
                        absacc[:, m:m + 1], wfl[:, off:off + ncol], axis=AX.X, apply_absolute_value=True))
                    r_hist[m] = tok_r
                SY.wait(VE, tok_r)
                VE.wait(VE, tok_r)
                t1 = VE.inc(lambda e, ntile=ntile: e.reduce_sum(absacc[:, 0:1], absacc[:, 0:ntile], axis=AX.X))
                TE.wait(VE, t1)
                t2 = TE.inc(lambda e: e.matmul(ps_small[0:1, 0:1], absacc[:, 0:1], onescol[:], start=True, stop=True))
                VE.wait(TE, t2)
                t3 = VE.inc(lambda e, n=float(ntile * 128 * ncol): e.tensor_scalar(
                    st8[0:1, 0:1], ps_small[0:1, 0:1], 1.0 / n, 1e-5, ALU.mult, ALU.max))
                TE.wait(VE, t3)
                t4 = TE.inc(lambda e: e.matmul(ps_small[:, 0:1], ones1f[0:1, :], st8[0:1, 0:1], start=True, stop=True))
                VE.wait(TE, t4)
                t5 = VE.inc(lambda e, c=meanw_col: e.tensor_copy(meanw[:, c:c + 1], ps_small[:, 0:1]))
                VE.wait(VE, t5)
                t6 = VE.inc(lambda e, c=meanw_col: e.reciprocal(st8[:, 1:2], meanw[:, c:c + 1]))
                TE.wait(VE, t6)
                SC.wait(VE, t6)
                return t6

            def ffn_tern_tile(src_ap_fn, m):
                tok_l = SY.dma(lambda e, m=m: e.dma_start(out=wfl[:, 0:1024], in_=src_ap_fn(m)), slot=0)
                SC.wait_dma(SY, tok_l)
                ta = SC.inc(lambda e: e.activation(wfl[:, 0:1024], wfl[:, 0:1024], AF.Identity, bias=magic_t[:], scale=st8[:, 1:2]))
                SC.wait(SC, ta)
                tb = SC.inc(lambda e: e.activation(wfl[:, 0:1024], wfl[:, 0:1024], AF.Copy, bias=-MAGIC, scale=1.0))
                VE.wait(SC, tb)
                tc = VE.inc(lambda e: e.tensor_scalar(wql[:, 0:1024], wfl[:, 0:1024], -1.0, 1.0, ALU.max, ALU.min))
                SY.wait(VE, tc)
                TE.wait(VE, tc)
                return tc

            def prep_ffn_gu(w_ap, scr_ap, meanw_col):
                ffn_mean_pass(w_ap, 32, 1024, meanw_col)
                for grp in range(4):
                    for mm in range(8):
                        m = grp * 8 + mm
                        VE.wait_dma(SY, state["last_store"])
                        ffn_tern_tile(lambda m: w_ap[m * 128:(m + 1) * 128, :], m)
                        for k in range(8):
                            fn = lambda e, k=k: e.transpose(psT0[:, k * 128:(k + 1) * 128], wql[:, k * 128:(k + 1) * 128], ident[:])
                            if k == 7:
                                TE.inc(fn)
                            else:
                                TE.do(fn)
                        VE.wait(TE, TE.n)
                        dst = wtb[:].rearrange("p (k d) -> p k d", k=8)[:, :, mm * 128:(mm + 1) * 128]
                        src = psT0[:].rearrange("p (k c) -> p k c", k=8)
                        tok_e = VE.inc(lambda e, dst=dst, src=src: e.tensor_copy(dst, src))
                        state["last_evac"] = tok_e
                        TE.wait(VE, tok_e)
                        SC.wait(VE, tok_e)
                    for k in range(8):
                        SY.wait(VE, state["last_evac"])
                        tok_s = SY.dma(lambda e, k=k, scr_ap=scr_ap, grp=grp: e.dma_start(
                            out=scr_ap[k * 128:(k + 1) * 128, grp * 1024:(grp + 1) * 1024],
                            in_=wtb[:, k * 1024:(k + 1) * 1024]), slot=3)
                    state["last_store"] = tok_s

            def prep_ffn_wd():
                ffn_mean_pass(w_down, 8, 4096, 2)
                for kg in range(4):
                    for m in range(8):
                        VE.wait_dma(SY, state["last_store"])
                        ffn_tern_tile(lambda m, kg=kg: w_down[m * 128:(m + 1) * 128, kg * 1024:(kg + 1) * 1024], m)
                        for k in range(8):
                            fn = lambda e, k=k: e.transpose(psT0[:, k * 128:(k + 1) * 128], wql[:, k * 128:(k + 1) * 128], ident[:])
                            if k == 7:
                                TE.inc(fn)
                            else:
                                TE.do(fn)
                        VE.wait(TE, TE.n)
                        dst = wtb[:].rearrange("p (k d) -> p k d", k=8)[:, :, m * 128:(m + 1) * 128]
                        src = psT0[:].rearrange("p (k c) -> p k c", k=8)
                        tok_e = VE.inc(lambda e, dst=dst, src=src: e.tensor_copy(dst, src))
                        state["last_evac"] = tok_e
                        TE.wait(VE, tok_e)
                        SC.wait(VE, tok_e)
                    for k in range(8):
                        SY.wait(VE, state["last_evac"])
                        tok_s = SY.dma(lambda e, k=k, kg=kg: e.dma_start(
                            out=scr["wdT"][(kg * 8 + k) * 128:(kg * 8 + k + 1) * 128, :],
                            in_=wtb[:, k * 1024:(k + 1) * 1024]), slot=3)
                    state["last_store"] = tok_s

            prep_ffn_gu(w_gate, scr["wgT"], 0)
            prep_ffn_gu(w_up, scr["wuT"], 1)
            prep_ffn_wd()
            barrier()

        ln_state = {}

        def emit_ln_tile(tile, dst_fm, gate_tok, quant, par=0):
            pb = par * 8
            sx, ssq = lnstat[:, pb + 0:pb + 1], lnstat[:, pb + 1:pb + 2]
            mean, rstd, nbias = lnstat[:, pb + 2:pb + 3], lnstat[:, pb + 3:pb + 4], lnstat[:, pb + 4:pb + 5]
            src = xr[:, tile * D:(tile + 1) * D]
            SC.wait(VE, gate_tok)
            VE.wait(VE, gate_tok)
            VE.wait(SC, ln_state.get(par, 0))
            t1 = SC.inc(lambda e, src=src: e.activation(trash1k[:], src, AF.Square, bias=0.0, scale=1.0, accum_out=ssq))
            t2 = VE.inc(lambda e, src=src: e.reduce_sum(sx, src, axis=AX.X))
            VE.wait(VE, t2)
            t3 = VE.inc(lambda e: e.tensor_scalar_mul(mean, sx, 1.0 / D))
            VE.wait(VE, t3)
            t4 = VE.inc(lambda e: e.tensor_mul(nbias, mean, mean))
            VE.wait(SC, t1)
            VE.wait(VE, t4)
            t5 = VE.inc(lambda e: e.scalar_tensor_tensor(rstd, ssq, 1.0 / D, nbias, ALU.mult, ALU.subtract))
            SC.wait(VE, t5)
            t6 = SC.inc(lambda e: e.activation(rstd, rstd, AF.Sqrt, bias=eps_t[:], scale=1.0))
            VE.wait(SC, t6)
            t7 = VE.inc(lambda e: e.reciprocal(rstd, rstd))
            VE.wait(VE, t7)
            t8 = VE.inc(lambda e: e.tensor_mul(nbias, mean, rstd))
            VE.wait(VE, t8)
            t9 = VE.inc(lambda e: e.tensor_scalar_mul(nbias, nbias, -1.0))
            SC.wait(VE, t9)
            xtok = xln_tok[:, par * 1024:(par + 1) * 1024]
            if not quant:
                t_x = SC.inc(lambda e, src=src, xtok=xtok: e.activation(xtok, src, AF.Identity, bias=nbias, scale=rstd))
            else:
                amax, sxq, qsc = lnstat[:, pb + 5:pb + 6], lnstat[:, pb + 6:pb + 7], lnstat[:, pb + 7:pb + 8]
                xf = bufs["xf"][:, par * 1024:(par + 1) * 1024]
                t_f = SC.inc(lambda e, src=src, xf=xf: e.activation(xf, src, AF.Identity, bias=nbias, scale=rstd))
                VE.wait(SC, t_f)
                ta = VE.inc(lambda e, xf=xf, amax=amax: e.tensor_reduce(amax, xf, AX.X, ALU.max, apply_absolute_value=True))
                VE.wait(VE, ta)
                tb = VE.inc(lambda e, amax=amax: e.tensor_scalar_max(amax, amax, 1e-5))
                VE.wait(VE, tb)
                tc = VE.inc(lambda e, amax=amax, sxq=sxq: e.reciprocal(sxq, amax))
                VE.wait(VE, tc)
                td = VE.inc(lambda e, sxq=sxq: e.tensor_scalar_mul(sxq, sxq, 127.0))
                tq = VE.inc(lambda e, amax=amax, qsc=qsc: e.tensor_scalar_mul(qsc, amax, 1.0 / 127.0))
                SC.wait(VE, td)
                tg = SC.inc(lambda e, xf=xf, sxq=sxq: e.activation(xf, xf, AF.Identity, bias=magic_t[:], scale=sxq))
                SC.wait(SC, tg)
                t_x = SC.inc(lambda e, xf=xf, xtok=xtok: e.activation(xtok, xf, AF.Copy, bias=-MAGIC, scale=1.0))
            ln_state[par] = t_x
            TE.wait(SC, t_x)
            for k in range(8):
                fn = lambda e, k=k, xtok=xtok: e.transpose(psLN[:, k * 128:(k + 1) * 128], xtok[:, k * 128:(k + 1) * 128], ident[:])
                if k == 7:
                    TE.inc(fn)
                else:
                    TE.do(fn)
            VE.wait(TE, TE.n)
            src_ps = psLN[:].rearrange("p (k c) -> p k c", k=8)
            t_e = VE.inc(lambda e, dst=dst_fm, src=src_ps: e.tensor_copy(dst_fm, src))
            TE.wait(VE, t_e)
            return t_e

        def emit_attn(is_b, resid_tok):
            nkv = NH if is_b else NKV
            vheads = 8 if is_b else 4
            with contextlib.ExitStack() as actx:
                vbuf = sb("vbuf", [128, 16 * vheads * 65], BF16, actx)
                ctxb = sb("ctxb", [128, 8 * 2048], BF16, actx)
                kbuf = sb("kbuf", [64, 2048], BF16, actx)
                qbuf = sb("qbuf", [64, 2048], BF16, actx)
                ptb = sb("ptb", [128, 2 * 2048], BF16, actx)
                lrow = sb("lrow", [65, 2048], BF16, actx)
                lmatT = sb("lmatT", [16, 2048], F32, actx)
                ctxn = sb("ctxn", [128, 1024], BF16, actx)
                recipb = sb("recipb", [16, 2048], BF16, actx)
                wslice = sb("wslice", [128, 2 * 1024], BF16, actx)
                if is_b:
                    mabuf = sb("mabuf", [128, 16 * 16], F32, actx)
                    mTb = lrow[0:16, :]
                ps_st = [psum("ps_st0", [128, 512], F32, actx), psum("ps_st1", [128, 512], F32, actx)]
                ps_ctx = psum("ps_ctx", [128, 2048], F32, actx)

                for seq in range(B):
                    base = seq * NTS
                    qoff = 0
                    with contextlib.ExitStack() as bctx:
                        xlnT = sb("xlnT", [128, 8 * 2048], BF16, bctx)
                        wv_sb = sb("wv_sb", [128, 8 * vheads * 64], BF16, bctx)
                        if not is_b:
                            wk_sb = sb("wk_sb", [128, 8 * 256], BF16, bctx)
                        xlnT_r = xlnT[:].rearrange("p (k t) -> p k t", k=8)

                        for tt in range(NTS):
                            tile = base + tt
                            emit_ln_tile(tile, xlnT_r[:, :, tt * 128:(tt + 1) * 128], resid_tok[tile], quant=False, par=tt % 2)
                        t_ln_all = VE.n
                        if seq == 0 and not is_b:
                            def _tap(name, ap, shape):
                                def go(t):
                                    for s in (VE, SC, TE):
                                        GP.wait(s, s.n)
                                    GP.dma(lambda e, t=t, ap=ap: e.dma_start(out=t, in_=ap))
                                dbg(name, go, shape)
                            _tap("xlnT", xlnT[:, 0:2048], [128, 2048])

                        if not is_b:
                            t_wk = 0
                            for k in range(8):
                                t_wk = SY.dma(lambda e, k=k: e.dma_start(out=wk_sb[:, k * 256:(k + 1) * 256],
                                                                  in_=scr["wkT_a"][k * 128:(k + 1) * 128, :]), slot=2)

                        prev_proj_mm = [0, 0]
                        wsl_tok = [0] * (NH + 1)

                        def prefetch_slices(h):
                            if h >= NH:
                                return
                            wb = (h % 2) * 1024
                            SY.wait(TE, prev_proj_mm[h % 2])
                            if is_b:
                                SY.dma(lambda e, h=h, wb=wb: e.dma_start(
                                    out=wslice[:, wb:wb + 512].rearrange("p (k c) -> p k c", k=8),
                                    in_=scr["qkvT_b"][:, h * 64:(h + 1) * 64].rearrange("(k p) c -> p k c", p=128)), slot=h % 2)
                                wsl_tok[h] = SY.dma(lambda e, h=h, wb=wb: e.dma_start(
                                    out=wslice[:, wb + 512:wb + 1024].rearrange("p (k c) -> p k c", k=8),
                                    in_=scr["qkvT_b"][:, 1024 + h * 64:1024 + (h + 1) * 64].rearrange("(k p) c -> p k c", p=128)), slot=h % 2)
                            else:
                                wsl_tok[h] = SY.dma(lambda e, h=h, wb=wb: e.dma_start(
                                    out=wslice[:, wb:wb + 512].rearrange("p (k c) -> p k c", k=8),
                                    in_=scr["wqT_a"][:, h * 64:(h + 1) * 64].rearrange("(k p) c -> p k c", p=128)), slot=h % 2)

                        prefetch_slices(0)
                        t_kq = 0
                        t_av_hist = {}
                        t_ctx_ev = 0
                        t_lm = [0] * NH
                        for h in range(NH):
                            g = h if is_b else h // 4
                            hh = g % vheads
                            if (is_b and h % 8 == 0) or (not is_b and h == 0):
                                v0 = (h // 8) * 8 if is_b else 0
                                ndv = vheads * 64
                                GP.wait(TE, TE.n)
                                t_vb = GP.inc(lambda e: e.memset(vbuf[:], 1.0))
                                t_wv = 0
                                if is_b:
                                    vsrc = scr["qkvT_b"]
                                    col0 = 2048 + v0 * 64
                                else:
                                    vsrc = scr["wvT_a"]
                                    col0 = 0
                                for k in range(8):
                                    t_wv = SY.dma(lambda e, k=k, vsrc=vsrc, col0=col0, ndv=ndv: e.dma_start(
                                        out=wv_sb[:, k * ndv:(k + 1) * ndv],
                                        in_=vsrc[k * 128:(k + 1) * 128, col0:col0 + ndv]), slot=2)
                                TE.wait(GP, t_vb)
                                TE.wait_dma(SY, t_wv)
                                TE.wait(VE, t_ln_all)
                                vbuf_r = vbuf[:].rearrange("p (t h c) -> p t h c", t=16, h=vheads)
                                for tt in range(NTS):
                                    for (c0, cw, bank) in c512(0, ndv):
                                        for k in range(8):
                                            last = (k == 7) and (not is_b)
                                            fn = lambda e, k=k, tt=tt, c0=c0, cw=cw, last=last: e.matmul(
                                                ps_ctx[:, c0:c0 + cw],
                                                xlnT_r[:, k, tt * 128:(tt + 1) * 128],
                                                wv_sb[:, k * ndv + c0:k * ndv + c0 + cw],
                                                start=(k == 0), stop=last)
                                            if last:
                                                t_mmv = TE.inc(fn)
                                            else:
                                                TE.do(fn)
                                        if is_b:
                                            t_mmv = TE.inc(lambda e, c0=c0, cw=cw, col0=col0: e.matmul(
                                                ps_ctx[:, c0:c0 + cw], ones1[:],
                                                bv_row[0:1, col0 - 2048 + c0:col0 - 2048 + c0 + cw],
                                                start=False, stop=True))
                                        VE.wait(TE, t_mmv)
                                        VE.wait(GP, t_vb)
                                        nh0 = c0 // 64
                                        nh1 = (c0 + cw) // 64
                                        dst = vbuf_r[:, tt, nh0:nh1, 0:64]
                                        t_vev = VE.inc(lambda e, dst=dst, c0=c0, cw=cw: e.tensor_copy(
                                            dst, ps_ctx[:, c0:c0 + cw].rearrange("p (h c) -> p h c", c=64)))
                                        TE.wait(VE, t_vev)

                            def proj64(buf, wof, bias):
                                t_ev = 0
                                ev_hist = {}
                                for pi, (c0, cw, bank) in enumerate(c512(0, 2048)):
                                    pq = ps_st[pi % 2]
                                    if pi - 2 in ev_hist:
                                        TE.wait(SC, ev_hist[pi - 2])
                                    for k in range(8):
                                        fn = lambda e, k=k, c0=c0, cw=cw, buf=buf, wof=wof, pq=pq: e.matmul(
                                            pq[0:64, 0:cw], wof(k), xlnT_r[:, k, c0:c0 + cw],
                                            start=(k == 0), stop=(k == 7))
                                        if k == 7:
                                            t_mm = TE.inc(fn)
                                        else:
                                            TE.do(fn)
                                    SC.wait(TE, t_mm)
                                    if bias is None:
                                        t_ev = SC.inc(lambda e, buf=buf, c0=c0, cw=cw, pq=pq: e.activation(
                                            buf[:, c0:c0 + cw], pq[0:64, 0:cw], AF.Copy, bias=0.0, scale=1.0))
                                    else:
                                        t_ev = SC.inc(lambda e, buf=buf, c0=c0, cw=cw, bias=bias, pq=pq: e.activation(
                                            buf[:, c0:c0 + cw], pq[0:64, 0:cw], AF.Identity, bias=bias, scale=1.0))
                                    ev_hist[pi] = t_ev
                                TE.wait(SC, t_ev)
                                return t_ev, t_mm

                            wb = (h % 2) * 1024
                            TE.wait_dma(SY, wsl_tok[h])
                            if is_b:
                                wq_k = lambda k, wb=wb: wslice[:, wb + k * 64:wb + (k + 1) * 64]
                                wk_k = lambda k, wb=wb: wslice[:, wb + 512 + k * 64:wb + 512 + (k + 1) * 64]
                                bq = bqkT[(h % 2) * 64:(h % 2) * 64 + 64, h // 2:h // 2 + 1]
                                bk = bqkT[(h % 2) * 64:(h % 2) * 64 + 64, 8 + h // 2:8 + h // 2 + 1]
                            else:
                                TE.wait_dma(SY, t_wk)
                                wq_k = lambda k, wb=wb: wslice[:, wb + k * 64:wb + (k + 1) * 64]
                                wk_k = lambda k, g=g: wk_sb[:, k * 256 + g * 64:k * 256 + (g + 1) * 64]
                                bq = bk = None

                            if is_b or h % 4 == 0:
                                proj64(kbuf, wk_k, bk)
                            t_kq, t_proj_mm = proj64(qbuf, wq_k, bq)
                            prev_proj_mm[h % 2] = t_proj_mm
                            TE.wait(SC, t_kq)
                            prefetch_slices(h + 1)
                            if seq == 0 and h == 0 and not is_b:
                                def _tap2(name, ap, shape):
                                    def go(t):
                                        GP.wait(SC, SC.n)
                                        GP.wait(VE, VE.n)
                                        GP.dma(lambda e, t=t, ap=ap: e.dma_start(out=t, in_=ap))
                                    dbg(name, go, shape)
                                _tap2("kbuf", kbuf[:], [64, 2048])
                                _tap2("qbuf", qbuf[:], [64, 2048])
                                _tap2("vbuf", vbuf[:, 0:1040], [128, 1040])

                            cnt = 0
                            exp_hist = {}
                            avq = []
                            last_av_of = {}

                            def flush_one():
                                fn_av, tok_e, jj = avq.pop(0)
                                TE.wait(SC, tok_e)
                                last_av_of[jj] = TE.inc(fn_av)

                            for j in range(NTS):
                                q0 = j * 128
                                par = j % 2
                                while avq and avq[0][2] <= j - 2:
                                    flush_one()
                                if j - 2 in last_av_of:
                                    SC.wait(TE, last_av_of[j - 2])
                                chunks = c512(q0, 2048)
                                for ci, (c0, cw, bank) in enumerate(chunks):
                                    pst = ps_st[cnt % 2]
                                    if cnt - 2 in exp_hist:
                                        TE.wait(SC, exp_hist[cnt - 2])
                                    fn = lambda e, j=j, c0=c0, cw=cw, pst=pst: e.matmul(
                                        pst[:, 0:cw], kbuf[:, j * 128:(j + 1) * 128], qbuf[:, c0:c0 + cw],
                                        start=True, stop=True)
                                    t_mm = TE.inc(fn)
                                    if ci == 0:
                                        VE.wait(TE, t_mm)
                                        t_mk = VE.inc(lambda e, pst=pst: e.tensor_tensor(pst[:, 0:128], pst[:, 0:128], maskT[:], ALU.add))
                                        SC.wait(VE, t_mk)
                                    SC.wait(TE, t_mm)
                                    t_exp = SC.inc(lambda e, pst=pst, par=par, c0=c0, cw=cw: e.activation(
                                        ptb[:, par * 2048 + c0:par * 2048 + c0 + cw], pst[:, 0:cw],
                                        AF.Exp, bias=0.0, scale=0.125))
                                    exp_hist[cnt] = t_exp
                                    vsl = vbuf[:, j * (vheads * 65) + hh * 65:j * (vheads * 65) + (hh + 1) * 65]
                                    fn_av = lambda e, j=j, vsl=vsl, par=par, c0=c0, cw=cw, bank=bank: e.matmul(
                                        ps_ctx[0:65, c0:c0 + cw], vsl,
                                        ptb[:, par * 2048 + c0:par * 2048 + c0 + cw],
                                        start=(j == 0), stop=(j == min(NTS - 1, 4 * bank + 3)))
                                    avq.append((fn_av, t_exp, j))
                                    while len(avq) > 2:
                                        flush_one()
                                    cnt += 1
                            while avq:
                                flush_one()
                            t_av = last_av_of[NTS - 1]

                            if is_b:
                                scnt = 0
                                mx_hist = {}
                                for i in range(NTS):
                                    L = (i + 1) * 128
                                    chunks = c512(0, L)
                                    mparts = st8[:, 0:4]
                                    for ci, (c0, cw, bank) in enumerate(chunks):
                                        pss = ps_st[scnt % 2]
                                        if scnt - 2 in mx_hist:
                                            TE.wait(VE, mx_hist[scnt - 2])
                                        fn = lambda e, i=i, c0=c0, cw=cw, pss=pss: e.matmul(
                                            pss[:, 0:cw], qbuf[:, i * 128:(i + 1) * 128], kbuf[:, c0:c0 + cw],
                                            start=True, stop=True)
                                        t_mm = TE.inc(fn)
                                        VE.wait(TE, t_mm)
                                        if ci == len(chunks) - 1:
                                            off = L - 128 - c0
                                            t_mk = VE.inc(lambda e, pss=pss, off=off: e.tensor_tensor(
                                                pss[:, off:off + 128], pss[:, off:off + 128], maskS[:], ALU.add))
                                            VE.wait(VE, t_mk)
                                        t_mx = VE.inc(lambda e, pss=pss, ci=ci, cw=cw, mparts=mparts: e.tensor_scalar(
                                            trash1k[:, 0:cw], pss[:, 0:cw], 1.0, None, ALU.mult, ALU.max,
                                            accum_out=mparts[:, ci:ci + 1]))
                                        mx_hist[scnt] = t_mx
                                        scnt += 1
                                    VE.wait(VE, t_mx)
                                    t_mr = VE.inc(lambda e, i=i, h=h, n=len(chunks), mparts=mparts: e.reduce_max(
                                        mabuf[:, i * 16 + h:i * 16 + h + 1], mparts[:, 0:n], axis=AX.X))
                                TE.wait(VE, t_mx)

                            SC.wait(TE, t_av)
                            if h >= 2:
                                SC.wait_dma(GP, t_lm[h - 2])
                            lr = lrow[(h % 2) * 64:(h % 2) * 64 + 1, :]
                            t_l = SC.inc(lambda e, lr=lr: e.copy(lr, ps_ctx[64:65, :]))
                            GP.wait(SC, t_l)
                            t_lm[h] = GP.dma(lambda e, h=h, lr=lr: e.dma_start(out=lmatT[h:h + 1, :], in_=lr), slot=2 + h % 2)
                            dst_c = ctxb[:].rearrange("p (k t) -> p k t", k=8)[(h % 2) * 64:(h % 2) * 64 + 64, h // 2, :]
                            t_ctx_ev = SC.inc(lambda e, dst_c=dst_c: e.activation(dst_c, ps_ctx[0:64, :], AF.Copy, bias=0.0, scale=1.0))
                            TE.wait(SC, t_ctx_ev)
                            TE.wait(SC, t_l)

                    with contextlib.ExitStack() as cctx:
                        wo_sb = sb("wo_sb", [128, 8 * 1024], BF16, cctx)
                        SY.wait(TE, TE.n)
                        SY.wait(SC, SC.n)
                        t_wo = 0
                        wo_src = scr["woT_b"] if is_b else scr["woT_a"]
                        for k in range(8):
                            t_wo = SY.dma(lambda e, k=k, wo_src=wo_src: e.dma_start(
                                out=wo_sb[:, k * 1024:(k + 1) * 1024], in_=wo_src[k * 128:(k + 1) * 128, :]), slot=2)
                        if is_b:
                            VE.wait_dma(GP, t_lm[NH - 1])
                            VE.wait_dma(GP, t_lm[NH - 2])
                            for tt in range(NTS):
                                TE.wait(VE, t_mr)
                                t_tp = TE.inc(lambda e, tt=tt: e.transpose(ps_small[0:16, 0:128], mabuf[:, tt * 16:(tt + 1) * 16], identf[:]))
                                VE.wait(TE, t_tp)
                                t_vm = VE.inc(lambda e, tt=tt: e.tensor_copy(mTb[:, tt * 128:(tt + 1) * 128], ps_small[0:16, 0:128]))
                                TE.wait(VE, t_vm)
                            SC.wait(VE, t_vm)
                            t_em = SC.inc(lambda e: e.activation(mTb[:], mTb[:], AF.Exp, bias=0.0, scale=0.125))
                            VE.wait(SC, t_em)
                            VE.wait_dma(GP, t_lm[NH - 1])
                            VE.wait_dma(GP, t_lm[NH - 2])
                            t_dn = VE.inc(lambda e: e.scalar_tensor_tensor(lmatT[:], mTb[:], ghost_sb[:], lmatT[:], ALU.mult, ALU.add))
                            VE.wait(VE, t_dn)
                            t_rc = VE.inc(lambda e: e.reciprocal(lmatT[:], lmatT[:]))
                        else:
                            VE.wait_dma(GP, t_lm[NH - 1])
                            VE.wait_dma(GP, t_lm[NH - 2])
                            t_rc = VE.inc(lambda e: e.reciprocal(lmatT[:], lmatT[:]))
                        VE.wait(VE, t_rc)
                        t_rc = VE.inc(lambda e: e.tensor_copy(recipb[:], lmatT[:]))

                        if seq == 0 and not is_b:
                            def _tap3(name, ap, shape):
                                def go(t):
                                    GP.wait(VE, VE.n)
                                    GP.wait(SC, SC.n)
                                    GP.dma(lambda e, t=t, ap=ap: e.dma_start(out=t, in_=ap))
                                dbg(name, go, shape)
                            _tap3("lmatT", lmatT[:], [16, 2048])
                            _tap3("ctxb", ctxb[:, 0:2048], [128, 2048])
                            _tap3("ptb", ptb[:], [128, 2048])
                        TE.wait(VE, t_rc)
                        TE.wait(SC, t_ctx_ev)
                        TE.wait_dma(SY, t_wo)
                        for tt in range(NTS):
                            tile = base + tt
                            for c in range(8):
                                fn = lambda e, c=c, tt=tt: e.matmul(
                                    ps_ctx[:, c * 128:(c + 1) * 128], e16[:, c * 128:(c + 1) * 128],
                                    recipb[:, tt * 128:(tt + 1) * 128], start=True, stop=True)
                                if c == 7:
                                    t_rm = TE.inc(fn)
                                else:
                                    TE.do(fn)
                            VE.wait(TE, t_rm)
                            src_c = ctxb[:].rearrange("p (k t) -> p k t", k=8)[:, :, tt * 128:(tt + 1) * 128]
                            dst_n = ctxn[:].rearrange("p (k c) -> p k c", k=8)
                            src_r = ps_ctx[:, 0:1024].rearrange("p (k c) -> p k c", k=8)
                            t_nm = VE.inc(lambda e, src_c=src_c, dst_n=dst_n, src_r=src_r: e.tensor_tensor(dst_n, src_c, src_r, ALU.mult))
                            TE.wait(VE, t_nm)
                            for nn in range(2):
                                for k in range(8):
                                    fn = lambda e, nn=nn, k=k, last=(k == 7 and not is_b): e.matmul(
                                        ps_st[nn][:, 0:512], ctxn[:, k * 128:(k + 1) * 128],
                                        wo_sb[:, k * 1024 + nn * 512:k * 1024 + (nn + 1) * 512],
                                        start=(k == 0), stop=(k == 7 and not is_b))
                                    if k == 7 and not is_b:
                                        t_mo = TE.inc(fn)
                                    else:
                                        TE.do(fn)
                                if is_b:
                                    t_mo = TE.inc(lambda e, nn=nn: e.matmul(
                                        ps_st[nn][:, 0:512], ones1[:], bo_row[0:1, nn * 512:(nn + 1) * 512],
                                        start=False, stop=True))
                                VE.wait(TE, t_mo)
                                dst = xr[:, tile * D + nn * 512:tile * D + (nn + 1) * 512]
                                t_res = VE.inc(lambda e, dst=dst, nn=nn: e.tensor_tensor(dst, ps_st[nn][:, 0:512], dst, ALU.add))
                                TE.wait(VE, t_res)
                            resid_tok[tile] = t_res
                        VE.wait(TE, TE.n)
                        SC.wait(TE, TE.n)

        def emit_ffn(final, resid_tok):
            GT = 8
            with contextlib.ExitStack() as fctx:
                stash = sb("stash", [128, GT * 4096], BF16, fctx)
                qscales = sb("qscales", [128, 4 * GT], F32, fctx)
                bufs["xf"] = sb("xln_f32", [128, 2 * 1024], F32, fctx)
                outstage = sb("outstage", [128, 2 * 512], F32, fctx) if final else None
                t_outdma = {}
                gate_sc = lambda tt: qscales[:, tt:tt + 1]
                up_sc = lambda tt: qscales[:, GT + tt:GT + tt + 1]
                down_sc = lambda tt: qscales[:, 2 * GT + tt:2 * GT + tt + 1]

                last_dn_mm = [0]
                for grp in range(4):
                    with contextlib.ExitStack() as f1:
                        rT = sb("rT", [128, 8 * 1024], BF16, f1)
                        wgu = sb("wgu", [128, 2 * 4096], BF16, f1)
                        silu_t = sb("silu_t", [128, 512], F32, f1)
                        ps_g = [psum("ps_g0", [128, 512], F32, f1), psum("ps_g1", [128, 512], F32, f1)]
                        ps_u = [psum("ps_u0", [128, 512], F32, f1), psum("ps_u1", [128, 512], F32, f1)]
                        rT_r = rT[:].rearrange("p (k t) -> p k t", k=8)
                        t_qsc_tok = [0] * GT
                        for tt in range(GT):
                            tile = grp * GT + tt
                            emit_ln_tile(tile, rT_r[:, :, tt * 128:(tt + 1) * 128], resid_tok[tile], quant=True, par=tt % 2)
                            if grp == 0 and tt == 0:
                                def _tapl(name, ap, shape):
                                    def go(t):
                                        GP.wait(VE, VE.n)
                                        GP.wait(SC, SC.n)
                                        GP.dma(lambda e, t=t, ap=ap: e.dma_start(out=t, in_=ap))
                                    dbg(name, go, shape)
                                _tapl("lnstat0", lnstat[:], [128, 8])
                            qsc = lnstat[:, (tt % 2) * 8 + 7:(tt % 2) * 8 + 8]
                            t1 = VE.inc(lambda e, tt=tt, qsc=qsc: e.tensor_mul(gate_sc(tt), qsc, meanw[:, 0:1]))
                            t2 = VE.inc(lambda e, tt=tt, qsc=qsc: e.tensor_mul(up_sc(tt), qsc, meanw[:, 1:2]))
                            t_qsc_tok[tt] = t2
                        t_wgu = [0, 0]
                        t_gu_by_buf = [0, 0]
                        t_gu2 = 0
                        t_gu2_by_par = {}
                        t_si_by_par = {}
                        CW = 256
                        for c in range(16):
                            buf = c % 2 if K_DBUF else 0
                            SY.wait(TE, t_gu_by_buf[buf])
                            tok = 0
                            for k in range(8):
                                SY.dma(lambda e, k=k, c=c, buf=buf: e.dma_start(
                                    out=wgu[:, buf * 4096 + k * CW:buf * 4096 + (k + 1) * CW],
                                    in_=scr["wgT"][k * 128:(k + 1) * 128, c * CW:(c + 1) * CW]), slot=buf)
                                tok = SY.dma(lambda e, k=k, c=c, buf=buf: e.dma_start(
                                    out=wgu[:, buf * 4096 + 2048 + k * CW:buf * 4096 + 2048 + (k + 1) * CW],
                                    in_=scr["wuT"][k * 128:(k + 1) * 128, c * CW:(c + 1) * CW]), slot=buf)
                            t_wgu[buf] = tok
                            TE.wait_dma(SY, t_wgu[buf])
                            for tt in range(GT):
                                par = tt % 2
                                if par in t_gu2_by_par:
                                    TE.wait(VE, t_gu2_by_par[par])
                                for k in range(8):
                                    TE.do(lambda e, k=k, tt=tt, buf=buf, par=par: e.matmul(
                                        ps_g[par][:, 0:CW], rT_r[:, k, tt * 128:(tt + 1) * 128],
                                        wgu[:, buf * 4096 + k * CW:buf * 4096 + (k + 1) * CW],
                                        start=(k == 0), stop=(k == 7)))
                                    fn = lambda e, k=k, tt=tt, buf=buf, par=par: e.matmul(
                                        ps_u[par][:, 0:CW], rT_r[:, k, tt * 128:(tt + 1) * 128],
                                        wgu[:, buf * 4096 + 2048 + k * CW:buf * 4096 + 2048 + (k + 1) * CW],
                                        start=(k == 0), stop=(k == 7))
                                    if k == 7:
                                        t_gu = TE.inc(fn)
                                    else:
                                        TE.do(fn)
                                t_gu_by_buf[buf] = t_gu
                                SC.wait(TE, t_gu)
                                if c == 0:
                                    SC.wait(VE, t_qsc_tok[tt])
                                if par in t_gu2_by_par:
                                    SC.wait(VE, t_gu2_by_par[par])
                                t_si = SC.inc(lambda e, par=par, tt=tt: e.activation(
                                    silu_t[:, par * CW:par * CW + CW], ps_g[par][:, 0:CW], AF.Silu, bias=0.0, scale=gate_sc(tt)))
                                VE.wait(SC, t_si)
                                dst = stash[:, tt * 4096 + c * CW:tt * 4096 + (c + 1) * CW]
                                t_gu2 = VE.inc(lambda e, dst=dst, par=par, tt=tt: e.scalar_tensor_tensor(
                                    dst, ps_u[par][:, 0:CW], up_sc(tt), silu_t[:, par * CW:par * CW + CW], ALU.mult, ALU.mult))
                                t_gu2_by_par[par] = t_gu2
                                if not K_PIPE_GU:
                                    TE.wait(VE, t_gu2)

                    if grp == 0:
                        def _tapscr(name, src_ap, shape):
                            def go(t):
                                GP.dma(lambda e, t=t, src_ap=src_ap: e.dma_start(out=t, in_=src_ap))
                            dbg(name, go, shape)
                        _tapscr("scrg", scr["wgT"][0:128, 3072:4096], [128, 1024])
                        _tapscr("scru", scr["wuT"][0:128, 3072:4096], [128, 1024])
                        def _tapf(name, ap, shape):
                            def go(t):
                                for s in (VE, SC, TE):
                                    GP.wait(s, s.n)
                                GP.dma(lambda e, t=t, ap=ap: e.dma_start(out=t, in_=ap))
                                for s in (VE, SC, TE):
                                    s.wait_dma_all(GP)
                            dbg(name, go, shape)
                        _tapf("meanw", meanw[:], [128, 3])
                        _tapf("gu0", stash[:, 0:4096], [128, 4096])
                        _tapf("gu4", stash[:, 4 * 4096:5 * 4096], [128, 4096])
                        _tapf("qsc", qscales[:], [128, 32])
                    t_phase1_mm = TE.n
                    with contextlib.ExitStack() as f2:
                        wdf = sb("wdf", [128, 32 * 512], BF16, f2)
                        ps_t2 = [psum("ps_t2a", [128, 2048], BF16, f2), psum("ps_t2b", [128, 2048], BF16, f2)]
                        ps_d = [psum("ps_d0", [128, 512], F32, f2), psum("ps_d1", [128, 512], F32, f2)]
                        t_q_done = [0] * GT
                        u_last = {}
                        bk_hist = {}
                        rcnt = 0
                        for tt in range(GT):
                            tile = grp * GT + tt
                            p = tt % 2
                            amax = qtmp[:, p:p + 1]
                            sx2 = qtmp[:, 2 + p:3 + p]
                            q2t = qtmp[:, 4 + p:5 + p]
                            VE.wait(VE, t_gu2_by_par[1] if 1 in t_gu2_by_par else t_gu2)
                            VE.wait(VE, t_gu2_by_par.get(0, 0))
                            VE.wait(SC, u_last.get(p, 0))
                            ta = VE.inc(lambda e, tt=tt, amax=amax: e.tensor_reduce(
                                amax, stash[:, tt * 4096:(tt + 1) * 4096], AX.X, ALU.max, apply_absolute_value=True))
                            VE.wait(VE, ta)
                            tb = VE.inc(lambda e, amax=amax: e.tensor_scalar_max(amax, amax, 1e-5))
                            VE.wait(VE, tb)
                            tc = VE.inc(lambda e, amax=amax, sx2=sx2: e.reciprocal(sx2, amax))
                            VE.wait(VE, tc)
                            td = VE.inc(lambda e, sx2=sx2: e.tensor_scalar_mul(sx2, sx2, 127.0))
                            tq2 = VE.inc(lambda e, amax=amax, q2t=q2t: e.tensor_scalar_mul(q2t, amax, 1.0 / 127.0))
                            VE.wait(VE, tq2)
                            tq3 = VE.inc(lambda e, tt=tt, q2t=q2t: e.tensor_mul(down_sc(tt), q2t, meanw[:, 2:3]))
                            SC.wait(VE, td)
                            xfh = bufs["xf"][:, p * 1024:(p + 1) * 1024]
                            for hf in range(4):
                                sl = slice(tt * 4096 + hf * 1024, tt * 4096 + (hf + 1) * 1024)
                                u1 = SC.inc(lambda e, sl=sl, xfh=xfh, sx2=sx2: e.activation(xfh, stash[:, sl], AF.Identity, bias=magic_t[:], scale=sx2))
                                SC.wait(SC, u1)
                                u2 = SC.inc(lambda e, sl=sl, xfh=xfh: e.activation(stash[:, sl], xfh, AF.Copy, bias=-MAGIC, scale=1.0))
                                SC.wait(SC, u2)
                            u_last[p] = u2
                            for rnd in range(2):
                                pt = ps_t2[rcnt % 2]
                                if rcnt - 2 in bk_hist:
                                    TE.wait(VE, bk_hist[rcnt - 2])
                                TE.wait(SC, u2)
                                for kk in range(16):
                                    k = rnd * 16 + kk
                                    fn = lambda e, k=k, kk=kk, tt=tt, pt=pt: e.transpose(
                                        pt[:, kk * 128:(kk + 1) * 128],
                                        stash[:, tt * 4096 + k * 128:tt * 4096 + (k + 1) * 128], ident[:])
                                    if kk == 15:
                                        t_tr = TE.inc(fn)
                                    else:
                                        TE.do(fn)
                                VE.wait(TE, t_tr)
                                dst = stash[:, tt * 4096 + rnd * 2048:tt * 4096 + (rnd + 1) * 2048]
                                t_bk = VE.inc(lambda e, dst=dst, pt=pt: e.tensor_copy(dst, pt[:]))
                                bk_hist[rcnt] = t_bk
                                rcnt += 1
                            t_q_done[tt] = t_bk
                            if grp == 0 and tt in (0, 4):
                                def _tapg(name, ap, shape):
                                    def go(t):
                                        GP.wait(VE, VE.n)
                                        GP.wait(SC, SC.n)
                                        GP.dma(lambda e, t=t, ap=ap: e.dma_start(out=t, in_=ap))
                                        for s in (VE, SC, TE):
                                            s.wait_dma_all(GP)
                                    dbg(name, go, shape)
                                _tapg("r2T%d" % tt, stash[:, tt * 4096:(tt + 1) * 4096], [128, 4096])
                        t_res_by_par = {}
                        for hf in range(2):
                            SY.wait(TE, max(last_dn_mm[0], t_phase1_mm))
                            wd_tok = []
                            for k in range(32):
                                wd_tok.append(SY.dma(lambda e, k=k, hf=hf: e.dma_start(
                                    out=wdf[:, k * 512:(k + 1) * 512],
                                    in_=scr["wdT"][k * 128:(k + 1) * 128, hf * 512:(hf + 1) * 512]), slot=hf))
                            TE.wait_dma(SY, wd_tok[-1])
                            for tt in range(GT):
                                tile = grp * GT + tt
                                par = tt % 2
                                TE.wait(VE, t_q_done[tt])
                                if par in t_res_by_par:
                                    TE.wait(VE, t_res_by_par[par])
                                for k in range(32):
                                    fn = lambda e, k=k, tt=tt, par=par: e.matmul(
                                        ps_d[par][:, 0:512], stash[:, tt * 4096 + k * 128:tt * 4096 + (k + 1) * 128],
                                        wdf[:, k * 512:(k + 1) * 512], start=(k == 0), stop=(k == 31))
                                    if k == 31:
                                        t_dm = TE.inc(fn)
                                    else:
                                        TE.do(fn)
                                last_dn_mm[0] = t_dm
                                VE.wait(TE, t_dm)
                                if not K_PIPE_DN:
                                    pass
                                dst = xr[:, tile * D + hf * 512:tile * D + (hf + 1) * 512]
                                if final:
                                    reg = tt % 2
                                    ob = outstage[:, reg * 512:reg * 512 + 512]
                                    VE.wait_dma(SY, t_outdma.get(reg, 0))
                                    t_res = VE.inc(lambda e, dst=dst, par=par, tt=tt, ob=ob: e.scalar_tensor_tensor(
                                        ob, ps_d[par][:, 0:512], down_sc(tt), dst, ALU.mult, ALU.add))
                                    SY.wait(VE, t_res)
                                    t_outdma[reg] = SY.dma(lambda e, tile=tile, hf=hf, ob=ob: e.dma_start(
                                        out=y_flat[tile * 128:(tile + 1) * 128, hf * 512:(hf + 1) * 512], in_=ob), slot=2 + reg)
                                else:
                                    t_res = VE.inc(lambda e, dst=dst, par=par, tt=tt: e.scalar_tensor_tensor(
                                        dst, ps_d[par][:, 0:512], down_sc(tt), dst, ALU.mult, ALU.add))
                                t_res_by_par[par] = t_res
                                if not K_PIPE_DN:
                                    TE.wait(VE, t_res)
                                resid_tok[tile] = t_res
                    barrier()

        resid_tok = [0] * NT
        VE.wait_dma(GP, t_x)
        SC.wait_dma(GP, t_x)
        VE.wait_dma(GP, t_ghost)
        TE.wait_dma(GP, t_bo)

        def _tap_sbuf(name, sb_ap, shape):
            def go(t):
                for s in (VE, SC, TE):
                    GP.wait(s, s.n)
                GP.wait_dma_all(GP)
                GP.dma(lambda e, t=t, sb_ap=sb_ap: e.dma_start(out=t, in_=sb_ap))
            dbg(name, go, shape)

        if n_sub >= 1:
            _tap_sbuf("xr0", xr[:, 0:1024], [128, 1024])
            emit_attn(False, resid_tok)
        if n_sub >= 2:
            barrier()
            emit_ffn(final=(n_sub == 2), resid_tok=resid_tok)
        if n_sub >= 3:
            barrier()
            emit_attn(True, resid_tok)
        if n_sub >= 4:
            barrier()
            emit_ffn(final=True, resid_tok=resid_tok)

        if n_sub in (1, 3):
            barrier()
            dumpb = sb("dumpb", [128, 1024], F32)
            for tile in range(NT):
                t_c = SC.inc(lambda e, tile=tile: e.activation(
                    dumpb[:], xr[:, tile * D:(tile + 1) * D], AF.Copy, bias=0.0, scale=1.0))
                SY.wait(SC, t_c)
                tok = SY.dma(lambda e, tile=tile: e.dma_start(out=y_flat[tile * 128:(tile + 1) * 128, :], in_=dumpb[:]))
                SC.wait_dma(SY, tok)

        SY.wait_dma_all(SY)
        GP.wait_dma_all(GP)

        finalize_barrier_ops()
        with nc.Block() as blk:
            blk.tensor(lambda e: TE.run(e))
            blk.vector(lambda e: VE.run(e))
            blk.scalar(lambda e: SC.run(e))
            blk.gpsimd(lambda e: GP.run(e))
            blk.sync(lambda e: SY.run(e))

    return nc


_NC_CACHE = {}


def kernel(L0_attn_a_x, L0_attn_a_wq, L0_attn_a_wk, L0_attn_a_wv, L0_attn_a_wo,
           L0_attn_b_Wqkv_w, L0_attn_b_Wqkv_b, L0_attn_b_Wo_w, L0_attn_b_Wo_b,
           L0_attn_b_ghost, L0_ffn_w_gate, L0_ffn_w_up, L0_ffn_w_down):
    n_sub = N_SUB
    key = (n_sub, tuple(DEBUG))
    if key not in _NC_CACHE:
        _NC_CACHE[key] = build(n_sub)
    nc = _NC_CACHE[key]
    x = np.ascontiguousarray(L0_attn_a_x)
    in_maps = []
    for c in range(8):
        in_maps.append({
            "x": x[2 * c:2 * c + 2],
            "wq_a": L0_attn_a_wq, "wk_a": L0_attn_a_wk, "wv_a": L0_attn_a_wv, "wo_a": L0_attn_a_wo,
            "wqkv_b": L0_attn_b_Wqkv_w, "bqkv_b": L0_attn_b_Wqkv_b,
            "wo_b": L0_attn_b_Wo_w, "bo_b": L0_attn_b_Wo_b, "ghost": L0_attn_b_ghost,
            "w_gate": L0_ffn_w_gate, "w_up": L0_ffn_w_up, "w_down": L0_ffn_w_down,
        })
    res = run_bass_kernel_spmd(nc, in_maps, core_ids=list(range(8)))
    global _dbg_out
    _dbg_out = {k: np.stack([res.results[c]["dbg_" + k] for c in range(8)]) for k in DEBUG}
    return np.concatenate([res.results[c]["y"] for c in range(8)], axis=0)


# version 08
# speedup vs baseline: 1.0945x; 1.0821x over previous
import contextlib
import numpy as np
import concourse.bass as bass
import concourse.mybir as mybir
from concourse.bass_utils import run_bass_kernel_spmd

F32 = mybir.dt.float32
BF16 = mybir.dt.bfloat16
AF = mybir.ActivationFunctionType
ALU = mybir.AluOpType
AX = mybir.AxisListType

MAGIC = float(2 ** 23 + 2 ** 22)
NEG = -1.0e30

D = 1024
S = 2048
B = 2
NT = 32
NTS = 16
NH = 16
HD = 64
NKV = 4
INNER = 4096
N_SUB = 4
K_DBUF = True
K_PIPE_GU = True
K_PIPE_DN = True
K_WD_KG = True
DEBUG = []
_dbg_out = {}


NSLOT = 4


class St:

    def __init__(self, name):
        self.name = name
        self.ops = []
        self.n = 0
        self.dn = [0] * NSLOT
        self.sem = None
        self.dsem = [None] * NSLOT

    def do(self, fn):
        self.ops.append(("o", fn))

    def inc(self, fn):
        self.n += 1
        self.ops.append(("i", fn))
        return self.n

    def wait(self, other, tok):
        if tok > 0:
            self.ops.append(("w", other, tok))

    def dma(self, fn, slot=0):
        self.dn[slot] += 16
        self.ops.append(("d", fn, slot))
        return (slot, self.dn[slot])

    def wait_dma(self, other, tok):
        if tok and tok[1] > 0:
            self.ops.append(("wd", other, tok))

    def wait_dma_all(self, other):
        for s in range(NSLOT):
            if other.dn[s]:
                self.ops.append(("wd", other, (s, other.dn[s])))

    def run(self, e):
        for op in self.ops:
            k = op[0]
            if k == "o":
                op[1](e)
            elif k == "i":
                op[1](e).then_inc(self.sem, 1)
            elif k == "d":
                op[1](e).then_inc(self.dsem[op[2]], 16)
            elif k == "w":
                e.wait_ge(op[1].sem, op[2])
            elif k == "wd":
                slot, v = op[2]
                e.wait_ge(op[1].dsem[slot], v)
            elif k == "b":
                e.sem_inc(op[1], op[2])
            elif k == "wb":
                e.wait_ge(op[1], op[2])


def c512(q0, qend):
    out = []
    c0 = q0
    while c0 < qend:
        nxt = min((c0 // 512 + 1) * 512, qend)
        out.append((c0, nxt - c0, c0 // 512))
        c0 = nxt
    return out


def build(n_sub=N_SUB, DEBUG=None):
    DEBUG = DEBUG if DEBUG is not None else globals()['DEBUG']
    K_DBUF = globals()["K_DBUF"]
    K_PIPE_GU = globals()["K_PIPE_GU"]
    K_PIPE_DN = globals()["K_PIPE_DN"]
    K_WD_KG = globals()["K_WD_KG"]
    nc = bass.Bass("TRN2", target_bir_lowering=False)

    x_in = nc.dram_tensor("x", [B, S, D], F32, kind="ExternalInput").ap()
    y_out = nc.dram_tensor("y", [B, S, D], F32, kind="ExternalOutput").ap()
    wq_a = nc.dram_tensor("wq_a", [D, D], F32, kind="ExternalInput").ap()
    wk_a = nc.dram_tensor("wk_a", [NKV * HD, D], F32, kind="ExternalInput").ap()
    wv_a = nc.dram_tensor("wv_a", [NKV * HD, D], F32, kind="ExternalInput").ap()
    wo_a = nc.dram_tensor("wo_a", [D, D], F32, kind="ExternalInput").ap()
    wqkv_b = nc.dram_tensor("wqkv_b", [3 * D, D], F32, kind="ExternalInput").ap()
    bqkv_b = nc.dram_tensor("bqkv_b", [3 * D], F32, kind="ExternalInput").ap()
    wo_b = nc.dram_tensor("wo_b", [D, D], F32, kind="ExternalInput").ap()
    bo_b = nc.dram_tensor("bo_b", [D], F32, kind="ExternalInput").ap()
    ghost = nc.dram_tensor("ghost", [NH], F32, kind="ExternalInput").ap()
    w_gate = nc.dram_tensor("w_gate", [INNER, D], F32, kind="ExternalInput").ap()
    w_up = nc.dram_tensor("w_up", [INNER, D], F32, kind="ExternalInput").ap()
    w_down = nc.dram_tensor("w_down", [D, INNER], F32, kind="ExternalInput").ap()

    scr = {}
    scr["wqT_a"] = nc.dram_tensor("scr_wqT_a", [D, D], BF16, kind="Internal").ap()
    scr["wkT_a"] = nc.dram_tensor("scr_wkT_a", [D, NKV * HD], BF16, kind="Internal").ap()
    scr["wvT_a"] = nc.dram_tensor("scr_wvT_a", [D, NKV * HD], BF16, kind="Internal").ap()
    scr["woT_a"] = nc.dram_tensor("scr_woT_a", [D, D], BF16, kind="Internal").ap()
    scr["qkvT_b"] = nc.dram_tensor("scr_qkvT_b", [D, 3 * D], BF16, kind="Internal").ap()
    scr["woT_b"] = nc.dram_tensor("scr_woT_b", [D, D], BF16, kind="Internal").ap()
    scr["wgT"] = nc.dram_tensor("scr_wgT", [D, INNER], BF16, kind="Internal").ap()
    scr["wuT"] = nc.dram_tensor("scr_wuT", [D, INNER], BF16, kind="Internal").ap()
    scr["wdT"] = nc.dram_tensor("scr_wdT", [INNER, D], BF16, kind="Internal").ap()

    x_flat = x_in.rearrange("b s d -> (b s) d")
    y_flat = y_out.rearrange("b s d -> (b s) d")

    dbg_tensors = {}

    def dbg(name, ap_fn, shape, dt=F32):
        if name not in DEBUG:
            return
        t = nc.dram_tensor("dbg_" + name, shape, dt, kind="ExternalOutput").ap()
        dbg_tensors[name] = t
        ap_fn(t)

    TE, VE, SC, GP, SY = (St(n) for n in ("te", "ve", "sc", "gp", "sy"))
    streams = [TE, VE, SC, GP, SY]
    barn = [0]

    def barrier():
        for s in streams:
            s.wait_dma_all(s)
            s.ops.append(("b", None, 1))
        barn[0] += len(streams)
        for s in streams:
            s.ops.append(("wb", None, barn[0]))

    uid = [0]
    with contextlib.ExitStack() as ctx:
        def sb(name, shape, dt=BF16, st=ctx):
            uid[0] += 1
            return st.enter_context(nc.sbuf_tensor("%s_%d" % (name, uid[0]), shape, dt))

        def psum(name, shape, dt=F32, st=ctx):
            uid[0] += 1
            return st.enter_context(nc.psum_tensor("%s_%d" % (name, uid[0]), shape, dt))

        for s in streams:
            s.sem = ctx.enter_context(nc.semaphore(s.name + "_c"))
            for sl in range(NSLOT):
                s.dsem[sl] = ctx.enter_context(nc.semaphore("%s_d%d" % (s.name, sl)))
        barsem = ctx.enter_context(nc.semaphore("bar"))
        for s in streams:
            pass

        def finalize_barrier_ops():
            for s in streams:
                s.ops = [(op[0], barsem, op[2]) if op[0] in ("b", "wb") else op for op in s.ops]

        xr = sb("xr", [128, NT * D], BF16)
        ident = sb("ident", [128, 128], BF16)
        identf = sb("identf", [128, 128], F32)
        maskS = sb("maskS", [128, 128], F32)
        maskT = sb("maskT", [128, 128], F32)
        e16 = sb("e16", [16, 8 * 128], BF16)
        ones1 = sb("ones1", [1, 128], BF16)
        ones1f = sb("ones1f", [1, 128], F32)
        onescol = sb("onescol", [128, 1], F32)
        ghost_sb = sb("ghost_sb", [16, 1], F32)
        eps_t = sb("eps_t", [128, 1], F32)
        magic_t = sb("magic_t", [128, 1], F32)
        bqkT = sb("bqkT", [128, 24], F32)
        bv_row = sb("bv_row", [1, D], BF16)
        bo_row = sb("bo_row", [1, D], BF16)
        meanw = sb("meanw", [128, 3], F32)
        lnstat = sb("lnstat", [128, 128], F32)
        qtmp = sb("qtmp", [128, 8], F32)
        st8 = sb("st8", [128, 8], F32)
        trash1k = sb("trash1k", [128, 1024], BF16)
        xln_tok = sb("xln_tok", [128, 2 * 1024], BF16)
        bufs = {}
        psLN = psum("psLN", [128, 1024], BF16)
        ps_small = psum("ps_small", [128, 128], F32)

        GP.do(lambda e: e.memset(ident[:], 0.0))
        GP.do(lambda e: e.affine_select(out=ident[:], in_=ident[:], compare_op=ALU.not_equal,
                                        fill=1.0, base=0, pattern=[[-1, 128]], channel_multiplier=1))
        GP.do(lambda e: e.memset(identf[:], 0.0))
        GP.do(lambda e: e.affine_select(out=identf[:], in_=identf[:], compare_op=ALU.not_equal,
                                        fill=1.0, base=0, pattern=[[-1, 128]], channel_multiplier=1))
        GP.do(lambda e: e.memset(maskS[:], 0.0))
        GP.do(lambda e: e.affine_select(out=maskS[:], in_=maskS[:], compare_op=ALU.is_ge,
                                        fill=NEG, base=0, pattern=[[-1, 128]], channel_multiplier=1))
        GP.do(lambda e: e.memset(maskT[:], 0.0))
        GP.do(lambda e: e.affine_select(out=maskT[:], in_=maskT[:], compare_op=ALU.is_ge,
                                        fill=NEG, base=0, pattern=[[1, 128]], channel_multiplier=-1))
        GP.do(lambda e: e.memset(e16[:], 0.0))
        for c in range(8):
            for half in range(2):
                cols = slice(c * 128 + half * 64, c * 128 + half * 64 + 64)
                GP.do(lambda e, cols=cols, c=c, half=half: e.affine_select(
                    out=e16[:, cols], in_=e16[:, cols], compare_op=ALU.not_equal,
                    fill=1.0, base=-(2 * c + half), pattern=[[0, 64]], channel_multiplier=1))
        GP.do(lambda e: e.memset(ones1[:], 1.0))
        GP.do(lambda e: e.memset(ones1f[:], 1.0))
        GP.do(lambda e: e.memset(onescol[:], 1.0))
        GP.do(lambda e: e.memset(eps_t[:], 1e-5))
        t_gp_const = GP.inc(lambda e: e.memset(magic_t[:], MAGIC))
        t_ghost = GP.dma(lambda e: e.dma_start(out=ghost_sb[:], in_=ghost.rearrange("(h o) -> h o", o=1)), slot=2)
        t_bv = GP.dma(lambda e: e.dma_start(out=bv_row[:], in_=bqkv_b.rearrange("(o d) -> o d", o=1)[0:1, 2048:3072]), slot=2)
        t_bo = GP.dma(lambda e: e.dma_start(out=bo_row[:], in_=bo_b.rearrange("(o d) -> o d", o=1)), slot=2)
        t_x = 0
        for piece in range(4):
            dst = xr[:].rearrange("p (t d) -> p t d", d=D)[:, piece * 8:(piece + 1) * 8, :]
            src = x_flat[piece * 1024:(piece + 1) * 1024, :].rearrange("(t p) d -> p t d", p=128)
            t_x = GP.dma(lambda e, dst=dst, src=src: e.dma_start(out=dst, in_=src), slot=1)

        with contextlib.ExitStack() as pctx:
            wbuf = sb("wbuf", [128, 2 * 1024], BF16, pctx)
            wfl = sb("wfl", [128, 8192], F32, pctx)
            wql = sb("wql", [128, 2 * 1024], BF16, pctx)
            wtb = sb("wtb", [128, 8 * 1024], BF16, pctx)
            absacc = sb("absacc", [128, 32], F32, pctx)
            b24 = sb("b24", [128, 128], F32, pctx)
            psT0 = [psum("psT0a", [128, 1024], BF16, pctx), psum("psT0b", [128, 1024], BF16, pctx)]

            GP.do(lambda e: e.memset(b24[:], 0.0))
            t_b24 = GP.dma(lambda e: e.dma_start(out=b24[0:24, :], in_=bqkv_b.rearrange("(a b) -> a b", b=128)), slot=2)
            TE.wait_dma(GP, t_b24)
            TE.wait(GP, t_gp_const)
            t_te_b = TE.inc(lambda e: e.transpose(ps_small[:], b24[:], identf[:]))
            VE.wait(TE, t_te_b)
            t_bqk = VE.inc(lambda e: e.tensor_copy(bqkT[:], ps_small[:, 0:24]))
            TE.wait(VE, t_bqk)

            state = {"last_store": 0, "last_evac": 0, "g": 0,
                     "T": {}, "ev": {}, "tc": {}}

            def prep_cast_w2(w_ap, dout, scr_ap):
                ntile = dout // 128
                for g0 in range(0, ntile, 8):
                    gn = min(8, ntile - g0)
                    for mm in range(gn):
                        m = g0 + mm
                        g = state["g"]
                        state["g"] = g + 1
                        p = g % 2
                        GP.wait(TE, state["T"].get(g - 2, 0))
                        tok_l = GP.dma(lambda e, m=m, p=p, w_ap=w_ap: e.dma_start(
                            out=wbuf[:, p * 1024:(p + 1) * 1024], in_=w_ap[m * 128:(m + 1) * 128, :]),
                            slot=3 if p else 0)
                        TE.wait_dma(GP, tok_l)
                        VE.wait_dma(SY, state["last_store"])
                        TE.wait(VE, state["ev"].get(g - 2, 0))
                        for k in range(8):
                            fn = lambda e, k=k, p=p: e.transpose(psT0[p][:, k * 128:(k + 1) * 128],
                                                                 wbuf[:, p * 1024 + k * 128:p * 1024 + (k + 1) * 128], ident[:])
                            if k == 7:
                                state["T"][g] = TE.inc(fn)
                            else:
                                TE.do(fn)
                        VE.wait(TE, state["T"][g])
                        dst = wtb[:].rearrange("p (k d) -> p k d", k=8)[:, :, mm * 128:(mm + 1) * 128]
                        src2 = psT0[p][:].rearrange("p (k c) -> p k c", k=8)
                        state["ev"][g] = VE.inc(lambda e, dst=dst, src2=src2: e.tensor_copy(dst, src2))
                        state["last_evac"] = state["ev"][g]
                    for k in range(8):
                        SY.wait(VE, state["last_evac"])
                        tok_s = SY.dma(lambda e, k=k, scr_ap=scr_ap, g0=g0, gn=gn: e.dma_start(
                            out=scr_ap[k * 128:(k + 1) * 128, g0 * 128:g0 * 128 + gn * 128],
                            in_=wtb[:, k * 1024:k * 1024 + gn * 128]), slot=3)
                    state["last_store"] = tok_s
                return tok_s

            TE.wait(GP, t_gp_const)
            prep_cast_w2(wq_a, D, scr["wqT_a"])
            prep_cast_w2(wk_a, NKV * HD, scr["wkT_a"])
            prep_cast_w2(wv_a, NKV * HD, scr["wvT_a"])
            prep_cast_w2(wo_a, D, scr["woT_a"])
            prep_cast_w2(wqkv_b, 3 * D, scr["qkvT_b"])
            prep_cast_w2(wo_b, D, scr["woT_b"])

            def ffn_mean_pass(w_ap, ntile, ncol, meanw_col):
                tok_r = 0
                r_hist = {}
                SY.wait(VE, state["tc"].get(state["g"] - 1, 0))
                SY.wait(VE, state["tc"].get(state["g"] - 2, 0))
                for m in range(ntile):
                    off = (m % 2) * 4096
                    if m - 2 in r_hist:
                        SY.wait(VE, r_hist[m - 2])
                    tok_l = SY.dma(lambda e, m=m, w_ap=w_ap, ncol=ncol, off=off: e.dma_start(
                        out=wfl[:, off:off + ncol], in_=w_ap[m * 128:(m + 1) * 128, :]), slot=m % 2)
                    VE.wait_dma(SY, tok_l)
                    tok_r = VE.inc(lambda e, m=m, ncol=ncol, off=off: e.reduce_sum(
                        absacc[:, m:m + 1], wfl[:, off:off + ncol], axis=AX.X, apply_absolute_value=True))
                    r_hist[m] = tok_r
                SY.wait(VE, tok_r)
                VE.wait(VE, tok_r)
                t1 = VE.inc(lambda e, ntile=ntile: e.reduce_sum(absacc[:, 0:1], absacc[:, 0:ntile], axis=AX.X))
                TE.wait(VE, t1)
                t2 = TE.inc(lambda e: e.matmul(ps_small[0:1, 0:1], absacc[:, 0:1], onescol[:], start=True, stop=True))
                VE.wait(TE, t2)
                t3 = VE.inc(lambda e, n=float(ntile * 128 * ncol): e.tensor_scalar(
                    st8[0:1, 0:1], ps_small[0:1, 0:1], 1.0 / n, 1e-5, ALU.mult, ALU.max))
                TE.wait(VE, t3)
                t4 = TE.inc(lambda e: e.matmul(ps_small[:, 0:1], ones1f[0:1, :], st8[0:1, 0:1], start=True, stop=True))
                VE.wait(TE, t4)
                t5 = VE.inc(lambda e, c=meanw_col: e.tensor_copy(meanw[:, c:c + 1], ps_small[:, 0:1]))
                VE.wait(VE, t5)
                t6 = VE.inc(lambda e, c=meanw_col: e.reciprocal(st8[:, 1:2], meanw[:, c:c + 1]))
                TE.wait(VE, t6)
                SC.wait(VE, t6)
                return t6

            def prep_tern(src_ap_fn, n_m, store_fn):
                for m in range(n_m):
                    g = state["g"]
                    state["g"] = g + 1
                    p = g % 2
                    off = p * 4096
                    SY.wait(VE, state["tc"].get(g - 2, 0))
                    tok_l = SY.dma(lambda e, m=m, off=off: e.dma_start(
                        out=wfl[:, off:off + 1024], in_=src_ap_fn(m)), slot=p)
                    SC.wait_dma(SY, tok_l)
                    ta = SC.inc(lambda e, off=off: e.activation(wfl[:, off:off + 1024], wfl[:, off:off + 1024], AF.Identity, bias=magic_t[:], scale=st8[:, 1:2]))
                    SC.wait(SC, ta)
                    tb = SC.inc(lambda e, off=off: e.activation(wfl[:, off:off + 1024], wfl[:, off:off + 1024], AF.Copy, bias=-MAGIC, scale=1.0))
                    VE.wait(SC, tb)
                    VE.wait(TE, state["T"].get(g - 2, 0))
                    tc = VE.inc(lambda e, p=p, off=off: e.tensor_scalar(
                        wql[:, p * 1024:(p + 1) * 1024], wfl[:, off:off + 1024], -1.0, 1.0, ALU.max, ALU.min))
                    state["tc"][g] = tc
                    VE.wait_dma(SY, state["last_store"])
                    TE.wait(VE, tc)
                    TE.wait(VE, state["ev"].get(g - 2, 0))
                    for k in range(8):
                        fn = lambda e, k=k, p=p: e.transpose(psT0[p][:, k * 128:(k + 1) * 128],
                                                             wql[:, p * 1024 + k * 128:p * 1024 + (k + 1) * 128], ident[:])
                        if k == 7:
                            state["T"][g] = TE.inc(fn)
                        else:
                            TE.do(fn)
                    VE.wait(TE, state["T"][g])
                    dst = wtb[:].rearrange("p (k d) -> p k d", k=8)[:, :, (m % 8) * 128:(m % 8) * 128 + 128]
                    src2 = psT0[p][:].rearrange("p (k c) -> p k c", k=8)
                    state["ev"][g] = VE.inc(lambda e, dst=dst, src2=src2: e.tensor_copy(dst, src2))
                    state["last_evac"] = state["ev"][g]
                    if m % 8 == 7:
                        store_fn(m // 8)

            def prep_ffn_gu(w_ap, scr_ap, meanw_col):
                ffn_mean_pass(w_ap, 32, 1024, meanw_col)

                def store(grp):
                    tok_s = 0
                    for k in range(8):
                        SY.wait(VE, state["last_evac"])
                        tok_s = SY.dma(lambda e, k=k, grp=grp, scr_ap=scr_ap: e.dma_start(
                            out=scr_ap[k * 128:(k + 1) * 128, grp * 1024:(grp + 1) * 1024],
                            in_=wtb[:, k * 1024:(k + 1) * 1024]), slot=3)
                    state["last_store"] = tok_s
                prep_tern(lambda m, w_ap=w_ap: w_ap[m * 128:(m + 1) * 128, :], 32, store)

            def prep_ffn_wd():
                ffn_mean_pass(w_down, 8, 4096, 2)

                def store(grp):
                    tok_s = 0
                    for k in range(8):
                        SY.wait(VE, state["last_evac"])
                        tok_s = SY.dma(lambda e, k=k, kg=grp: e.dma_start(
                            out=scr["wdT"][(kg * 8 + k) * 128:(kg * 8 + k + 1) * 128, :],
                            in_=wtb[:, k * 1024:(k + 1) * 1024]), slot=3)
                    state["last_store"] = tok_s

                def src_fn(m):
                    kg, mm = m // 8, m % 8
                    return w_down[mm * 128:(mm + 1) * 128, kg * 1024:(kg + 1) * 1024]
                prep_tern(src_fn, 32, store)

            prep_ffn_gu(w_gate, scr["wgT"], 0)
            prep_ffn_gu(w_up, scr["wuT"], 1)
            prep_ffn_wd()
            barrier()

        ln_state = {"last_sc": 0}
        tr_state = {}

        def ln_stats_batch(tiles, gate_toks):
            nt = len(tiles)
            VE.wait(SC, ln_state["last_sc"])
            for i, tile in enumerate(tiles):
                gtok = gate_toks[i]
                SC.wait(VE, gtok)
                VE.wait(VE, gtok)
                srcx = xr[:, tile * D:(tile + 1) * D]
                t_sq = SC.inc(lambda e, srcx=srcx, i=i: e.activation(
                    trash1k[:], srcx, AF.Square, bias=0.0, scale=1.0, accum_out=lnstat[:, 16 + i:17 + i]))
                VE.inc(lambda e, srcx=srcx, i=i: e.reduce_sum(lnstat[:, i:i + 1], srcx, axis=AX.X))
            VE.wait(SC, t_sq)
            VE.wait(VE, VE.n)
            sx, ssq = lnstat[:, 0:nt], lnstat[:, 16:16 + nt]
            mean, rstd, nbias = lnstat[:, 32:32 + nt], lnstat[:, 48:48 + nt], lnstat[:, 64:64 + nt]
            t1 = VE.inc(lambda e, sx=sx, mean=mean: e.tensor_scalar_mul(mean, sx, 1.0 / D))
            VE.wait(VE, t1)
            tmp = lnstat[:, 96:96 + nt]
            t2 = VE.inc(lambda e, mean=mean, tmp=tmp: e.tensor_mul(tmp, mean, mean))
            VE.wait(VE, t2)
            t3 = VE.inc(lambda e, ssq=ssq, tmp=tmp, rstd=rstd: e.scalar_tensor_tensor(rstd, ssq, 1.0 / D, tmp, ALU.mult, ALU.subtract))
            SC.wait(VE, t3)
            t4 = SC.inc(lambda e, rstd=rstd: e.activation(rstd, rstd, AF.Sqrt, bias=eps_t[:], scale=1.0))
            VE.wait(SC, t4)
            t5 = VE.inc(lambda e, rstd=rstd: e.reciprocal(rstd, rstd))
            VE.wait(VE, t5)
            t6 = VE.inc(lambda e, mean=mean, rstd=rstd, nbias=nbias: e.scalar_tensor_tensor(nbias, mean, -1.0, rstd, ALU.mult, ALU.mult))
            SC.wait(VE, t6)
            return t6

        def ln_norm_tile(i, tile, dst_fm, par, quant):
            srcx = xr[:, tile * D:(tile + 1) * D]
            rstd = lnstat[:, 48 + i:49 + i]
            nbias = lnstat[:, 64 + i:65 + i]
            xtok = xln_tok[:, (par % 2) * 1024:(par % 2) * 1024 + 1024]
            SC.wait(TE, tr_state.get(par % 2, 0))
            if not quant:
                t_x = SC.inc(lambda e, srcx=srcx, xtok=xtok, rstd=rstd, nbias=nbias: e.activation(
                    xtok, srcx, AF.Identity, bias=nbias, scale=rstd))
            else:
                sxq = lnstat[:, 96 + i:97 + i]
                xf = bufs["xf"][:, par * 1024:(par + 1) * 1024]
                tg = SC.inc(lambda e, xf=xf, sxq=sxq: e.activation(xf, xf, AF.Identity, bias=magic_t[:], scale=sxq))
                SC.wait(SC, tg)
                t_x = SC.inc(lambda e, xf=xf, xtok=xtok: e.activation(xtok, xf, AF.Copy, bias=-MAGIC, scale=1.0))
            ln_state["last_sc"] = t_x
            TE.wait(SC, t_x)
            for k in range(8):
                fn = lambda e, k=k, xtok=xtok: e.transpose(psLN[:, k * 128:(k + 1) * 128], xtok[:, k * 128:(k + 1) * 128], ident[:])
                if k == 7:
                    tr_state[par % 2] = TE.inc(fn)
                else:
                    TE.do(fn)
            VE.wait(TE, TE.n)
            src_ps = psLN[:].rearrange("p (k c) -> p k c", k=8)
            t_e = VE.inc(lambda e, dst=dst_fm, src=src_ps: e.tensor_copy(dst_fm, src))
            TE.wait(VE, t_e)
            return t_e

        def ln_quant_pre(i, tile, par):
            srcx = xr[:, tile * D:(tile + 1) * D]
            rstd = lnstat[:, 48 + i:49 + i]
            nbias = lnstat[:, 64 + i:65 + i]
            amax = lnstat[:, 80 + i:81 + i]
            xf = bufs["xf"][:, par * 1024:(par + 1) * 1024]
            t_f = SC.inc(lambda e, srcx=srcx, xf=xf, rstd=rstd, nbias=nbias: e.activation(
                xf, srcx, AF.Identity, bias=nbias, scale=rstd))
            ln_state["last_sc"] = t_f
            VE.wait(SC, t_f)
            t_a = VE.inc(lambda e, xf=xf, amax=amax: e.tensor_reduce(amax, xf, AX.X, ALU.max, apply_absolute_value=True))
            return t_a

        def ln_quant_scales(i0, nt):
            amax = lnstat[:, 80 + i0:80 + i0 + nt]
            sxq = lnstat[:, 96 + i0:96 + i0 + nt]
            qsc = lnstat[:, 112 + i0:112 + i0 + nt]
            VE.wait(VE, VE.n)
            t1 = VE.inc(lambda e, amax=amax, qsc=qsc: e.tensor_scalar(qsc, amax, 1.0 / 127.0, 1e-5 / 127.0, ALU.mult, ALU.max))
            VE.wait(VE, t1)
            t2 = VE.inc(lambda e, qsc=qsc, sxq=sxq: e.reciprocal(sxq, qsc))
            SC.wait(VE, t2)
            return t2

        def emit_attn(is_b, resid_tok):
            nkv = NH if is_b else NKV
            vheads = 8 if is_b else 4
            with contextlib.ExitStack() as actx:
                vbuf = sb("vbuf", [128, 16 * vheads * 65], BF16, actx)
                ctxb = sb("ctxb", [128, 8 * 2048], BF16, actx)
                kbuf = sb("kbuf", [64, 2048], BF16, actx)
                qbuf = sb("qbuf", [64, 2048], BF16, actx)
                ptb = sb("ptb", [128, 2 * 2048], BF16, actx)
                lrow = sb("lrow", [65, 2048], BF16, actx)
                lmatT = sb("lmatT", [16, 2048], F32, actx)
                ctxn = sb("ctxn", [128, 1024], BF16, actx)
                recipb = sb("recipb", [16, 2048], BF16, actx)
                wslice = sb("wslice", [128, 2 * 1024], BF16, actx)
                if is_b:
                    mabuf = sb("mabuf", [128, 16 * 16], F32, actx)
                    mTb = lrow[0:16, :]
                ps_st = [psum("ps_st0", [128, 512], F32, actx), psum("ps_st1", [128, 512], F32, actx)]
                ps_ctx = psum("ps_ctx", [128, 2048], F32, actx)

                for seq in range(B):
                    base = seq * NTS
                    qoff = 0
                    with contextlib.ExitStack() as bctx:
                        xlnT = sb("xlnT", [128, 8 * 2048], BF16, bctx)
                        wv_sb = sb("wv_sb", [128, 8 * vheads * 64], BF16, bctx)
                        if not is_b:
                            wk_sb = sb("wk_sb", [128, 8 * 256], BF16, bctx)
                        xlnT_r = xlnT[:].rearrange("p (k t) -> p k t", k=8)

                        ln_stats_batch([base + tt for tt in range(NTS)], [resid_tok[base + tt] for tt in range(NTS)])
                        for tt in range(NTS):
                            tile = base + tt
                            ln_norm_tile(tt, tile, xlnT_r[:, :, tt * 128:(tt + 1) * 128], tt % 2, quant=False)
                        t_ln_all = VE.n
                        if seq == 0 and not is_b:
                            def _tap(name, ap, shape):
                                def go(t):
                                    for s in (VE, SC, TE):
                                        GP.wait(s, s.n)
                                    GP.dma(lambda e, t=t, ap=ap: e.dma_start(out=t, in_=ap))
                                dbg(name, go, shape)
                            _tap("xlnT", xlnT[:, 0:2048], [128, 2048])

                        if not is_b:
                            t_wk = 0
                            for k in range(8):
                                t_wk = SY.dma(lambda e, k=k: e.dma_start(out=wk_sb[:, k * 256:(k + 1) * 256],
                                                                  in_=scr["wkT_a"][k * 128:(k + 1) * 128, :]), slot=2)

                        prev_proj_mm = [0, 0]
                        wsl_tok = [0] * (NH + 1)

                        def prefetch_slices(h):
                            if h >= NH:
                                return
                            wb = (h % 2) * 1024
                            SY.wait(TE, prev_proj_mm[h % 2])
                            if is_b:
                                SY.dma(lambda e, h=h, wb=wb: e.dma_start(
                                    out=wslice[:, wb:wb + 512].rearrange("p (k c) -> p k c", k=8),
                                    in_=scr["qkvT_b"][:, h * 64:(h + 1) * 64].rearrange("(k p) c -> p k c", p=128)), slot=h % 2)
                                wsl_tok[h] = SY.dma(lambda e, h=h, wb=wb: e.dma_start(
                                    out=wslice[:, wb + 512:wb + 1024].rearrange("p (k c) -> p k c", k=8),
                                    in_=scr["qkvT_b"][:, 1024 + h * 64:1024 + (h + 1) * 64].rearrange("(k p) c -> p k c", p=128)), slot=h % 2)
                            else:
                                wsl_tok[h] = SY.dma(lambda e, h=h, wb=wb: e.dma_start(
                                    out=wslice[:, wb:wb + 512].rearrange("p (k c) -> p k c", k=8),
                                    in_=scr["wqT_a"][:, h * 64:(h + 1) * 64].rearrange("(k p) c -> p k c", p=128)), slot=h % 2)

                        prefetch_slices(0)
                        t_kq = 0
                        t_av_hist = {}
                        t_ctx_ev = 0
                        t_lm = [0] * NH
                        for h in range(NH):
                            g = h if is_b else h // 4
                            hh = g % vheads
                            if (is_b and h % 8 == 0) or (not is_b and h == 0):
                                v0 = (h // 8) * 8 if is_b else 0
                                ndv = vheads * 64
                                GP.wait(TE, TE.n)
                                t_vb = GP.inc(lambda e: e.memset(vbuf[:], 1.0))
                                t_wv = 0
                                if is_b:
                                    vsrc = scr["qkvT_b"]
                                    col0 = 2048 + v0 * 64
                                else:
                                    vsrc = scr["wvT_a"]
                                    col0 = 0
                                for k in range(8):
                                    t_wv = SY.dma(lambda e, k=k, vsrc=vsrc, col0=col0, ndv=ndv: e.dma_start(
                                        out=wv_sb[:, k * ndv:(k + 1) * ndv],
                                        in_=vsrc[k * 128:(k + 1) * 128, col0:col0 + ndv]), slot=2)
                                TE.wait(GP, t_vb)
                                TE.wait_dma(SY, t_wv)
                                TE.wait(VE, t_ln_all)
                                vbuf_r = vbuf[:].rearrange("p (t h c) -> p t h c", t=16, h=vheads)
                                for tt in range(NTS):
                                    for (c0, cw, bank) in c512(0, ndv):
                                        for k in range(8):
                                            last = (k == 7) and (not is_b)
                                            fn = lambda e, k=k, tt=tt, c0=c0, cw=cw, last=last: e.matmul(
                                                ps_ctx[:, c0:c0 + cw],
                                                xlnT_r[:, k, tt * 128:(tt + 1) * 128],
                                                wv_sb[:, k * ndv + c0:k * ndv + c0 + cw],
                                                start=(k == 0), stop=last)
                                            if last:
                                                t_mmv = TE.inc(fn)
                                            else:
                                                TE.do(fn)
                                        if is_b:
                                            t_mmv = TE.inc(lambda e, c0=c0, cw=cw, col0=col0: e.matmul(
                                                ps_ctx[:, c0:c0 + cw], ones1[:],
                                                bv_row[0:1, col0 - 2048 + c0:col0 - 2048 + c0 + cw],
                                                start=False, stop=True))
                                        VE.wait(TE, t_mmv)
                                        VE.wait(GP, t_vb)
                                        nh0 = c0 // 64
                                        nh1 = (c0 + cw) // 64
                                        dst = vbuf_r[:, tt, nh0:nh1, 0:64]
                                        t_vev = VE.inc(lambda e, dst=dst, c0=c0, cw=cw: e.tensor_copy(
                                            dst, ps_ctx[:, c0:c0 + cw].rearrange("p (h c) -> p h c", c=64)))
                                        TE.wait(VE, t_vev)

                            def proj64(buf, wof, bias):
                                t_ev = 0
                                ev_hist = {}
                                for pi, (c0, cw, bank) in enumerate(c512(0, 2048)):
                                    pq = ps_st[pi % 2]
                                    if pi - 2 in ev_hist:
                                        TE.wait(SC, ev_hist[pi - 2])
                                    for k in range(8):
                                        fn = lambda e, k=k, c0=c0, cw=cw, buf=buf, wof=wof, pq=pq: e.matmul(
                                            pq[0:64, 0:cw], wof(k), xlnT_r[:, k, c0:c0 + cw],
                                            start=(k == 0), stop=(k == 7))
                                        if k == 7:
                                            t_mm = TE.inc(fn)
                                        else:
                                            TE.do(fn)
                                    SC.wait(TE, t_mm)
                                    if bias is None:
                                        t_ev = SC.inc(lambda e, buf=buf, c0=c0, cw=cw, pq=pq: e.activation(
                                            buf[:, c0:c0 + cw], pq[0:64, 0:cw], AF.Copy, bias=0.0, scale=1.0))
                                    else:
                                        t_ev = SC.inc(lambda e, buf=buf, c0=c0, cw=cw, bias=bias, pq=pq: e.activation(
                                            buf[:, c0:c0 + cw], pq[0:64, 0:cw], AF.Identity, bias=bias, scale=1.0))
                                    ev_hist[pi] = t_ev
                                TE.wait(SC, t_ev)
                                return t_ev, t_mm

                            wb = (h % 2) * 1024
                            TE.wait_dma(SY, wsl_tok[h])
                            if is_b:
                                wq_k = lambda k, wb=wb: wslice[:, wb + k * 64:wb + (k + 1) * 64]
                                wk_k = lambda k, wb=wb: wslice[:, wb + 512 + k * 64:wb + 512 + (k + 1) * 64]
                                bq = bqkT[(h % 2) * 64:(h % 2) * 64 + 64, h // 2:h // 2 + 1]
                                bk = bqkT[(h % 2) * 64:(h % 2) * 64 + 64, 8 + h // 2:8 + h // 2 + 1]
                            else:
                                TE.wait_dma(SY, t_wk)
                                wq_k = lambda k, wb=wb: wslice[:, wb + k * 64:wb + (k + 1) * 64]
                                wk_k = lambda k, g=g: wk_sb[:, k * 256 + g * 64:k * 256 + (g + 1) * 64]
                                bq = bk = None

                            if is_b or h % 4 == 0:
                                proj64(kbuf, wk_k, bk)
                            t_kq, t_proj_mm = proj64(qbuf, wq_k, bq)
                            prev_proj_mm[h % 2] = t_proj_mm
                            TE.wait(SC, t_kq)
                            prefetch_slices(h + 1)
                            if seq == 0 and h in (0, 14, 15) and not is_b:
                                def _tap2(name, ap, shape):
                                    def go(t):
                                        GP.wait(SC, SC.n)
                                        GP.wait(VE, VE.n)
                                        GP.dma(lambda e, t=t, ap=ap: e.dma_start(out=t, in_=ap))
                                    dbg(name, go, shape)
                                _tap2("kbuf%d" % h, kbuf[:], [64, 2048])
                                _tap2("qbuf%d" % h, qbuf[:], [64, 2048])

                            cnt = 0
                            exp_hist = {}
                            avq = []
                            last_av_of = {}

                            def flush_one():
                                fn_av, tok_e, jj = avq.pop(0)
                                TE.wait(SC, tok_e)
                                last_av_of[jj] = TE.inc(fn_av)

                            for j in range(NTS):
                                q0 = j * 128
                                par = j % 2
                                while avq and avq[0][2] <= j - 2:
                                    flush_one()
                                if j - 2 in last_av_of:
                                    SC.wait(TE, last_av_of[j - 2])
                                chunks = c512(q0, 2048)
                                for ci, (c0, cw, bank) in enumerate(chunks):
                                    pst = ps_st[cnt % 2]
                                    if cnt - 2 in exp_hist:
                                        TE.wait(SC, exp_hist[cnt - 2])
                                    fn = lambda e, j=j, c0=c0, cw=cw, pst=pst: e.matmul(
                                        pst[:, 0:cw], kbuf[:, j * 128:(j + 1) * 128], qbuf[:, c0:c0 + cw],
                                        start=True, stop=True)
                                    t_mm = TE.inc(fn)
                                    if ci == 0:
                                        VE.wait(TE, t_mm)
                                        t_mk = VE.inc(lambda e, pst=pst: e.tensor_tensor(pst[:, 0:128], pst[:, 0:128], maskT[:], ALU.add))
                                        SC.wait(VE, t_mk)
                                    SC.wait(TE, t_mm)
                                    t_exp = SC.inc(lambda e, pst=pst, par=par, c0=c0, cw=cw: e.activation(
                                        ptb[:, par * 2048 + c0:par * 2048 + c0 + cw], pst[:, 0:cw],
                                        AF.Exp, bias=0.0, scale=0.125))
                                    exp_hist[cnt] = t_exp
                                    vsl = vbuf[:, j * (vheads * 65) + hh * 65:j * (vheads * 65) + (hh + 1) * 65]
                                    fn_av = lambda e, j=j, vsl=vsl, par=par, c0=c0, cw=cw, bank=bank: e.matmul(
                                        ps_ctx[0:65, c0:c0 + cw], vsl,
                                        ptb[:, par * 2048 + c0:par * 2048 + c0 + cw],
                                        start=(j == 0), stop=(j == min(NTS - 1, 4 * bank + 3)))
                                    avq.append((fn_av, t_exp, j))
                                    while len(avq) > 2:
                                        flush_one()
                                    cnt += 1
                            while avq:
                                flush_one()
                            t_av = last_av_of[NTS - 1]

                            if is_b:
                                scnt = 0
                                mx_hist = {}
                                for i in range(NTS):
                                    L = (i + 1) * 128
                                    chunks = c512(0, L)
                                    mparts = st8[:, 0:4]
                                    for ci, (c0, cw, bank) in enumerate(chunks):
                                        pss = ps_st[scnt % 2]
                                        if scnt - 2 in mx_hist:
                                            TE.wait(VE, mx_hist[scnt - 2])
                                        fn = lambda e, i=i, c0=c0, cw=cw, pss=pss: e.matmul(
                                            pss[:, 0:cw], qbuf[:, i * 128:(i + 1) * 128], kbuf[:, c0:c0 + cw],
                                            start=True, stop=True)
                                        t_mm = TE.inc(fn)
                                        VE.wait(TE, t_mm)
                                        if ci == len(chunks) - 1:
                                            off = L - 128 - c0
                                            t_mk = VE.inc(lambda e, pss=pss, off=off: e.tensor_tensor(
                                                pss[:, off:off + 128], pss[:, off:off + 128], maskS[:], ALU.add))
                                            VE.wait(VE, t_mk)
                                        t_mx = VE.inc(lambda e, pss=pss, ci=ci, cw=cw, mparts=mparts: e.tensor_scalar(
                                            trash1k[:, 0:cw], pss[:, 0:cw], 1.0, None, ALU.mult, ALU.max,
                                            accum_out=mparts[:, ci:ci + 1]))
                                        mx_hist[scnt] = t_mx
                                        scnt += 1
                                    VE.wait(VE, t_mx)
                                    t_mr = VE.inc(lambda e, i=i, h=h, n=len(chunks), mparts=mparts: e.reduce_max(
                                        mabuf[:, i * 16 + h:i * 16 + h + 1], mparts[:, 0:n], axis=AX.X))
                                TE.wait(VE, t_mx)

                            SC.wait(TE, t_av)
                            if h >= 2:
                                SC.wait_dma(GP, t_lm[h - 2])
                            lr = lrow[(h % 2) * 64:(h % 2) * 64 + 1, :]
                            t_l = SC.inc(lambda e, lr=lr: e.copy(lr, ps_ctx[64:65, :]))
                            GP.wait(SC, t_l)
                            t_lm[h] = GP.dma(lambda e, h=h, lr=lr: e.dma_start(out=lmatT[h:h + 1, :], in_=lr), slot=2 + h % 2)
                            dst_c = ctxb[:].rearrange("p (k t) -> p k t", k=8)[(h % 2) * 64:(h % 2) * 64 + 64, h // 2, :]
                            t_ctx_ev = SC.inc(lambda e, dst_c=dst_c: e.activation(dst_c, ps_ctx[0:64, :], AF.Copy, bias=0.0, scale=1.0))
                            TE.wait(SC, t_ctx_ev)
                            TE.wait(SC, t_l)

                    with contextlib.ExitStack() as cctx:
                        wo_sb = sb("wo_sb", [128, 8 * 1024], BF16, cctx)
                        SY.wait(TE, TE.n)
                        SY.wait(SC, SC.n)
                        t_wo = 0
                        wo_src = scr["woT_b"] if is_b else scr["woT_a"]
                        for k in range(8):
                            t_wo = SY.dma(lambda e, k=k, wo_src=wo_src: e.dma_start(
                                out=wo_sb[:, k * 1024:(k + 1) * 1024], in_=wo_src[k * 128:(k + 1) * 128, :]), slot=2)
                        if is_b:
                            VE.wait_dma(GP, t_lm[NH - 1])
                            VE.wait_dma(GP, t_lm[NH - 2])
                            for tt in range(NTS):
                                TE.wait(VE, t_mr)
                                t_tp = TE.inc(lambda e, tt=tt: e.transpose(ps_small[0:16, 0:128], mabuf[:, tt * 16:(tt + 1) * 16], identf[:]))
                                VE.wait(TE, t_tp)
                                t_vm = VE.inc(lambda e, tt=tt: e.tensor_copy(mTb[:, tt * 128:(tt + 1) * 128], ps_small[0:16, 0:128]))
                                TE.wait(VE, t_vm)
                            SC.wait(VE, t_vm)
                            t_em = SC.inc(lambda e: e.activation(mTb[:], mTb[:], AF.Exp, bias=0.0, scale=0.125))
                            VE.wait(SC, t_em)
                            VE.wait_dma(GP, t_lm[NH - 1])
                            VE.wait_dma(GP, t_lm[NH - 2])
                            t_dn = VE.inc(lambda e: e.scalar_tensor_tensor(lmatT[:], mTb[:], ghost_sb[:], lmatT[:], ALU.mult, ALU.add))
                            VE.wait(VE, t_dn)
                            t_rc = VE.inc(lambda e: e.reciprocal(lmatT[:], lmatT[:]))
                        else:
                            VE.wait_dma(GP, t_lm[NH - 1])
                            VE.wait_dma(GP, t_lm[NH - 2])
                            t_rc = VE.inc(lambda e: e.reciprocal(lmatT[:], lmatT[:]))
                        VE.wait(VE, t_rc)
                        t_rc = VE.inc(lambda e: e.tensor_copy(recipb[:], lmatT[:]))

                        if seq == 0 and not is_b:
                            def _tap3(name, ap, shape):
                                def go(t):
                                    GP.wait(VE, VE.n)
                                    GP.wait(SC, SC.n)
                                    GP.dma(lambda e, t=t, ap=ap: e.dma_start(out=t, in_=ap))
                                dbg(name, go, shape)
                            pass
                            _tap3("ptb", ptb[:], [128, 2048])
                        TE.wait(VE, t_rc)
                        TE.wait(SC, t_ctx_ev)
                        TE.wait_dma(SY, t_wo)
                        for tt in range(NTS):
                            tile = base + tt
                            for c in range(8):
                                fn = lambda e, c=c, tt=tt: e.matmul(
                                    ps_ctx[:, c * 128:(c + 1) * 128], e16[:, c * 128:(c + 1) * 128],
                                    recipb[:, tt * 128:(tt + 1) * 128], start=True, stop=True)
                                if c == 7:
                                    t_rm = TE.inc(fn)
                                else:
                                    TE.do(fn)
                            VE.wait(TE, t_rm)
                            src_c = ctxb[:].rearrange("p (k t) -> p k t", k=8)[:, :, tt * 128:(tt + 1) * 128]
                            dst_n = ctxn[:].rearrange("p (k c) -> p k c", k=8)
                            src_r = ps_ctx[:, 0:1024].rearrange("p (k c) -> p k c", k=8)
                            t_nm = VE.inc(lambda e, src_c=src_c, dst_n=dst_n, src_r=src_r: e.tensor_tensor(dst_n, src_c, src_r, ALU.mult))
                            TE.wait(VE, t_nm)
                            for nn in range(2):
                                for k in range(8):
                                    fn = lambda e, nn=nn, k=k, last=(k == 7 and not is_b): e.matmul(
                                        ps_st[nn][:, 0:512], ctxn[:, k * 128:(k + 1) * 128],
                                        wo_sb[:, k * 1024 + nn * 512:k * 1024 + (nn + 1) * 512],
                                        start=(k == 0), stop=(k == 7 and not is_b))
                                    if k == 7 and not is_b:
                                        t_mo = TE.inc(fn)
                                    else:
                                        TE.do(fn)
                                if is_b:
                                    t_mo = TE.inc(lambda e, nn=nn: e.matmul(
                                        ps_st[nn][:, 0:512], ones1[:], bo_row[0:1, nn * 512:(nn + 1) * 512],
                                        start=False, stop=True))
                                VE.wait(TE, t_mo)
                                dst = xr[:, tile * D + nn * 512:tile * D + (nn + 1) * 512]
                                t_res = VE.inc(lambda e, dst=dst, nn=nn: e.tensor_tensor(dst, ps_st[nn][:, 0:512], dst, ALU.add))
                                TE.wait(VE, t_res)
                            resid_tok[tile] = t_res
                        VE.wait(TE, TE.n)
                        SC.wait(TE, TE.n)

        def emit_ffn(final, resid_tok):
            GT = 8
            with contextlib.ExitStack() as fctx:
                stash = sb("stash", [128, GT * 4096], BF16, fctx)
                qscales = sb("qscales", [128, 4 * GT], F32, fctx)
                bufs["xf"] = sb("xln_f32", [128, 4 * 1024], F32, fctx)
                outstage = sb("outstage", [128, 2 * 512], F32, fctx) if final else None
                t_outdma = {}
                gate_sc = lambda tt: qscales[:, tt:tt + 1]
                up_sc = lambda tt: qscales[:, GT + tt:GT + tt + 1]
                down_sc = lambda tt: qscales[:, 2 * GT + tt:2 * GT + tt + 1]

                last_dn_mm = [0]
                for grp in range(4):
                    with contextlib.ExitStack() as f1:
                        rT = sb("rT", [128, 8 * 1024], BF16, f1)
                        wgu = sb("wgu", [128, 2 * 4096], BF16, f1)
                        silu_t = sb("silu_t", [128, 512], F32, f1)
                        ps_g = [psum("ps_g0", [128, 512], F32, f1), psum("ps_g1", [128, 512], F32, f1)]
                        ps_u = [psum("ps_u0", [128, 512], F32, f1), psum("ps_u1", [128, 512], F32, f1)]
                        rT_r = rT[:].rearrange("p (k t) -> p k t", k=8)
                        t_qsc_tok = [0] * GT
                        ln_stats_batch([grp * GT + tt for tt in range(GT)], [resid_tok[grp * GT + tt] for tt in range(GT)])
                        for half in range(2):
                            h0 = half * 4
                            for tt in range(h0, h0 + 4):
                                ln_quant_pre(tt, grp * GT + tt, tt % 4)
                            ln_quant_scales(h0, 4)
                            t_g = VE.inc(lambda e, h0=h0: e.tensor_scalar(qscales[:, h0:h0 + 4], lnstat[:, 112 + h0:116 + h0], meanw[:, 0:1], None, ALU.mult))
                            t_u = VE.inc(lambda e, h0=h0: e.tensor_scalar(qscales[:, GT + h0:GT + h0 + 4], lnstat[:, 112 + h0:116 + h0], meanw[:, 1:2], None, ALU.mult))
                            for tt in range(h0, h0 + 4):
                                t_qsc_tok[tt] = t_u
                                ln_norm_tile(tt, grp * GT + tt, rT_r[:, :, tt * 128:(tt + 1) * 128], tt % 4, quant=True)
                        t_wgu = [0, 0]
                        t_gu_by_buf = [0, 0]
                        t_gu2 = 0
                        t_gu2_by_par = {}
                        t_si_by_par = {}
                        CW = 256
                        for c in range(16):
                            buf = c % 2 if K_DBUF else 0
                            SY.wait(TE, t_gu_by_buf[buf])
                            tok = 0
                            for k in range(8):
                                SY.dma(lambda e, k=k, c=c, buf=buf: e.dma_start(
                                    out=wgu[:, buf * 4096 + k * CW:buf * 4096 + (k + 1) * CW],
                                    in_=scr["wgT"][k * 128:(k + 1) * 128, c * CW:(c + 1) * CW]), slot=buf)
                                tok = SY.dma(lambda e, k=k, c=c, buf=buf: e.dma_start(
                                    out=wgu[:, buf * 4096 + 2048 + k * CW:buf * 4096 + 2048 + (k + 1) * CW],
                                    in_=scr["wuT"][k * 128:(k + 1) * 128, c * CW:(c + 1) * CW]), slot=buf)
                            t_wgu[buf] = tok
                            TE.wait_dma(SY, t_wgu[buf])
                            for tt in range(GT):
                                par = tt % 2
                                if par in t_gu2_by_par:
                                    TE.wait(VE, t_gu2_by_par[par])
                                for k in range(8):
                                    TE.do(lambda e, k=k, tt=tt, buf=buf, par=par: e.matmul(
                                        ps_g[par][:, 0:CW], rT_r[:, k, tt * 128:(tt + 1) * 128],
                                        wgu[:, buf * 4096 + k * CW:buf * 4096 + (k + 1) * CW],
                                        start=(k == 0), stop=(k == 7)))
                                    fn = lambda e, k=k, tt=tt, buf=buf, par=par: e.matmul(
                                        ps_u[par][:, 0:CW], rT_r[:, k, tt * 128:(tt + 1) * 128],
                                        wgu[:, buf * 4096 + 2048 + k * CW:buf * 4096 + 2048 + (k + 1) * CW],
                                        start=(k == 0), stop=(k == 7))
                                    if k == 7:
                                        t_gu = TE.inc(fn)
                                    else:
                                        TE.do(fn)
                                t_gu_by_buf[buf] = t_gu
                                SC.wait(TE, t_gu)
                                if c == 0:
                                    SC.wait(VE, t_qsc_tok[tt])
                                if par in t_gu2_by_par:
                                    SC.wait(VE, t_gu2_by_par[par])
                                t_si = SC.inc(lambda e, par=par, tt=tt: e.activation(
                                    silu_t[:, par * CW:par * CW + CW], ps_g[par][:, 0:CW], AF.Silu, bias=0.0, scale=gate_sc(tt)))
                                VE.wait(SC, t_si)
                                dst = stash[:, tt * 4096 + c * CW:tt * 4096 + (c + 1) * CW]
                                t_gu2 = VE.inc(lambda e, dst=dst, par=par, tt=tt: e.scalar_tensor_tensor(
                                    dst, ps_u[par][:, 0:CW], up_sc(tt), silu_t[:, par * CW:par * CW + CW], ALU.mult, ALU.mult))
                                t_gu2_by_par[par] = t_gu2
                                if not K_PIPE_GU:
                                    TE.wait(VE, t_gu2)

                    if grp == 0:
                        def _tapscr(name, src_ap, shape):
                            def go(t):
                                GP.dma(lambda e, t=t, src_ap=src_ap: e.dma_start(out=t, in_=src_ap))
                            dbg(name, go, shape)
                        _tapscr("scrg", scr["wgT"][0:128, 3072:4096], [128, 1024])
                        _tapscr("scru", scr["wuT"][0:128, 3072:4096], [128, 1024])
                        def _tapf(name, ap, shape):
                            def go(t):
                                for s in (VE, SC, TE):
                                    GP.wait(s, s.n)
                                GP.dma(lambda e, t=t, ap=ap: e.dma_start(out=t, in_=ap))
                                for s in (VE, SC, TE):
                                    s.wait_dma_all(GP)
                            dbg(name, go, shape)
                        _tapf("meanw", meanw[:], [128, 3])
                        _tapf("gu0", stash[:, 0:4096], [128, 4096])
                        _tapf("gu4", stash[:, 4 * 4096:5 * 4096], [128, 4096])
                        _tapf("qsc", qscales[:], [128, 32])
                    t_phase1_mm = TE.n
                    with contextlib.ExitStack() as f2:
                        wdf = sb("wdf", [128, 32 * 512], BF16, f2)
                        ps_t2 = [psum("ps_t2a", [128, 2048], BF16, f2), psum("ps_t2b", [128, 2048], BF16, f2)]
                        ps_d = [psum("ps_d0", [128, 512], F32, f2), psum("ps_d1", [128, 512], F32, f2)]
                        t_q_done = [0] * GT
                        u_last = {}
                        bk_hist = {}
                        rcnt = 0
                        for tt in range(GT):
                            tile = grp * GT + tt
                            p = tt % 2
                            amax = qtmp[:, p:p + 1]
                            sx2 = qtmp[:, 2 + p:3 + p]
                            q2t = qtmp[:, 4 + p:5 + p]
                            VE.wait(VE, t_gu2_by_par[1] if 1 in t_gu2_by_par else t_gu2)
                            VE.wait(VE, t_gu2_by_par.get(0, 0))
                            VE.wait(SC, u_last.get(p, 0))
                            ta = VE.inc(lambda e, tt=tt, amax=amax: e.tensor_reduce(
                                amax, stash[:, tt * 4096:(tt + 1) * 4096], AX.X, ALU.max, apply_absolute_value=True))
                            VE.wait(VE, ta)
                            tb = VE.inc(lambda e, amax=amax: e.tensor_scalar_max(amax, amax, 1e-5))
                            VE.wait(VE, tb)
                            tc = VE.inc(lambda e, amax=amax, sx2=sx2: e.reciprocal(sx2, amax))
                            VE.wait(VE, tc)
                            td = VE.inc(lambda e, sx2=sx2: e.tensor_scalar_mul(sx2, sx2, 127.0))
                            tq2 = VE.inc(lambda e, amax=amax, q2t=q2t: e.tensor_scalar_mul(q2t, amax, 1.0 / 127.0))
                            VE.wait(VE, tq2)
                            tq3 = VE.inc(lambda e, tt=tt, q2t=q2t: e.tensor_mul(down_sc(tt), q2t, meanw[:, 2:3]))
                            SC.wait(VE, td)
                            xfh = bufs["xf"][:, p * 1024:(p + 1) * 1024]
                            for hf in range(4):
                                sl = slice(tt * 4096 + hf * 1024, tt * 4096 + (hf + 1) * 1024)
                                u1 = SC.inc(lambda e, sl=sl, xfh=xfh, sx2=sx2: e.activation(xfh, stash[:, sl], AF.Identity, bias=magic_t[:], scale=sx2))
                                SC.wait(SC, u1)
                                u2 = SC.inc(lambda e, sl=sl, xfh=xfh: e.activation(stash[:, sl], xfh, AF.Copy, bias=-MAGIC, scale=1.0))
                                SC.wait(SC, u2)
                            u_last[p] = u2
                            for rnd in range(2):
                                pt = ps_t2[rcnt % 2]
                                if rcnt - 2 in bk_hist:
                                    TE.wait(VE, bk_hist[rcnt - 2])
                                TE.wait(SC, u2)
                                for kk in range(16):
                                    k = rnd * 16 + kk
                                    fn = lambda e, k=k, kk=kk, tt=tt, pt=pt: e.transpose(
                                        pt[:, kk * 128:(kk + 1) * 128],
                                        stash[:, tt * 4096 + k * 128:tt * 4096 + (k + 1) * 128], ident[:])
                                    if kk == 15:
                                        t_tr = TE.inc(fn)
                                    else:
                                        TE.do(fn)
                                VE.wait(TE, t_tr)
                                dst = stash[:, tt * 4096 + rnd * 2048:tt * 4096 + (rnd + 1) * 2048]
                                t_bk = VE.inc(lambda e, dst=dst, pt=pt: e.tensor_copy(dst, pt[:]))
                                bk_hist[rcnt] = t_bk
                                rcnt += 1
                            t_q_done[tt] = t_bk
                            if grp == 0 and tt in (0, 4):
                                def _tapg(name, ap, shape):
                                    def go(t):
                                        GP.wait(VE, VE.n)
                                        GP.wait(SC, SC.n)
                                        GP.dma(lambda e, t=t, ap=ap: e.dma_start(out=t, in_=ap))
                                        for s in (VE, SC, TE):
                                            s.wait_dma_all(GP)
                                    dbg(name, go, shape)
                                _tapg("r2T%d" % tt, stash[:, tt * 4096:(tt + 1) * 4096], [128, 4096])
                        t_res_by_par = {}
                        for hf in range(2):
                            SY.wait(TE, max(last_dn_mm[0], t_phase1_mm))
                            wd_tok = []
                            for k in range(32):
                                wd_tok.append(SY.dma(lambda e, k=k, hf=hf: e.dma_start(
                                    out=wdf[:, k * 512:(k + 1) * 512],
                                    in_=scr["wdT"][k * 128:(k + 1) * 128, hf * 512:(hf + 1) * 512]), slot=hf))
                            TE.wait_dma(SY, wd_tok[-1])
                            for tt in range(GT):
                                tile = grp * GT + tt
                                par = tt % 2
                                TE.wait(VE, t_q_done[tt])
                                if par in t_res_by_par:
                                    TE.wait(VE, t_res_by_par[par])
                                for k in range(32):
                                    fn = lambda e, k=k, tt=tt, par=par: e.matmul(
                                        ps_d[par][:, 0:512], stash[:, tt * 4096 + k * 128:tt * 4096 + (k + 1) * 128],
                                        wdf[:, k * 512:(k + 1) * 512], start=(k == 0), stop=(k == 31))
                                    if k == 31:
                                        t_dm = TE.inc(fn)
                                    else:
                                        TE.do(fn)
                                last_dn_mm[0] = t_dm
                                VE.wait(TE, t_dm)
                                if not K_PIPE_DN:
                                    pass
                                dst = xr[:, tile * D + hf * 512:tile * D + (hf + 1) * 512]
                                if final:
                                    reg = tt % 2
                                    ob = outstage[:, reg * 512:reg * 512 + 512]
                                    VE.wait_dma(SY, t_outdma.get(reg, 0))
                                    t_res = VE.inc(lambda e, dst=dst, par=par, tt=tt, ob=ob: e.scalar_tensor_tensor(
                                        ob, ps_d[par][:, 0:512], down_sc(tt), dst, ALU.mult, ALU.add))
                                    SY.wait(VE, t_res)
                                    t_outdma[reg] = SY.dma(lambda e, tile=tile, hf=hf, ob=ob: e.dma_start(
                                        out=y_flat[tile * 128:(tile + 1) * 128, hf * 512:(hf + 1) * 512], in_=ob), slot=2 + reg)
                                else:
                                    t_res = VE.inc(lambda e, dst=dst, par=par, tt=tt: e.scalar_tensor_tensor(
                                        dst, ps_d[par][:, 0:512], down_sc(tt), dst, ALU.mult, ALU.add))
                                t_res_by_par[par] = t_res
                                if not K_PIPE_DN:
                                    TE.wait(VE, t_res)
                                resid_tok[tile] = t_res
                    barrier()

        def _tapq(name, src_ap, shape):
            def go(t):
                GP.wait_dma_all(SY)
                GP.dma(lambda e, t=t, src_ap=src_ap: e.dma_start(out=t, in_=src_ap))
            dbg(name, go, shape)
        _tapq("scrq_tail", scr["wqT_a"][0:128, 768:1024], [128, 256])
        _tapq("scrq_head", scr["wqT_a"][0:128, 0:256], [128, 256])

        resid_tok = [0] * NT
        VE.wait_dma(GP, t_x)
        SC.wait_dma(GP, t_x)
        VE.wait_dma(GP, t_ghost)
        TE.wait_dma(GP, t_bo)

        def _tap_sbuf(name, sb_ap, shape):
            def go(t):
                for s in (VE, SC, TE):
                    GP.wait(s, s.n)
                GP.wait_dma_all(GP)
                GP.dma(lambda e, t=t, sb_ap=sb_ap: e.dma_start(out=t, in_=sb_ap))
            dbg(name, go, shape)

        if n_sub >= 1:
            _tap_sbuf("xr0", xr[:, 0:1024], [128, 1024])
            emit_attn(False, resid_tok)
        if n_sub >= 2:
            barrier()
            emit_ffn(final=(n_sub == 2), resid_tok=resid_tok)
        if n_sub >= 3:
            barrier()
            emit_attn(True, resid_tok)
        if n_sub >= 4:
            barrier()
            emit_ffn(final=True, resid_tok=resid_tok)

        if n_sub in (1, 3):
            barrier()
            dumpb = sb("dumpb", [128, 1024], F32)
            for tile in range(NT):
                t_c = SC.inc(lambda e, tile=tile: e.activation(
                    dumpb[:], xr[:, tile * D:(tile + 1) * D], AF.Copy, bias=0.0, scale=1.0))
                SY.wait(SC, t_c)
                tok = SY.dma(lambda e, tile=tile: e.dma_start(out=y_flat[tile * 128:(tile + 1) * 128, :], in_=dumpb[:]))
                SC.wait_dma(SY, tok)

        SY.wait_dma_all(SY)
        GP.wait_dma_all(GP)

        finalize_barrier_ops()
        with nc.Block() as blk:
            blk.tensor(lambda e: TE.run(e))
            blk.vector(lambda e: VE.run(e))
            blk.scalar(lambda e: SC.run(e))
            blk.gpsimd(lambda e: GP.run(e))
            blk.sync(lambda e: SY.run(e))

    return nc


_NC_CACHE = {}


def kernel(L0_attn_a_x, L0_attn_a_wq, L0_attn_a_wk, L0_attn_a_wv, L0_attn_a_wo,
           L0_attn_b_Wqkv_w, L0_attn_b_Wqkv_b, L0_attn_b_Wo_w, L0_attn_b_Wo_b,
           L0_attn_b_ghost, L0_ffn_w_gate, L0_ffn_w_up, L0_ffn_w_down):
    n_sub = N_SUB
    key = (n_sub, tuple(DEBUG))
    if key not in _NC_CACHE:
        _NC_CACHE[key] = build(n_sub)
    nc = _NC_CACHE[key]
    x = np.ascontiguousarray(L0_attn_a_x)
    in_maps = []
    for c in range(8):
        in_maps.append({
            "x": x[2 * c:2 * c + 2],
            "wq_a": L0_attn_a_wq, "wk_a": L0_attn_a_wk, "wv_a": L0_attn_a_wv, "wo_a": L0_attn_a_wo,
            "wqkv_b": L0_attn_b_Wqkv_w, "bqkv_b": L0_attn_b_Wqkv_b,
            "wo_b": L0_attn_b_Wo_w, "bo_b": L0_attn_b_Wo_b, "ghost": L0_attn_b_ghost,
            "w_gate": L0_ffn_w_gate, "w_up": L0_ffn_w_up, "w_down": L0_ffn_w_down,
        })
    res = run_bass_kernel_spmd(nc, in_maps, core_ids=list(range(8)))
    global _dbg_out
    _dbg_out = {k: np.stack([res.results[c]["dbg_" + k] for c in range(8)]) for k in DEBUG}
    return np.concatenate([res.results[c]["y"] for c in range(8)], axis=0)


# version 09 (final)
# speedup vs baseline: 1.1119x; 1.0158x over previous
import contextlib
import numpy as np
import concourse.bass as bass
import concourse.mybir as mybir
from concourse.bass_utils import run_bass_kernel_spmd

F32 = mybir.dt.float32
BF16 = mybir.dt.bfloat16
AF = mybir.ActivationFunctionType
ALU = mybir.AluOpType
AX = mybir.AxisListType

MAGIC = float(2 ** 23 + 2 ** 22)
NEG = -1.0e30

D = 1024
S = 2048
B = 2
NT = 32
NTS = 16
NH = 16
HD = 64
NKV = 4
INNER = 4096
N_SUB = 4
K_DBUF = True
K_PIPE_GU = True
K_PIPE_DN = True
K_WD_KG = True
DEBUG = []
_dbg_out = {}


NSLOT = 4


class St:

    def __init__(self, name):
        self.name = name
        self.ops = []
        self.n = 0
        self.dn = [0] * NSLOT
        self.sem = None
        self.dsem = [None] * NSLOT

    def do(self, fn):
        self.ops.append(("o", fn))

    def inc(self, fn):
        self.n += 1
        self.ops.append(("i", fn))
        return self.n

    def wait(self, other, tok):
        if tok > 0:
            self.ops.append(("w", other, tok))

    def dma(self, fn, slot=0):
        self.dn[slot] += 16
        self.ops.append(("d", fn, slot))
        return (slot, self.dn[slot])

    def wait_dma(self, other, tok):
        if tok and tok[1] > 0:
            self.ops.append(("wd", other, tok))

    def wait_dma_all(self, other):
        for s in range(NSLOT):
            if other.dn[s]:
                self.ops.append(("wd", other, (s, other.dn[s])))

    def run(self, e):
        for op in self.ops:
            k = op[0]
            if k == "o":
                op[1](e)
            elif k == "i":
                op[1](e).then_inc(self.sem, 1)
            elif k == "d":
                op[1](e).then_inc(self.dsem[op[2]], 16)
            elif k == "w":
                e.wait_ge(op[1].sem, op[2])
            elif k == "wd":
                slot, v = op[2]
                e.wait_ge(op[1].dsem[slot], v)
            elif k == "b":
                e.sem_inc(op[1], op[2])
            elif k == "wb":
                e.wait_ge(op[1], op[2])


def c512(q0, qend):
    out = []
    c0 = q0
    while c0 < qend:
        nxt = min((c0 // 512 + 1) * 512, qend)
        out.append((c0, nxt - c0, c0 // 512))
        c0 = nxt
    return out


def build(n_sub=N_SUB, DEBUG=None):
    DEBUG = DEBUG if DEBUG is not None else globals()['DEBUG']
    K_DBUF = globals()["K_DBUF"]
    K_PIPE_GU = globals()["K_PIPE_GU"]
    K_PIPE_DN = globals()["K_PIPE_DN"]
    K_WD_KG = globals()["K_WD_KG"]
    nc = bass.Bass("TRN2", target_bir_lowering=False)

    x_in = nc.dram_tensor("x", [B, S, D], F32, kind="ExternalInput").ap()
    y_out = nc.dram_tensor("y", [B, S, D], F32, kind="ExternalOutput").ap()
    wq_a = nc.dram_tensor("wq_a", [D, D], F32, kind="ExternalInput").ap()
    wk_a = nc.dram_tensor("wk_a", [NKV * HD, D], F32, kind="ExternalInput").ap()
    wv_a = nc.dram_tensor("wv_a", [NKV * HD, D], F32, kind="ExternalInput").ap()
    wo_a = nc.dram_tensor("wo_a", [D, D], F32, kind="ExternalInput").ap()
    wqkv_b = nc.dram_tensor("wqkv_b", [3 * D, D], F32, kind="ExternalInput").ap()
    bqkv_b = nc.dram_tensor("bqkv_b", [3 * D], F32, kind="ExternalInput").ap()
    wo_b = nc.dram_tensor("wo_b", [D, D], F32, kind="ExternalInput").ap()
    bo_b = nc.dram_tensor("bo_b", [D], F32, kind="ExternalInput").ap()
    ghost = nc.dram_tensor("ghost", [NH], F32, kind="ExternalInput").ap()
    w_gate = nc.dram_tensor("w_gate", [INNER, D], F32, kind="ExternalInput").ap()
    w_up = nc.dram_tensor("w_up", [INNER, D], F32, kind="ExternalInput").ap()
    w_down = nc.dram_tensor("w_down", [D, INNER], F32, kind="ExternalInput").ap()

    scr = {}
    scr["wqT_a"] = nc.dram_tensor("scr_wqT_a", [D, D], BF16, kind="Internal").ap()
    scr["wkT_a"] = nc.dram_tensor("scr_wkT_a", [D, NKV * HD], BF16, kind="Internal").ap()
    scr["wvT_a"] = nc.dram_tensor("scr_wvT_a", [D, NKV * HD], BF16, kind="Internal").ap()
    scr["woT_a"] = nc.dram_tensor("scr_woT_a", [D, D], BF16, kind="Internal").ap()
    scr["qkvT_b"] = nc.dram_tensor("scr_qkvT_b", [D, 3 * D], BF16, kind="Internal").ap()
    scr["woT_b"] = nc.dram_tensor("scr_woT_b", [D, D], BF16, kind="Internal").ap()
    scr["wgT"] = nc.dram_tensor("scr_wgT", [D, INNER], BF16, kind="Internal").ap()
    scr["wuT"] = nc.dram_tensor("scr_wuT", [D, INNER], BF16, kind="Internal").ap()
    scr["wdT"] = nc.dram_tensor("scr_wdT", [INNER, D], BF16, kind="Internal").ap()

    x_flat = x_in.rearrange("b s d -> (b s) d")
    y_flat = y_out.rearrange("b s d -> (b s) d")

    dbg_tensors = {}

    def dbg(name, ap_fn, shape, dt=F32):
        if name not in DEBUG:
            return
        t = nc.dram_tensor("dbg_" + name, shape, dt, kind="ExternalOutput").ap()
        dbg_tensors[name] = t
        ap_fn(t)

    TE, VE, SC, GP, SY = (St(n) for n in ("te", "ve", "sc", "gp", "sy"))
    streams = [TE, VE, SC, GP, SY]
    barn = [0]

    def barrier():
        for s in streams:
            s.wait_dma_all(s)
            s.ops.append(("b", None, 1))
        barn[0] += len(streams)
        for s in streams:
            s.ops.append(("wb", None, barn[0]))

    uid = [0]
    with contextlib.ExitStack() as ctx:
        def sb(name, shape, dt=BF16, st=ctx):
            uid[0] += 1
            return st.enter_context(nc.sbuf_tensor("%s_%d" % (name, uid[0]), shape, dt))

        def psum(name, shape, dt=F32, st=ctx):
            uid[0] += 1
            return st.enter_context(nc.psum_tensor("%s_%d" % (name, uid[0]), shape, dt))

        for s in streams:
            s.sem = ctx.enter_context(nc.semaphore(s.name + "_c"))
            for sl in range(NSLOT):
                s.dsem[sl] = ctx.enter_context(nc.semaphore("%s_d%d" % (s.name, sl)))
        barsem = ctx.enter_context(nc.semaphore("bar"))
        for s in streams:
            pass

        def finalize_barrier_ops():
            for s in streams:
                s.ops = [(op[0], barsem, op[2]) if op[0] in ("b", "wb") else op for op in s.ops]

        xr = sb("xr", [128, NT * D], BF16)
        ident = sb("ident", [128, 128], BF16)
        identf = sb("identf", [128, 128], F32)
        maskS = sb("maskS", [128, 128], F32)
        maskT = sb("maskT", [128, 128], F32)
        e16 = sb("e16", [16, 8 * 128], BF16)
        ones1 = sb("ones1", [1, 128], BF16)
        ones1f = sb("ones1f", [1, 128], F32)
        onescol = sb("onescol", [128, 1], F32)
        ghost_sb = sb("ghost_sb", [16, 1], F32)
        eps_t = sb("eps_t", [128, 1], F32)
        magic_t = sb("magic_t", [128, 1], F32)
        bqkT = sb("bqkT", [128, 24], F32)
        bv_row = sb("bv_row", [1, D], BF16)
        bo_row = sb("bo_row", [1, D], BF16)
        meanw = sb("meanw", [128, 3], F32)
        lnstat = sb("lnstat", [128, 128], F32)
        qtmp = sb("qtmp", [128, 8], F32)
        st8 = sb("st8", [128, 8], F32)
        trash1k = sb("trash1k", [128, 1024], BF16)
        xln_tok = sb("xln_tok", [128, 2 * 1024], BF16)
        bufs = {}
        psLN = psum("psLN", [128, 1024], BF16)
        ps_small = psum("ps_small", [128, 128], F32)

        GP.do(lambda e: e.memset(ident[:], 0.0))
        GP.do(lambda e: e.affine_select(out=ident[:], in_=ident[:], compare_op=ALU.not_equal,
                                        fill=1.0, base=0, pattern=[[-1, 128]], channel_multiplier=1))
        GP.do(lambda e: e.memset(identf[:], 0.0))
        GP.do(lambda e: e.affine_select(out=identf[:], in_=identf[:], compare_op=ALU.not_equal,
                                        fill=1.0, base=0, pattern=[[-1, 128]], channel_multiplier=1))
        GP.do(lambda e: e.memset(maskS[:], 0.0))
        GP.do(lambda e: e.affine_select(out=maskS[:], in_=maskS[:], compare_op=ALU.is_ge,
                                        fill=NEG, base=0, pattern=[[-1, 128]], channel_multiplier=1))
        GP.do(lambda e: e.memset(maskT[:], 0.0))
        GP.do(lambda e: e.affine_select(out=maskT[:], in_=maskT[:], compare_op=ALU.is_ge,
                                        fill=NEG, base=0, pattern=[[1, 128]], channel_multiplier=-1))
        GP.do(lambda e: e.memset(e16[:], 0.0))
        for c in range(8):
            for half in range(2):
                cols = slice(c * 128 + half * 64, c * 128 + half * 64 + 64)
                GP.do(lambda e, cols=cols, c=c, half=half: e.affine_select(
                    out=e16[:, cols], in_=e16[:, cols], compare_op=ALU.not_equal,
                    fill=1.0, base=-(2 * c + half), pattern=[[0, 64]], channel_multiplier=1))
        GP.do(lambda e: e.memset(ones1[:], 1.0))
        GP.do(lambda e: e.memset(ones1f[:], 1.0))
        GP.do(lambda e: e.memset(onescol[:], 1.0))
        GP.do(lambda e: e.memset(eps_t[:], 1e-5))
        t_gp_const = GP.inc(lambda e: e.memset(magic_t[:], MAGIC))
        t_ghost = GP.dma(lambda e: e.dma_start(out=ghost_sb[:], in_=ghost.rearrange("(h o) -> h o", o=1)), slot=2)
        t_bv = GP.dma(lambda e: e.dma_start(out=bv_row[:], in_=bqkv_b.rearrange("(o d) -> o d", o=1)[0:1, 2048:3072]), slot=2)
        t_bo = GP.dma(lambda e: e.dma_start(out=bo_row[:], in_=bo_b.rearrange("(o d) -> o d", o=1)), slot=2)
        t_x = 0
        for piece in range(4):
            dst = xr[:].rearrange("p (t d) -> p t d", d=D)[:, piece * 8:(piece + 1) * 8, :]
            src = x_flat[piece * 1024:(piece + 1) * 1024, :].rearrange("(t p) d -> p t d", p=128)
            t_x = GP.dma(lambda e, dst=dst, src=src: e.dma_start(out=dst, in_=src), slot=1)

        with contextlib.ExitStack() as pctx:
            wbuf = sb("wbuf", [128, 2 * 1024], BF16, pctx)
            wfl = sb("wfl", [128, 8192], F32, pctx)
            wql = sb("wql", [128, 2 * 1024], BF16, pctx)
            wtb = sb("wtb", [128, 8 * 1024], BF16, pctx)
            absacc = sb("absacc", [128, 32], F32, pctx)
            b24 = sb("b24", [128, 128], F32, pctx)
            psT0 = [psum("psT0a", [128, 1024], BF16, pctx), psum("psT0b", [128, 1024], BF16, pctx)]

            GP.do(lambda e: e.memset(b24[:], 0.0))
            t_b24 = GP.dma(lambda e: e.dma_start(out=b24[0:24, :], in_=bqkv_b.rearrange("(a b) -> a b", b=128)), slot=2)
            TE.wait_dma(GP, t_b24)
            TE.wait(GP, t_gp_const)
            t_te_b = TE.inc(lambda e: e.transpose(ps_small[:], b24[:], identf[:]))
            VE.wait(TE, t_te_b)
            t_bqk = VE.inc(lambda e: e.tensor_copy(bqkT[:], ps_small[:, 0:24]))
            TE.wait(VE, t_bqk)

            state = {"last_store": 0, "last_evac": 0, "g": 0,
                     "T": {}, "ev": {}, "tc": {}}

            def prep_cast_w2(w_ap, dout, scr_ap):
                ntile = dout // 128
                for g0 in range(0, ntile, 8):
                    gn = min(8, ntile - g0)
                    for mm in range(gn):
                        m = g0 + mm
                        g = state["g"]
                        state["g"] = g + 1
                        p = g % 2
                        GP.wait(TE, state["T"].get(g - 2, 0))
                        tok_l = GP.dma(lambda e, m=m, p=p, w_ap=w_ap: e.dma_start(
                            out=wbuf[:, p * 1024:(p + 1) * 1024], in_=w_ap[m * 128:(m + 1) * 128, :]),
                            slot=3 if p else 0)
                        TE.wait_dma(GP, tok_l)
                        VE.wait_dma(SY, state["last_store"])
                        TE.wait(VE, state["ev"].get(g - 2, 0))
                        for k in range(8):
                            fn = lambda e, k=k, p=p: e.transpose(psT0[p][:, k * 128:(k + 1) * 128],
                                                                 wbuf[:, p * 1024 + k * 128:p * 1024 + (k + 1) * 128], ident[:])
                            if k == 7:
                                state["T"][g] = TE.inc(fn)
                            else:
                                TE.do(fn)
                        VE.wait(TE, state["T"][g])
                        dst = wtb[:].rearrange("p (k d) -> p k d", k=8)[:, :, mm * 128:(mm + 1) * 128]
                        src2 = psT0[p][:].rearrange("p (k c) -> p k c", k=8)
                        state["ev"][g] = VE.inc(lambda e, dst=dst, src2=src2: e.tensor_copy(dst, src2))
                        state["last_evac"] = state["ev"][g]
                    for k in range(8):
                        SY.wait(VE, state["last_evac"])
                        tok_s = SY.dma(lambda e, k=k, scr_ap=scr_ap, g0=g0, gn=gn: e.dma_start(
                            out=scr_ap[k * 128:(k + 1) * 128, g0 * 128:g0 * 128 + gn * 128],
                            in_=wtb[:, k * 1024:k * 1024 + gn * 128]), slot=3)
                    state["last_store"] = tok_s
                return tok_s

            TE.wait(GP, t_gp_const)
            prep_cast_w2(wq_a, D, scr["wqT_a"])
            prep_cast_w2(wk_a, NKV * HD, scr["wkT_a"])
            prep_cast_w2(wv_a, NKV * HD, scr["wvT_a"])
            prep_cast_w2(wo_a, D, scr["woT_a"])
            prep_cast_w2(wqkv_b, 3 * D, scr["qkvT_b"])
            prep_cast_w2(wo_b, D, scr["woT_b"])

            def ffn_mean_pass(w_ap, ntile, ncol, meanw_col):
                tok_r = 0
                r_hist = {}
                SY.wait(VE, state["tc"].get(state["g"] - 1, 0))
                SY.wait(VE, state["tc"].get(state["g"] - 2, 0))
                for m in range(ntile):
                    off = (m % 2) * 4096
                    if m - 2 in r_hist:
                        SY.wait(VE, r_hist[m - 2])
                    tok_l = SY.dma(lambda e, m=m, w_ap=w_ap, ncol=ncol, off=off: e.dma_start(
                        out=wfl[:, off:off + ncol], in_=w_ap[m * 128:(m + 1) * 128, :]), slot=m % 2)
                    VE.wait_dma(SY, tok_l)
                    tok_r = VE.inc(lambda e, m=m, ncol=ncol, off=off: e.reduce_sum(
                        absacc[:, m:m + 1], wfl[:, off:off + ncol], axis=AX.X, apply_absolute_value=True))
                    r_hist[m] = tok_r
                SY.wait(VE, tok_r)
                VE.wait(VE, tok_r)
                t1 = VE.inc(lambda e, ntile=ntile: e.reduce_sum(absacc[:, 0:1], absacc[:, 0:ntile], axis=AX.X))
                TE.wait(VE, t1)
                t2 = TE.inc(lambda e: e.matmul(ps_small[0:1, 0:1], absacc[:, 0:1], onescol[:], start=True, stop=True))
                VE.wait(TE, t2)
                t3 = VE.inc(lambda e, n=float(ntile * 128 * ncol): e.tensor_scalar(
                    st8[0:1, 0:1], ps_small[0:1, 0:1], 1.0 / n, 1e-5, ALU.mult, ALU.max))
                TE.wait(VE, t3)
                t4 = TE.inc(lambda e: e.matmul(ps_small[:, 0:1], ones1f[0:1, :], st8[0:1, 0:1], start=True, stop=True))
                VE.wait(TE, t4)
                t5 = VE.inc(lambda e, c=meanw_col: e.tensor_copy(meanw[:, c:c + 1], ps_small[:, 0:1]))
                VE.wait(VE, t5)
                t6 = VE.inc(lambda e, c=meanw_col: e.reciprocal(st8[:, 1:2], meanw[:, c:c + 1]))
                TE.wait(VE, t6)
                SC.wait(VE, t6)
                return t6

            def prep_tern(src_ap_fn, n_m, store_fn):
                for m in range(n_m):
                    g = state["g"]
                    state["g"] = g + 1
                    p = g % 2
                    off = p * 4096
                    SY.wait(VE, state["tc"].get(g - 2, 0))
                    tok_l = SY.dma(lambda e, m=m, off=off: e.dma_start(
                        out=wfl[:, off:off + 1024], in_=src_ap_fn(m)), slot=p)
                    SC.wait_dma(SY, tok_l)
                    ta = SC.inc(lambda e, off=off: e.activation(wfl[:, off:off + 1024], wfl[:, off:off + 1024], AF.Identity, bias=magic_t[:], scale=st8[:, 1:2]))
                    SC.wait(SC, ta)
                    tb = SC.inc(lambda e, off=off: e.activation(wfl[:, off:off + 1024], wfl[:, off:off + 1024], AF.Copy, bias=-MAGIC, scale=1.0))
                    VE.wait(SC, tb)
                    VE.wait(TE, state["T"].get(g - 2, 0))
                    tc = VE.inc(lambda e, p=p, off=off: e.tensor_scalar(
                        wql[:, p * 1024:(p + 1) * 1024], wfl[:, off:off + 1024], -1.0, 1.0, ALU.max, ALU.min))
                    state["tc"][g] = tc
                    VE.wait_dma(SY, state["last_store"])
                    TE.wait(VE, tc)
                    TE.wait(VE, state["ev"].get(g - 2, 0))
                    for k in range(8):
                        fn = lambda e, k=k, p=p: e.transpose(psT0[p][:, k * 128:(k + 1) * 128],
                                                             wql[:, p * 1024 + k * 128:p * 1024 + (k + 1) * 128], ident[:])
                        if k == 7:
                            state["T"][g] = TE.inc(fn)
                        else:
                            TE.do(fn)
                    VE.wait(TE, state["T"][g])
                    dst = wtb[:].rearrange("p (k d) -> p k d", k=8)[:, :, (m % 8) * 128:(m % 8) * 128 + 128]
                    src2 = psT0[p][:].rearrange("p (k c) -> p k c", k=8)
                    state["ev"][g] = VE.inc(lambda e, dst=dst, src2=src2: e.tensor_copy(dst, src2))
                    state["last_evac"] = state["ev"][g]
                    if m % 8 == 7:
                        store_fn(m // 8)

            def prep_ffn_gu(w_ap, scr_ap, meanw_col):
                ffn_mean_pass(w_ap, 32, 1024, meanw_col)

                def store(grp):
                    tok_s = 0
                    for k in range(8):
                        SY.wait(VE, state["last_evac"])
                        tok_s = SY.dma(lambda e, k=k, grp=grp, scr_ap=scr_ap: e.dma_start(
                            out=scr_ap[k * 128:(k + 1) * 128, grp * 1024:(grp + 1) * 1024],
                            in_=wtb[:, k * 1024:(k + 1) * 1024]), slot=3)
                    state["last_store"] = tok_s
                prep_tern(lambda m, w_ap=w_ap: w_ap[m * 128:(m + 1) * 128, :], 32, store)

            def prep_ffn_wd():
                ffn_mean_pass(w_down, 8, 4096, 2)

                def store(grp):
                    tok_s = 0
                    for k in range(8):
                        SY.wait(VE, state["last_evac"])
                        tok_s = SY.dma(lambda e, k=k, kg=grp: e.dma_start(
                            out=scr["wdT"][(kg * 8 + k) * 128:(kg * 8 + k + 1) * 128, :],
                            in_=wtb[:, k * 1024:(k + 1) * 1024]), slot=3)
                    state["last_store"] = tok_s

                def src_fn(m):
                    kg, mm = m // 8, m % 8
                    return w_down[mm * 128:(mm + 1) * 128, kg * 1024:(kg + 1) * 1024]
                prep_tern(src_fn, 32, store)

            prep_ffn_gu(w_gate, scr["wgT"], 0)
            prep_ffn_gu(w_up, scr["wuT"], 1)
            prep_ffn_wd()
            barrier()

        ln_state = {"last_sc": 0}
        tr_state = {}

        def ln_stats_batch(tiles, gate_toks):
            nt = len(tiles)
            VE.wait(SC, ln_state["last_sc"])
            for i, tile in enumerate(tiles):
                gtok = gate_toks[i]
                SC.wait(VE, gtok)
                VE.wait(VE, gtok)
                srcx = xr[:, tile * D:(tile + 1) * D]
                t_sq = SC.inc(lambda e, srcx=srcx, i=i: e.activation(
                    trash1k[:], srcx, AF.Square, bias=0.0, scale=1.0, accum_out=lnstat[:, 16 + i:17 + i]))
                VE.inc(lambda e, srcx=srcx, i=i: e.reduce_sum(lnstat[:, i:i + 1], srcx, axis=AX.X))
            VE.wait(SC, t_sq)
            VE.wait(VE, VE.n)
            sx, ssq = lnstat[:, 0:nt], lnstat[:, 16:16 + nt]
            mean, rstd, nbias = lnstat[:, 32:32 + nt], lnstat[:, 48:48 + nt], lnstat[:, 64:64 + nt]
            t1 = VE.inc(lambda e, sx=sx, mean=mean: e.tensor_scalar_mul(mean, sx, 1.0 / D))
            VE.wait(VE, t1)
            tmp = lnstat[:, 96:96 + nt]
            t2 = VE.inc(lambda e, mean=mean, tmp=tmp: e.tensor_mul(tmp, mean, mean))
            VE.wait(VE, t2)
            t3 = VE.inc(lambda e, ssq=ssq, tmp=tmp, rstd=rstd: e.scalar_tensor_tensor(rstd, ssq, 1.0 / D, tmp, ALU.mult, ALU.subtract))
            SC.wait(VE, t3)
            t4 = SC.inc(lambda e, rstd=rstd: e.activation(rstd, rstd, AF.Sqrt, bias=eps_t[:], scale=1.0))
            VE.wait(SC, t4)
            t5 = VE.inc(lambda e, rstd=rstd: e.reciprocal(rstd, rstd))
            VE.wait(VE, t5)
            t6 = VE.inc(lambda e, mean=mean, rstd=rstd, nbias=nbias: e.scalar_tensor_tensor(nbias, mean, -1.0, rstd, ALU.mult, ALU.mult))
            SC.wait(VE, t6)
            return t6

        def ln_norm_tile(i, tile, dst_fm, par, quant):
            srcx = xr[:, tile * D:(tile + 1) * D]
            rstd = lnstat[:, 48 + i:49 + i]
            nbias = lnstat[:, 64 + i:65 + i]
            xtok = xln_tok[:, (par % 2) * 1024:(par % 2) * 1024 + 1024]
            SC.wait(TE, tr_state.get(par % 2, 0))
            if not quant:
                t_x = SC.inc(lambda e, srcx=srcx, xtok=xtok, rstd=rstd, nbias=nbias: e.activation(
                    xtok, srcx, AF.Identity, bias=nbias, scale=rstd))
            else:
                sxq = lnstat[:, 96 + i:97 + i]
                xf = bufs["xf"][:, par * 1024:(par + 1) * 1024]
                tg = SC.inc(lambda e, xf=xf, sxq=sxq: e.activation(xf, xf, AF.Identity, bias=magic_t[:], scale=sxq))
                SC.wait(SC, tg)
                t_x = SC.inc(lambda e, xf=xf, xtok=xtok: e.activation(xtok, xf, AF.Copy, bias=-MAGIC, scale=1.0))
            ln_state["last_sc"] = t_x
            TE.wait(SC, t_x)
            for k in range(8):
                fn = lambda e, k=k, xtok=xtok: e.transpose(psLN[:, k * 128:(k + 1) * 128], xtok[:, k * 128:(k + 1) * 128], ident[:])
                if k == 7:
                    tr_state[par % 2] = TE.inc(fn)
                else:
                    TE.do(fn)
            VE.wait(TE, TE.n)
            src_ps = psLN[:].rearrange("p (k c) -> p k c", k=8)
            t_e = VE.inc(lambda e, dst=dst_fm, src=src_ps: e.tensor_copy(dst_fm, src))
            TE.wait(VE, t_e)
            return t_e

        def ln_quant_pre(i, tile, par):
            srcx = xr[:, tile * D:(tile + 1) * D]
            rstd = lnstat[:, 48 + i:49 + i]
            nbias = lnstat[:, 64 + i:65 + i]
            amax = lnstat[:, 80 + i:81 + i]
            xf = bufs["xf"][:, par * 1024:(par + 1) * 1024]
            t_f = SC.inc(lambda e, srcx=srcx, xf=xf, rstd=rstd, nbias=nbias: e.activation(
                xf, srcx, AF.Identity, bias=nbias, scale=rstd))
            ln_state["last_sc"] = t_f
            VE.wait(SC, t_f)
            t_a = VE.inc(lambda e, xf=xf, amax=amax: e.tensor_reduce(amax, xf, AX.X, ALU.max, apply_absolute_value=True))
            return t_a

        def ln_quant_scales(i0, nt):
            amax = lnstat[:, 80 + i0:80 + i0 + nt]
            sxq = lnstat[:, 96 + i0:96 + i0 + nt]
            qsc = lnstat[:, 112 + i0:112 + i0 + nt]
            VE.wait(VE, VE.n)
            t1 = VE.inc(lambda e, amax=amax, qsc=qsc: e.tensor_scalar(qsc, amax, 1.0 / 127.0, 1e-5 / 127.0, ALU.mult, ALU.max))
            VE.wait(VE, t1)
            t2 = VE.inc(lambda e, qsc=qsc, sxq=sxq: e.reciprocal(sxq, qsc))
            SC.wait(VE, t2)
            return t2

        def emit_attn(is_b, resid_tok):
            nkv = NH if is_b else NKV
            vheads = 8 if is_b else 4
            with contextlib.ExitStack() as actx:
                vbuf = sb("vbuf", [128, 16 * vheads * 65], BF16, actx)
                ctxb = sb("ctxb", [128, 8 * 2048], BF16, actx)
                kbuf = sb("kbuf", [64, 2048], BF16, actx)
                qbuf = sb("qbuf", [64, 2048], BF16, actx)
                ptb = sb("ptb", [128, 2 * 2048], BF16, actx)
                lrow = sb("lrow", [65, 2048], BF16, actx)
                lmatT = sb("lmatT", [16, 2048], F32, actx)
                ctxn = sb("ctxn", [128, 1024], BF16, actx)
                recipb = sb("recipb", [16, 2048], BF16, actx)
                wslice = sb("wslice", [128, 2 * 1024], BF16, actx)
                if is_b:
                    mabuf = sb("mabuf", [128, 16 * 16], F32, actx)
                    mTb = lrow[0:16, :]
                ps_st = [psum("ps_st0", [128, 512], F32, actx), psum("ps_st1", [128, 512], F32, actx)]
                ps_ctx = psum("ps_ctx", [128, 2048], F32, actx)

                for seq in range(B):
                    base = seq * NTS
                    qoff = 0
                    with contextlib.ExitStack() as bctx:
                        xlnT = sb("xlnT", [128, 8 * 2048], BF16, bctx)
                        wv_sb = sb("wv_sb", [128, 8 * vheads * 64], BF16, bctx)
                        if not is_b:
                            wk_sb = sb("wk_sb", [128, 8 * 256], BF16, bctx)
                        xlnT_r = xlnT[:].rearrange("p (k t) -> p k t", k=8)

                        ln_stats_batch([base + tt for tt in range(NTS)], [resid_tok[base + tt] for tt in range(NTS)])
                        for tt in range(NTS):
                            tile = base + tt
                            ln_norm_tile(tt, tile, xlnT_r[:, :, tt * 128:(tt + 1) * 128], tt % 2, quant=False)
                        t_ln_all = VE.n
                        if seq == 0 and not is_b:
                            def _tap(name, ap, shape):
                                def go(t):
                                    for s in (VE, SC, TE):
                                        GP.wait(s, s.n)
                                    GP.dma(lambda e, t=t, ap=ap: e.dma_start(out=t, in_=ap))
                                dbg(name, go, shape)
                            _tap("xlnT", xlnT[:, 0:2048], [128, 2048])

                        if not is_b:
                            t_wk = 0
                            for k in range(8):
                                t_wk = SY.dma(lambda e, k=k: e.dma_start(out=wk_sb[:, k * 256:(k + 1) * 256],
                                                                  in_=scr["wkT_a"][k * 128:(k + 1) * 128, :]), slot=2)

                        prev_proj_mm = [0, 0]
                        wsl_tok = [0] * (NH + 1)

                        def prefetch_slices(h):
                            if h >= NH:
                                return
                            wb = (h % 2) * 1024
                            SY.wait(TE, prev_proj_mm[h % 2])
                            if is_b:
                                SY.dma(lambda e, h=h, wb=wb: e.dma_start(
                                    out=wslice[:, wb:wb + 512].rearrange("p (k c) -> p k c", k=8),
                                    in_=scr["qkvT_b"][:, h * 64:(h + 1) * 64].rearrange("(k p) c -> p k c", p=128)), slot=h % 2)
                                wsl_tok[h] = SY.dma(lambda e, h=h, wb=wb: e.dma_start(
                                    out=wslice[:, wb + 512:wb + 1024].rearrange("p (k c) -> p k c", k=8),
                                    in_=scr["qkvT_b"][:, 1024 + h * 64:1024 + (h + 1) * 64].rearrange("(k p) c -> p k c", p=128)), slot=h % 2)
                            else:
                                wsl_tok[h] = SY.dma(lambda e, h=h, wb=wb: e.dma_start(
                                    out=wslice[:, wb:wb + 512].rearrange("p (k c) -> p k c", k=8),
                                    in_=scr["wqT_a"][:, h * 64:(h + 1) * 64].rearrange("(k p) c -> p k c", p=128)), slot=h % 2)

                        prefetch_slices(0)
                        kev_hold = [0, 0, 0, 0]
                        t_kq = 0
                        t_av_hist = {}
                        t_ctx_ev = 0
                        t_lm = [0] * NH
                        for h in range(NH):
                            g = h if is_b else h // 4
                            hh = g % vheads
                            if (is_b and h % 8 == 0) or (not is_b and h == 0):
                                v0 = (h // 8) * 8 if is_b else 0
                                ndv = vheads * 64
                                GP.wait(TE, TE.n)
                                t_vb = GP.inc(lambda e: e.memset(vbuf[:], 1.0))
                                t_wv = 0
                                if is_b:
                                    vsrc = scr["qkvT_b"]
                                    col0 = 2048 + v0 * 64
                                else:
                                    vsrc = scr["wvT_a"]
                                    col0 = 0
                                for k in range(8):
                                    t_wv = SY.dma(lambda e, k=k, vsrc=vsrc, col0=col0, ndv=ndv: e.dma_start(
                                        out=wv_sb[:, k * ndv:(k + 1) * ndv],
                                        in_=vsrc[k * 128:(k + 1) * 128, col0:col0 + ndv]), slot=2)
                                TE.wait(GP, t_vb)
                                TE.wait_dma(SY, t_wv)
                                TE.wait(VE, t_ln_all)
                                vbuf_r = vbuf[:].rearrange("p (t h c) -> p t h c", t=16, h=vheads)
                                for tt in range(NTS):
                                    for (c0, cw, bank) in c512(0, ndv):
                                        for k in range(8):
                                            last = (k == 7) and (not is_b)
                                            fn = lambda e, k=k, tt=tt, c0=c0, cw=cw, last=last: e.matmul(
                                                ps_ctx[:, c0:c0 + cw],
                                                xlnT_r[:, k, tt * 128:(tt + 1) * 128],
                                                wv_sb[:, k * ndv + c0:k * ndv + c0 + cw],
                                                start=(k == 0), stop=last)
                                            if last:
                                                t_mmv = TE.inc(fn)
                                            else:
                                                TE.do(fn)
                                        if is_b:
                                            t_mmv = TE.inc(lambda e, c0=c0, cw=cw, col0=col0: e.matmul(
                                                ps_ctx[:, c0:c0 + cw], ones1[:],
                                                bv_row[0:1, col0 - 2048 + c0:col0 - 2048 + c0 + cw],
                                                start=False, stop=True))
                                        VE.wait(TE, t_mmv)
                                        VE.wait(GP, t_vb)
                                        nh0 = c0 // 64
                                        nh1 = (c0 + cw) // 64
                                        dst = vbuf_r[:, tt, nh0:nh1, 0:64]
                                        t_vev = VE.inc(lambda e, dst=dst, c0=c0, cw=cw: e.tensor_copy(
                                            dst, ps_ctx[:, c0:c0 + cw].rearrange("p (h c) -> p h c", c=64)))
                                        TE.wait(VE, t_vev)

                            def proj64(buf, wof, bias):
                                t_ev = 0
                                ev_hist = {}
                                for pi, (c0, cw, bank) in enumerate(c512(0, 2048)):
                                    pq = ps_st[pi % 2]
                                    if pi - 2 in ev_hist:
                                        TE.wait(SC, ev_hist[pi - 2])
                                    for k in range(8):
                                        fn = lambda e, k=k, c0=c0, cw=cw, buf=buf, wof=wof, pq=pq: e.matmul(
                                            pq[0:64, 0:cw], wof(k), xlnT_r[:, k, c0:c0 + cw],
                                            start=(k == 0), stop=(k == 7))
                                        if k == 7:
                                            t_mm = TE.inc(fn)
                                        else:
                                            TE.do(fn)
                                    SC.wait(TE, t_mm)
                                    if bias is None:
                                        t_ev = SC.inc(lambda e, buf=buf, c0=c0, cw=cw, pq=pq: e.activation(
                                            buf[:, c0:c0 + cw], pq[0:64, 0:cw], AF.Copy, bias=0.0, scale=1.0))
                                    else:
                                        t_ev = SC.inc(lambda e, buf=buf, c0=c0, cw=cw, bias=bias, pq=pq: e.activation(
                                            buf[:, c0:c0 + cw], pq[0:64, 0:cw], AF.Identity, bias=bias, scale=1.0))
                                    ev_hist[pi] = t_ev
                                return t_ev, t_mm, [ev_hist[i] for i in range(4)]

                            wb = (h % 2) * 1024
                            TE.wait_dma(SY, wsl_tok[h])
                            if is_b:
                                wq_k = lambda k, wb=wb: wslice[:, wb + k * 64:wb + (k + 1) * 64]
                                wk_k = lambda k, wb=wb: wslice[:, wb + 512 + k * 64:wb + 512 + (k + 1) * 64]
                                bq = bqkT[(h % 2) * 64:(h % 2) * 64 + 64, h // 2:h // 2 + 1]
                                bk = bqkT[(h % 2) * 64:(h % 2) * 64 + 64, 8 + h // 2:8 + h // 2 + 1]
                            else:
                                TE.wait_dma(SY, t_wk)
                                wq_k = lambda k, wb=wb: wslice[:, wb + k * 64:wb + (k + 1) * 64]
                                wk_k = lambda k, g=g: wk_sb[:, k * 256 + g * 64:k * 256 + (g + 1) * 64]
                                bq = bk = None

                            if is_b or h % 4 == 0:
                                _, _, kev = proj64(kbuf, wk_k, bk)
                                kev_hold = kev
                            else:
                                kev = kev_hold
                            t_kq, t_proj_mm, qev = proj64(qbuf, wq_k, bq)
                            prev_proj_mm[h % 2] = t_proj_mm
                            prefetch_slices(h + 1)
                            if seq == 0 and h in (0, 14, 15) and not is_b:
                                def _tap2(name, ap, shape):
                                    def go(t):
                                        GP.wait(SC, SC.n)
                                        GP.wait(VE, VE.n)
                                        GP.dma(lambda e, t=t, ap=ap: e.dma_start(out=t, in_=ap))
                                    dbg(name, go, shape)
                                _tap2("kbuf%d" % h, kbuf[:], [64, 2048])
                                _tap2("qbuf%d" % h, qbuf[:], [64, 2048])

                            cnt = 0
                            exp_hist = {}
                            avq = []
                            last_av_of = {}

                            def flush_one():
                                fn_av, tok_e, jj = avq.pop(0)
                                TE.wait(SC, tok_e)
                                last_av_of[jj] = TE.inc(fn_av)

                            for j in range(NTS):
                                q0 = j * 128
                                par = j % 2
                                while avq and avq[0][2] <= j - 2:
                                    flush_one()
                                if j - 2 in last_av_of:
                                    SC.wait(TE, last_av_of[j - 2])
                                chunks = c512(q0, 2048)
                                TE.wait(SC, kev[(j * 128 + 127) // 512])
                                for ci, (c0, cw, bank) in enumerate(chunks):
                                    pst = ps_st[cnt % 2]
                                    if cnt == 0:
                                        TE.wait(SC, qev[2])
                                    if cnt == 1:
                                        TE.wait(SC, qev[3])
                                    if cnt - 2 in exp_hist:
                                        TE.wait(SC, exp_hist[cnt - 2])
                                    TE.wait(SC, qev[(c0 + cw - 1) // 512])
                                    fn = lambda e, j=j, c0=c0, cw=cw, pst=pst: e.matmul(
                                        pst[:, 0:cw], kbuf[:, j * 128:(j + 1) * 128], qbuf[:, c0:c0 + cw],
                                        start=True, stop=True)
                                    t_mm = TE.inc(fn)
                                    if ci == 0:
                                        VE.wait(TE, t_mm)
                                        t_mk = VE.inc(lambda e, pst=pst: e.tensor_tensor(pst[:, 0:128], pst[:, 0:128], maskT[:], ALU.add))
                                        SC.wait(VE, t_mk)
                                    SC.wait(TE, t_mm)
                                    t_exp = SC.inc(lambda e, pst=pst, par=par, c0=c0, cw=cw: e.activation(
                                        ptb[:, par * 2048 + c0:par * 2048 + c0 + cw], pst[:, 0:cw],
                                        AF.Exp, bias=0.0, scale=0.125))
                                    exp_hist[cnt] = t_exp
                                    vsl = vbuf[:, j * (vheads * 65) + hh * 65:j * (vheads * 65) + (hh + 1) * 65]
                                    fn_av = lambda e, j=j, vsl=vsl, par=par, c0=c0, cw=cw, bank=bank: e.matmul(
                                        ps_ctx[0:65, c0:c0 + cw], vsl,
                                        ptb[:, par * 2048 + c0:par * 2048 + c0 + cw],
                                        start=(j == 0), stop=(j == min(NTS - 1, 4 * bank + 3)))
                                    avq.append((fn_av, t_exp, j))
                                    while len(avq) > 2:
                                        flush_one()
                                    cnt += 1
                            while avq:
                                flush_one()
                            t_av = last_av_of[NTS - 1]

                            if is_b:
                                TE.wait(SC, t_kq)
                                TE.wait(SC, kev[3])
                                scnt = 0
                                mx_hist = {}
                                for i in range(NTS):
                                    L = (i + 1) * 128
                                    chunks = c512(0, L)
                                    mparts = st8[:, 0:4]
                                    for ci, (c0, cw, bank) in enumerate(chunks):
                                        pss = ps_st[scnt % 2]
                                        if scnt - 2 in mx_hist:
                                            TE.wait(VE, mx_hist[scnt - 2])
                                        fn = lambda e, i=i, c0=c0, cw=cw, pss=pss: e.matmul(
                                            pss[:, 0:cw], qbuf[:, i * 128:(i + 1) * 128], kbuf[:, c0:c0 + cw],
                                            start=True, stop=True)
                                        t_mm = TE.inc(fn)
                                        VE.wait(TE, t_mm)
                                        if ci == len(chunks) - 1:
                                            off = L - 128 - c0
                                            t_mk = VE.inc(lambda e, pss=pss, off=off: e.tensor_tensor(
                                                pss[:, off:off + 128], pss[:, off:off + 128], maskS[:], ALU.add))
                                            VE.wait(VE, t_mk)
                                        t_mx = VE.inc(lambda e, pss=pss, ci=ci, cw=cw, mparts=mparts: e.tensor_scalar(
                                            trash1k[:, 0:cw], pss[:, 0:cw], 1.0, None, ALU.mult, ALU.max,
                                            accum_out=mparts[:, ci:ci + 1]))
                                        mx_hist[scnt] = t_mx
                                        scnt += 1
                                    VE.wait(VE, t_mx)
                                    t_mr = VE.inc(lambda e, i=i, h=h, n=len(chunks), mparts=mparts: e.reduce_max(
                                        mabuf[:, i * 16 + h:i * 16 + h + 1], mparts[:, 0:n], axis=AX.X))
                                TE.wait(VE, t_mx)

                            SC.wait(TE, t_av)
                            if h >= 2:
                                SC.wait_dma(GP, t_lm[h - 2])
                            lr = lrow[(h % 2) * 64:(h % 2) * 64 + 1, :]
                            t_l = SC.inc(lambda e, lr=lr: e.copy(lr, ps_ctx[64:65, :]))
                            GP.wait(SC, t_l)
                            t_lm[h] = GP.dma(lambda e, h=h, lr=lr: e.dma_start(out=lmatT[h:h + 1, :], in_=lr), slot=2 + h % 2)
                            dst_c = ctxb[:].rearrange("p (k t) -> p k t", k=8)[(h % 2) * 64:(h % 2) * 64 + 64, h // 2, :]
                            t_ctx_ev = SC.inc(lambda e, dst_c=dst_c: e.activation(dst_c, ps_ctx[0:64, :], AF.Copy, bias=0.0, scale=1.0))
                            TE.wait(SC, t_ctx_ev)
                            TE.wait(SC, t_l)

                    with contextlib.ExitStack() as cctx:
                        wo_sb = sb("wo_sb", [128, 8 * 1024], BF16, cctx)
                        SY.wait(TE, TE.n)
                        SY.wait(SC, SC.n)
                        t_wo = 0
                        wo_src = scr["woT_b"] if is_b else scr["woT_a"]
                        for k in range(8):
                            t_wo = SY.dma(lambda e, k=k, wo_src=wo_src: e.dma_start(
                                out=wo_sb[:, k * 1024:(k + 1) * 1024], in_=wo_src[k * 128:(k + 1) * 128, :]), slot=2)
                        if is_b:
                            VE.wait_dma(GP, t_lm[NH - 1])
                            VE.wait_dma(GP, t_lm[NH - 2])
                            for tt in range(NTS):
                                TE.wait(VE, t_mr)
                                t_tp = TE.inc(lambda e, tt=tt: e.transpose(ps_small[0:16, 0:128], mabuf[:, tt * 16:(tt + 1) * 16], identf[:]))
                                VE.wait(TE, t_tp)
                                t_vm = VE.inc(lambda e, tt=tt: e.tensor_copy(mTb[:, tt * 128:(tt + 1) * 128], ps_small[0:16, 0:128]))
                                TE.wait(VE, t_vm)
                            SC.wait(VE, t_vm)
                            t_em = SC.inc(lambda e: e.activation(mTb[:], mTb[:], AF.Exp, bias=0.0, scale=0.125))
                            VE.wait(SC, t_em)
                            VE.wait_dma(GP, t_lm[NH - 1])
                            VE.wait_dma(GP, t_lm[NH - 2])
                            t_dn = VE.inc(lambda e: e.scalar_tensor_tensor(lmatT[:], mTb[:], ghost_sb[:], lmatT[:], ALU.mult, ALU.add))
                            VE.wait(VE, t_dn)
                            t_rc = VE.inc(lambda e: e.reciprocal(lmatT[:], lmatT[:]))
                        else:
                            VE.wait_dma(GP, t_lm[NH - 1])
                            VE.wait_dma(GP, t_lm[NH - 2])
                            t_rc = VE.inc(lambda e: e.reciprocal(lmatT[:], lmatT[:]))
                        VE.wait(VE, t_rc)
                        t_rc = VE.inc(lambda e: e.tensor_copy(recipb[:], lmatT[:]))

                        if seq == 0 and not is_b:
                            def _tap3(name, ap, shape):
                                def go(t):
                                    GP.wait(VE, VE.n)
                                    GP.wait(SC, SC.n)
                                    GP.dma(lambda e, t=t, ap=ap: e.dma_start(out=t, in_=ap))
                                dbg(name, go, shape)
                            pass
                            _tap3("ptb", ptb[:], [128, 2048])
                        TE.wait(VE, t_rc)
                        TE.wait(SC, t_ctx_ev)
                        TE.wait_dma(SY, t_wo)
                        for tt in range(NTS):
                            tile = base + tt
                            for c in range(8):
                                fn = lambda e, c=c, tt=tt: e.matmul(
                                    ps_ctx[:, c * 128:(c + 1) * 128], e16[:, c * 128:(c + 1) * 128],
                                    recipb[:, tt * 128:(tt + 1) * 128], start=True, stop=True)
                                if c == 7:
                                    t_rm = TE.inc(fn)
                                else:
                                    TE.do(fn)
                            VE.wait(TE, t_rm)
                            src_c = ctxb[:].rearrange("p (k t) -> p k t", k=8)[:, :, tt * 128:(tt + 1) * 128]
                            dst_n = ctxn[:].rearrange("p (k c) -> p k c", k=8)
                            src_r = ps_ctx[:, 0:1024].rearrange("p (k c) -> p k c", k=8)
                            t_nm = VE.inc(lambda e, src_c=src_c, dst_n=dst_n, src_r=src_r: e.tensor_tensor(dst_n, src_c, src_r, ALU.mult))
                            TE.wait(VE, t_nm)
                            for nn in range(2):
                                for k in range(8):
                                    fn = lambda e, nn=nn, k=k, last=(k == 7 and not is_b): e.matmul(
                                        ps_st[nn][:, 0:512], ctxn[:, k * 128:(k + 1) * 128],
                                        wo_sb[:, k * 1024 + nn * 512:k * 1024 + (nn + 1) * 512],
                                        start=(k == 0), stop=(k == 7 and not is_b))
                                    if k == 7 and not is_b:
                                        t_mo = TE.inc(fn)
                                    else:
                                        TE.do(fn)
                                if is_b:
                                    t_mo = TE.inc(lambda e, nn=nn: e.matmul(
                                        ps_st[nn][:, 0:512], ones1[:], bo_row[0:1, nn * 512:(nn + 1) * 512],
                                        start=False, stop=True))
                                VE.wait(TE, t_mo)
                                dst = xr[:, tile * D + nn * 512:tile * D + (nn + 1) * 512]
                                t_res = VE.inc(lambda e, dst=dst, nn=nn: e.tensor_tensor(dst, ps_st[nn][:, 0:512], dst, ALU.add))
                                TE.wait(VE, t_res)
                            resid_tok[tile] = t_res
                        VE.wait(TE, TE.n)
                        SC.wait(TE, TE.n)

        def emit_ffn(final, resid_tok):
            GT = 8
            with contextlib.ExitStack() as fctx:
                stash = sb("stash", [128, GT * 4096], BF16, fctx)
                qscales = sb("qscales", [128, 4 * GT], F32, fctx)
                bufs["xf"] = sb("xln_f32", [128, 4 * 1024], F32, fctx)
                outstage = sb("outstage", [128, 2 * 512], F32, fctx) if final else None
                t_outdma = {}
                gate_sc = lambda tt: qscales[:, tt:tt + 1]
                up_sc = lambda tt: qscales[:, GT + tt:GT + tt + 1]
                down_sc = lambda tt: qscales[:, 2 * GT + tt:2 * GT + tt + 1]

                last_dn_mm = [0]
                for grp in range(4):
                    with contextlib.ExitStack() as f1:
                        rT = sb("rT", [128, 8 * 1024], BF16, f1)
                        wgu = sb("wgu", [128, 2 * 4096], BF16, f1)
                        silu_t = sb("silu_t", [128, 512], F32, f1)
                        ps_g = [psum("ps_g0", [128, 512], F32, f1), psum("ps_g1", [128, 512], F32, f1)]
                        ps_u = [psum("ps_u0", [128, 512], F32, f1), psum("ps_u1", [128, 512], F32, f1)]
                        rT_r = rT[:].rearrange("p (k t) -> p k t", k=8)
                        t_qsc_tok = [0] * GT
                        ln_stats_batch([grp * GT + tt for tt in range(GT)], [resid_tok[grp * GT + tt] for tt in range(GT)])
                        for half in range(2):
                            h0 = half * 4
                            for tt in range(h0, h0 + 4):
                                ln_quant_pre(tt, grp * GT + tt, tt % 4)
                            ln_quant_scales(h0, 4)
                            t_g = VE.inc(lambda e, h0=h0: e.tensor_scalar(qscales[:, h0:h0 + 4], lnstat[:, 112 + h0:116 + h0], meanw[:, 0:1], None, ALU.mult))
                            t_u = VE.inc(lambda e, h0=h0: e.tensor_scalar(qscales[:, GT + h0:GT + h0 + 4], lnstat[:, 112 + h0:116 + h0], meanw[:, 1:2], None, ALU.mult))
                            for tt in range(h0, h0 + 4):
                                t_qsc_tok[tt] = t_u
                                ln_norm_tile(tt, grp * GT + tt, rT_r[:, :, tt * 128:(tt + 1) * 128], tt % 4, quant=True)
                        t_wgu = [0, 0]
                        t_gu_by_buf = [0, 0]
                        t_gu2 = 0
                        t_gu2_by_par = {}
                        t_si_by_par = {}
                        CW = 256
                        for c in range(16):
                            buf = c % 2 if K_DBUF else 0
                            SY.wait(TE, t_gu_by_buf[buf])
                            tok = 0
                            for k in range(8):
                                SY.dma(lambda e, k=k, c=c, buf=buf: e.dma_start(
                                    out=wgu[:, buf * 4096 + k * CW:buf * 4096 + (k + 1) * CW],
                                    in_=scr["wgT"][k * 128:(k + 1) * 128, c * CW:(c + 1) * CW]), slot=buf)
                                tok = SY.dma(lambda e, k=k, c=c, buf=buf: e.dma_start(
                                    out=wgu[:, buf * 4096 + 2048 + k * CW:buf * 4096 + 2048 + (k + 1) * CW],
                                    in_=scr["wuT"][k * 128:(k + 1) * 128, c * CW:(c + 1) * CW]), slot=buf)
                            t_wgu[buf] = tok
                            TE.wait_dma(SY, t_wgu[buf])
                            for tt in range(GT):
                                par = tt % 2
                                if par in t_gu2_by_par:
                                    TE.wait(VE, t_gu2_by_par[par])
                                for k in range(8):
                                    TE.do(lambda e, k=k, tt=tt, buf=buf, par=par: e.matmul(
                                        ps_g[par][:, 0:CW], rT_r[:, k, tt * 128:(tt + 1) * 128],
                                        wgu[:, buf * 4096 + k * CW:buf * 4096 + (k + 1) * CW],
                                        start=(k == 0), stop=(k == 7)))
                                    fn = lambda e, k=k, tt=tt, buf=buf, par=par: e.matmul(
                                        ps_u[par][:, 0:CW], rT_r[:, k, tt * 128:(tt + 1) * 128],
                                        wgu[:, buf * 4096 + 2048 + k * CW:buf * 4096 + 2048 + (k + 1) * CW],
                                        start=(k == 0), stop=(k == 7))
                                    if k == 7:
                                        t_gu = TE.inc(fn)
                                    else:
                                        TE.do(fn)
                                t_gu_by_buf[buf] = t_gu
                                SC.wait(TE, t_gu)
                                if c == 0:
                                    SC.wait(VE, t_qsc_tok[tt])
                                if par in t_gu2_by_par:
                                    SC.wait(VE, t_gu2_by_par[par])
                                t_si = SC.inc(lambda e, par=par, tt=tt: e.activation(
                                    silu_t[:, par * CW:par * CW + CW], ps_g[par][:, 0:CW], AF.Silu, bias=0.0, scale=gate_sc(tt)))
                                VE.wait(SC, t_si)
                                dst = stash[:, tt * 4096 + c * CW:tt * 4096 + (c + 1) * CW]
                                t_gu2 = VE.inc(lambda e, dst=dst, par=par, tt=tt: e.scalar_tensor_tensor(
                                    dst, ps_u[par][:, 0:CW], up_sc(tt), silu_t[:, par * CW:par * CW + CW], ALU.mult, ALU.mult))
                                t_gu2_by_par[par] = t_gu2
                                if not K_PIPE_GU:
                                    TE.wait(VE, t_gu2)

                    if grp == 0:
                        def _tapscr(name, src_ap, shape):
                            def go(t):
                                GP.dma(lambda e, t=t, src_ap=src_ap: e.dma_start(out=t, in_=src_ap))
                            dbg(name, go, shape)
                        _tapscr("scrg", scr["wgT"][0:128, 3072:4096], [128, 1024])
                        _tapscr("scru", scr["wuT"][0:128, 3072:4096], [128, 1024])
                        def _tapf(name, ap, shape):
                            def go(t):
                                for s in (VE, SC, TE):
                                    GP.wait(s, s.n)
                                GP.dma(lambda e, t=t, ap=ap: e.dma_start(out=t, in_=ap))
                                for s in (VE, SC, TE):
                                    s.wait_dma_all(GP)
                            dbg(name, go, shape)
                        _tapf("meanw", meanw[:], [128, 3])
                        _tapf("gu0", stash[:, 0:4096], [128, 4096])
                        _tapf("gu4", stash[:, 4 * 4096:5 * 4096], [128, 4096])
                        _tapf("qsc", qscales[:], [128, 32])
                    t_phase1_mm = TE.n
                    with contextlib.ExitStack() as f2:
                        wdf = sb("wdf", [128, 32 * 512], BF16, f2)
                        ps_t2 = [psum("ps_t2a", [128, 2048], BF16, f2), psum("ps_t2b", [128, 2048], BF16, f2)]
                        ps_d = [psum("ps_d0", [128, 512], F32, f2), psum("ps_d1", [128, 512], F32, f2)]
                        t_q_done = [0] * GT
                        u_last = {}
                        bk_hist = {}
                        rcnt = 0
                        for tt in range(GT):
                            tile = grp * GT + tt
                            p = tt % 2
                            amax = qtmp[:, p:p + 1]
                            sx2 = qtmp[:, 2 + p:3 + p]
                            q2t = qtmp[:, 4 + p:5 + p]
                            VE.wait(VE, t_gu2_by_par[1] if 1 in t_gu2_by_par else t_gu2)
                            VE.wait(VE, t_gu2_by_par.get(0, 0))
                            VE.wait(SC, u_last.get(p, 0))
                            ta = VE.inc(lambda e, tt=tt, amax=amax: e.tensor_reduce(
                                amax, stash[:, tt * 4096:(tt + 1) * 4096], AX.X, ALU.max, apply_absolute_value=True))
                            VE.wait(VE, ta)
                            tb = VE.inc(lambda e, amax=amax: e.tensor_scalar_max(amax, amax, 1e-5))
                            VE.wait(VE, tb)
                            tc = VE.inc(lambda e, amax=amax, sx2=sx2: e.reciprocal(sx2, amax))
                            VE.wait(VE, tc)
                            td = VE.inc(lambda e, sx2=sx2: e.tensor_scalar_mul(sx2, sx2, 127.0))
                            tq2 = VE.inc(lambda e, amax=amax, q2t=q2t: e.tensor_scalar_mul(q2t, amax, 1.0 / 127.0))
                            VE.wait(VE, tq2)
                            tq3 = VE.inc(lambda e, tt=tt, q2t=q2t: e.tensor_mul(down_sc(tt), q2t, meanw[:, 2:3]))
                            SC.wait(VE, td)
                            xfh = bufs["xf"][:, p * 1024:(p + 1) * 1024]
                            for hf in range(4):
                                sl = slice(tt * 4096 + hf * 1024, tt * 4096 + (hf + 1) * 1024)
                                u1 = SC.inc(lambda e, sl=sl, xfh=xfh, sx2=sx2: e.activation(xfh, stash[:, sl], AF.Identity, bias=magic_t[:], scale=sx2))
                                SC.wait(SC, u1)
                                u2 = SC.inc(lambda e, sl=sl, xfh=xfh: e.activation(stash[:, sl], xfh, AF.Copy, bias=-MAGIC, scale=1.0))
                                SC.wait(SC, u2)
                            u_last[p] = u2
                            for rnd in range(2):
                                pt = ps_t2[rcnt % 2]
                                if rcnt - 2 in bk_hist:
                                    TE.wait(VE, bk_hist[rcnt - 2])
                                TE.wait(SC, u2)
                                for kk in range(16):
                                    k = rnd * 16 + kk
                                    fn = lambda e, k=k, kk=kk, tt=tt, pt=pt: e.transpose(
                                        pt[:, kk * 128:(kk + 1) * 128],
                                        stash[:, tt * 4096 + k * 128:tt * 4096 + (k + 1) * 128], ident[:])
                                    if kk == 15:
                                        t_tr = TE.inc(fn)
                                    else:
                                        TE.do(fn)
                                VE.wait(TE, t_tr)
                                dst = stash[:, tt * 4096 + rnd * 2048:tt * 4096 + (rnd + 1) * 2048]
                                t_bk = VE.inc(lambda e, dst=dst, pt=pt: e.tensor_copy(dst, pt[:]))
                                bk_hist[rcnt] = t_bk
                                rcnt += 1
                            t_q_done[tt] = t_bk
                            if grp == 0 and tt in (0, 4):
                                def _tapg(name, ap, shape):
                                    def go(t):
                                        GP.wait(VE, VE.n)
                                        GP.wait(SC, SC.n)
                                        GP.dma(lambda e, t=t, ap=ap: e.dma_start(out=t, in_=ap))
                                        for s in (VE, SC, TE):
                                            s.wait_dma_all(GP)
                                    dbg(name, go, shape)
                                _tapg("r2T%d" % tt, stash[:, tt * 4096:(tt + 1) * 4096], [128, 4096])
                        t_res_by_par = {}
                        for hf in range(2):
                            SY.wait(TE, max(last_dn_mm[0], t_phase1_mm))
                            wd_tok = []
                            for k in range(32):
                                wd_tok.append(SY.dma(lambda e, k=k, hf=hf: e.dma_start(
                                    out=wdf[:, k * 512:(k + 1) * 512],
                                    in_=scr["wdT"][k * 128:(k + 1) * 128, hf * 512:(hf + 1) * 512]), slot=hf))
                            TE.wait_dma(SY, wd_tok[-1])
                            for tt in range(GT):
                                tile = grp * GT + tt
                                par = tt % 2
                                TE.wait(VE, t_q_done[tt])
                                if par in t_res_by_par:
                                    TE.wait(VE, t_res_by_par[par])
                                for k in range(32):
                                    fn = lambda e, k=k, tt=tt, par=par: e.matmul(
                                        ps_d[par][:, 0:512], stash[:, tt * 4096 + k * 128:tt * 4096 + (k + 1) * 128],
                                        wdf[:, k * 512:(k + 1) * 512], start=(k == 0), stop=(k == 31))
                                    if k == 31:
                                        t_dm = TE.inc(fn)
                                    else:
                                        TE.do(fn)
                                last_dn_mm[0] = t_dm
                                VE.wait(TE, t_dm)
                                if not K_PIPE_DN:
                                    pass
                                dst = xr[:, tile * D + hf * 512:tile * D + (hf + 1) * 512]
                                if final:
                                    reg = tt % 2
                                    ob = outstage[:, reg * 512:reg * 512 + 512]
                                    VE.wait_dma(SY, t_outdma.get(reg, 0))
                                    t_res = VE.inc(lambda e, dst=dst, par=par, tt=tt, ob=ob: e.scalar_tensor_tensor(
                                        ob, ps_d[par][:, 0:512], down_sc(tt), dst, ALU.mult, ALU.add))
                                    SY.wait(VE, t_res)
                                    t_outdma[reg] = SY.dma(lambda e, tile=tile, hf=hf, ob=ob: e.dma_start(
                                        out=y_flat[tile * 128:(tile + 1) * 128, hf * 512:(hf + 1) * 512], in_=ob), slot=2 + reg)
                                else:
                                    t_res = VE.inc(lambda e, dst=dst, par=par, tt=tt: e.scalar_tensor_tensor(
                                        dst, ps_d[par][:, 0:512], down_sc(tt), dst, ALU.mult, ALU.add))
                                t_res_by_par[par] = t_res
                                if not K_PIPE_DN:
                                    TE.wait(VE, t_res)
                                resid_tok[tile] = t_res
                    barrier()

        def _tapq(name, src_ap, shape):
            def go(t):
                GP.wait_dma_all(SY)
                GP.dma(lambda e, t=t, src_ap=src_ap: e.dma_start(out=t, in_=src_ap))
            dbg(name, go, shape)
        _tapq("scrq_tail", scr["wqT_a"][0:128, 768:1024], [128, 256])
        _tapq("scrq_head", scr["wqT_a"][0:128, 0:256], [128, 256])

        resid_tok = [0] * NT
        VE.wait_dma(GP, t_x)
        SC.wait_dma(GP, t_x)
        VE.wait_dma(GP, t_ghost)
        TE.wait_dma(GP, t_bo)

        def _tap_sbuf(name, sb_ap, shape):
            def go(t):
                for s in (VE, SC, TE):
                    GP.wait(s, s.n)
                GP.wait_dma_all(GP)
                GP.dma(lambda e, t=t, sb_ap=sb_ap: e.dma_start(out=t, in_=sb_ap))
            dbg(name, go, shape)

        if n_sub >= 1:
            _tap_sbuf("xr0", xr[:, 0:1024], [128, 1024])
            emit_attn(False, resid_tok)
        if n_sub >= 2:
            barrier()
            emit_ffn(final=(n_sub == 2), resid_tok=resid_tok)
        if n_sub >= 3:
            barrier()
            emit_attn(True, resid_tok)
        if n_sub >= 4:
            barrier()
            emit_ffn(final=True, resid_tok=resid_tok)

        if n_sub in (1, 3):
            barrier()
            dumpb = sb("dumpb", [128, 1024], F32)
            for tile in range(NT):
                t_c = SC.inc(lambda e, tile=tile: e.activation(
                    dumpb[:], xr[:, tile * D:(tile + 1) * D], AF.Copy, bias=0.0, scale=1.0))
                SY.wait(SC, t_c)
                tok = SY.dma(lambda e, tile=tile: e.dma_start(out=y_flat[tile * 128:(tile + 1) * 128, :], in_=dumpb[:]))
                SC.wait_dma(SY, tok)

        SY.wait_dma_all(SY)
        GP.wait_dma_all(GP)

        finalize_barrier_ops()
        with nc.Block() as blk:
            blk.tensor(lambda e: TE.run(e))
            blk.vector(lambda e: VE.run(e))
            blk.scalar(lambda e: SC.run(e))
            blk.gpsimd(lambda e: GP.run(e))
            blk.sync(lambda e: SY.run(e))

    return nc


_NC_CACHE = {}


def kernel(L0_attn_a_x, L0_attn_a_wq, L0_attn_a_wk, L0_attn_a_wv, L0_attn_a_wo,
           L0_attn_b_Wqkv_w, L0_attn_b_Wqkv_b, L0_attn_b_Wo_w, L0_attn_b_Wo_b,
           L0_attn_b_ghost, L0_ffn_w_gate, L0_ffn_w_up, L0_ffn_w_down):
    n_sub = N_SUB
    key = (n_sub, tuple(DEBUG))
    if key not in _NC_CACHE:
        _NC_CACHE[key] = build(n_sub)
    nc = _NC_CACHE[key]
    x = np.ascontiguousarray(L0_attn_a_x)
    in_maps = []
    for c in range(8):
        in_maps.append({
            "x": x[2 * c:2 * c + 2],
            "wq_a": L0_attn_a_wq, "wk_a": L0_attn_a_wk, "wv_a": L0_attn_a_wv, "wo_a": L0_attn_a_wo,
            "wqkv_b": L0_attn_b_Wqkv_w, "bqkv_b": L0_attn_b_Wqkv_b,
            "wo_b": L0_attn_b_Wo_w, "bo_b": L0_attn_b_Wo_b, "ghost": L0_attn_b_ghost,
            "w_gate": L0_ffn_w_gate, "w_up": L0_ffn_w_up, "w_down": L0_ffn_w_down,
        })
    res = run_bass_kernel_spmd(nc, in_maps, core_ids=list(range(8)))
    global _dbg_out
    _dbg_out = {k: np.stack([res.results[c]["dbg_" + k] for c in range(8)]) for k in DEBUG}
    return np.concatenate([res.results[c]["y"] for c in range(8)], axis=0)
